# version 21
# baseline (speedup 1.0000x reference)
import numpy as np
from contextlib import ExitStack

import concourse.bass as bass
import concourse.mybir as mybir
from concourse.bass_utils import run_bass_kernel_spmd

F32 = mybir.dt.float32
BF16 = mybir.dt.bfloat16
I32 = mybir.dt.int32
AF = mybir.ActivationFunctionType
ALU = mybir.AluOpType
AX = mybir.AxisListType

S = 2048
D = 2048
H = 16
NOPE = 128
ROPE = 64
VD = 128
LORA = 512
NEXP = 16
FF = 768
NT = S // 128
ND = D // 128
NFC = FF // 128
NQ = 4
HPQ = H // NQ
SCALE = float(1.0 / np.sqrt(NOPE + ROPE))
MAGIC = 12582912.0
TWO_PI = 6.283185307179586
CW1 = 6.28125
CW2 = float(np.float32(TWO_PI - CW1))
CW3 = float(TWO_PI - CW1 - np.float64(np.float32(TWO_PI - CW1)))
NEG_BIG = -1e30
LN_EPS = 1e-5


class DSem:
    def __init__(self, nc, stack, name):
        self.h = stack.enter_context(nc.semaphore(name))
        self.n = 0
        self.name = name

    def inc(self, instr, k):
        instr.then_inc(self.h, k)
        self.n += k
        return self.n


def build_nc(prev_marks):
    pass2 = prev_marks is not None
    marks = {}
    nc = bass.Bass("TRN2", target_bir_lowering=False)

    x = nc.dram_tensor("x", [S, D], F32, kind="ExternalInput").ap()
    wqn = nc.dram_tensor("wqn", [D, H * NOPE], F32, kind="ExternalInput").ap()
    wqr = nc.dram_tensor("wqr", [D, (H // 2) * 128], F32, kind="ExternalInput").ap()
    wkva = nc.dram_tensor("wkva", [D, LORA + ROPE], F32, kind="ExternalInput").ap()
    wkvb = nc.dram_tensor("wkvb", [LORA, H * 256], F32, kind="ExternalInput").ap()
    wo = nc.dram_tensor("wo", [H * VD, D], F32, kind="ExternalInput").ap()
    rw = nc.dram_tensor("rw", [D, NEXP], F32, kind="ExternalInput").ap()
    gw = nc.dram_tensor("gw", [NEXP, D, FF], F32, kind="ExternalInput").ap()
    uw = nc.dram_tensor("uw", [NEXP, D, FF], F32, kind="ExternalInput").ap()
    dw = nc.dram_tensor("dw", [NEXP, FF, D], F32, kind="ExternalInput").ap()
    invf = nc.dram_tensor("invf", [32, 1], F32, kind="ExternalInput").ap()
    out = nc.dram_tensor("out", [S, D], F32, kind="ExternalOutput").ap()
    scr = [nc.dram_tensor(f"scr{q}", [S, D], F32).ap() for q in range(NQ)]

    sem_by_name = {}

    def mark(key, sem):
        assert key not in marks, key
        marks[key] = (sem.name, sem.n)

    def wait(eng, key):
        if pass2:
            name, v = prev_marks[key]
            eng.wait_ge(sem_by_name[name], v)
        else:
            assert key not in marks or True
            eng.wait_ge(sem_by_name["PE"], 0)

    with ExitStack() as G:
        EG = G.enter_context
        sems = {}
        for nm in ["PE", "VE", "SE", "PO",
                   "dX0", "dX1", "dWQN", "dWQR", "dWKB", "dWO", "dKVA", "dRW", "dIF",
                   "dG0", "dG1", "dU0", "dU1", "dDW", "dSCR0", "dSCR1", "dOUT0", "dOUT1",
                   "dMISC", "dMRG0", "dMRG1", "dMG0", "dMG1", "dSCRA0", "dSCRA1", "dQR", "dQRS"]:
            sems[nm] = DSem(nc, G, nm)
            sem_by_name[nm] = sems[nm].h
        PE, VE, SE, PO = sems["PE"], sems["VE"], sems["SE"], sems["PO"]

        xnT = EG(nc.sbuf_tensor("xnT", [128, ND * 2048], BF16))
        comb = EG(nc.sbuf_tensor("comb", [128, NT * NEXP], F32))
        ident = EG(nc.sbuf_tensor("ident", [128, 128], BF16))
        identNeg = EG(nc.sbuf_tensor("identNeg", [128, 128], BF16))
        upperT = EG(nc.sbuf_tensor("upperT", [128, 128], BF16))
        colA = EG(nc.sbuf_tensor("colA", [128, 1], F32))
        colB = EG(nc.sbuf_tensor("colB", [128, 1], F32))
        colC = EG(nc.sbuf_tensor("colC", [128, 1], F32))
        colD = EG(nc.sbuf_tensor("colD", [128, 1], F32))
        colE = EG(nc.sbuf_tensor("colE", [128, 1], F32))
        block = EG(nc.Block())

        def xnt(dc):
            return xnT[:, dc * 2048:(dc + 1) * 2048]

        with ExitStack() as M:
            EM = M.enter_context
            cosr = EM(nc.sbuf_tensor("cosr", [64, 2048], BF16))
            sinr = EM(nc.sbuf_tensor("sinr", [64, 2048], BF16))
            cT = EM(nc.sbuf_tensor("cT", [128, 4 * 2048], BF16))
            krope = EM(nc.sbuf_tensor("krope", [64, 2048], BF16))
            wqnb = EM(nc.sbuf_tensor("wqnb", [128, 2048], BF16))
            wqrb = EM(nc.sbuf_tensor("wqrb", [128, 2048], BF16))
            knb = EM(nc.sbuf_tensor("knb", [128, 512], BF16))
            vwb = EM(nc.sbuf_tensor("vwb", [128, 512], BF16))

            with ExitStack() as P:
                EP = P.enter_context
                xbuf = [EP(nc.sbuf_tensor(f"xbuf{i}", [128, 2048], F32)) for i in range(2)]
                trash = EP(nc.sbuf_tensor("trash", [128, 2048], BF16))
                xnb = EP(nc.sbuf_tensor("xnb", [128, 2048], BF16))
                tidx = EP(nc.sbuf_tensor("tidx", [32, 2048], F32))
                ivf = EP(nc.sbuf_tensor("ivf", [32, 1], F32))
                frq = EP(nc.sbuf_tensor("frq", [32, 2048], F32))
                kfb = EP(nc.sbuf_tensor("kfb", [32, 2048], F32))
                lnc = EP(nc.sbuf_tensor("lnc", [128, 8], F32))
                psT0 = EP(nc.psum_tensor("psT0", [128, 1024], BF16))
                psT1 = EP(nc.psum_tensor("psT1", [128, 1024], BF16))

                @block.sync
                def _(sy):
                    i = sy.dma_start(ivf[:], invf)
                    sems["dIF"].inc(i, 16)
                    mark(("ivf",), sems["dIF"])
                    for tc in range(NT):
                        if tc >= 2:
                            wait(sy, ("lnused", tc - 2))
                        i = sy.dma_start(xbuf[tc % 2][:], x[tc * 128:(tc + 1) * 128, :])
                        sems[f"dX{tc % 2}"].inc(i, 16)
                        mark(("xload", tc), sems[f"dX{tc % 2}"])

                @block.gpsimd
                def _(g):
                    g.memset(ident[:], 0.0)
                    g.drain()
                    i = g.affine_select(out=ident[:], in_=ident[:], compare_op=ALU.not_equal,
                                        fill=1.0, base=0, pattern=[[-1, 128]], channel_multiplier=1)
                    PO.inc(i, 1)
                    mark(("ident",), PO)
                    g.memset(identNeg[:], 0.0)
                    g.drain()
                    i = g.affine_select(out=identNeg[:], in_=identNeg[:], compare_op=ALU.not_equal,
                                        fill=NEG_BIG, base=0, pattern=[[-1, 128]], channel_multiplier=1)
                    PO.inc(i, 1)
                    mark(("cmask",), PO)
                    g.memset(upperT[:], 0.0)
                    g.drain()
                    i = g.affine_select(out=upperT[:], in_=upperT[:], compare_op=ALU.is_ge,
                                        fill=1.0, base=0, pattern=[[-1, 128]], channel_multiplier=1)
                    PO.inc(i, 1)
                    mark(("upperT",), PO)
                    i = g.iota(tidx[:], pattern=[[1, 2048]], base=0, channel_multiplier=0,
                               allow_small_or_imprecise_dtypes=True)
                    PO.inc(i, 1)
                    mark(("tidx",), PO)

                @block.vector
                def _(v):
                    wait(v, ("tidx",))
                    wait(v, ("ivf",))
                    v.tensor_scalar(out=frq[:], in0=tidx[:], scalar1=ivf[:], scalar2=None, op0=ALU.mult)
                    v.drain()
                    v.tensor_scalar(out=kfb[:], in0=frq[:], scalar1=float(1.0 / TWO_PI),
                                    scalar2=MAGIC, op0=ALU.mult, op1=ALU.add)
                    v.drain()
                    v.tensor_scalar(out=kfb[:], in0=kfb[:], scalar1=MAGIC, scalar2=None, op0=ALU.subtract)
                    v.drain()
                    v.scalar_tensor_tensor(out=frq[:], in0=kfb[:], scalar=-CW1, in1=frq[:],
                                           op0=ALU.mult, op1=ALU.add)
                    v.drain()
                    v.scalar_tensor_tensor(out=frq[:], in0=kfb[:], scalar=-CW2, in1=frq[:],
                                           op0=ALU.mult, op1=ALU.add)
                    v.drain()
                    v.tensor_scalar(out=frq[:], in0=frq[:], scalar1=float(np.pi), scalar2=None, op0=ALU.min)
                    v.drain()
                    i = v.tensor_scalar(out=frq[:], in0=frq[:], scalar1=float(-np.pi), scalar2=None, op0=ALU.max)
                    v.drain()
                    VE.inc(i, 1)
                    mark(("ropearg",), VE)
                    v.tensor_scalar(out=kfb[:], in0=frq[:], scalar1=-1.0, scalar2=None, op0=ALU.mult)
                    v.drain()
                    v.tensor_tensor(out=kfb[:], in0=kfb[:], in1=frq[:], op=ALU.max)
                    v.drain()
                    i = v.tensor_scalar(out=kfb[:], in0=kfb[:], scalar1=-1.0, scalar2=float(np.pi / 2),
                                        op0=ALU.mult, op1=ALU.add)
                    v.drain()
                    VE.inc(i, 1)
                    mark(("cosarg",), VE)
                    def emit_xnb(j):
                        wait(v, ("rstd", j))
                        v.tensor_scalar(out=lnc[:, j % 2:j % 2 + 1], in0=lnc[:, j % 2:j % 2 + 1],
                                        scalar1=-1.0, scalar2=None, op0=ALU.mult)
                        v.drain()
                        i = v.tensor_scalar(out=xnb[:], in0=xbuf[j % 2][:],
                                            scalar1=lnc[:, j % 2:j % 2 + 1],
                                            scalar2=lnc[:, 4 + j % 2:5 + j % 2],
                                            op0=ALU.add, op1=ALU.mult)
                        v.drain()
                        VE.inc(i, 1)
                        mark(("xn", j), VE)
                        mark(("lnused", j), VE)

                    def emit_evac(j):
                        for half in range(2):
                            pt = psT0 if half == 0 else psT1
                            wait(v, ("xpose", j, half))
                            dst = xnT[:, half * 8 * 2048: half * 8 * 2048 + 8 * 2048].rearrange(
                                "p (dc c) -> p dc c", c=2048)[:, :, j * 128:j * 128 + 128]
                            src = pt[:].rearrange("p (j c) -> p j c", c=128)
                            i = v.tensor_copy(dst, src)
                            VE.inc(i, 1)
                            mark(("xnTevac", j, half), VE)

                    for tc in range(NT):
                        wait(v, ("xload", tc))
                        xb = xbuf[tc % 2]
                        v.reduce_sum(colA[:], xb[:], axis=AX.X)
                        v.drain()
                        v.tensor_scalar(out=lnc[:, tc % 2:tc % 2 + 1], in0=colA[:],
                                        scalar1=float(1.0 / D), scalar2=None, op0=ALU.mult)
                        v.drain()
                        wait(v, ("sq", tc))
                        v.tensor_scalar(out=lnc[:, 2 + tc % 2:3 + tc % 2], in0=colB[:],
                                        scalar1=float(1.0 / D), scalar2=None, op0=ALU.mult)
                        v.tensor_tensor(out=colA[:], in0=lnc[:, tc % 2:tc % 2 + 1],
                                        in1=lnc[:, tc % 2:tc % 2 + 1], op=ALU.mult)
                        v.drain()
                        v.tensor_tensor(out=lnc[:, 2 + tc % 2:3 + tc % 2],
                                        in0=lnc[:, 2 + tc % 2:3 + tc % 2], in1=colA[:], op=ALU.subtract)
                        v.drain()
                        i = v.tensor_scalar(out=lnc[:, 2 + tc % 2:3 + tc % 2],
                                            in0=lnc[:, 2 + tc % 2:3 + tc % 2], scalar1=LN_EPS,
                                            scalar2=None, op0=ALU.add)
                        v.drain()
                        VE.inc(i, 1)
                        mark(("var", tc), VE)
                        if tc >= 1:
                            emit_xnb(tc - 1)
                            emit_evac(tc - 1)
                    emit_xnb(NT - 1)
                    emit_evac(NT - 1)

                @block.scalar
                def _(s):
                    wait(s, ("ropearg",))
                    wait(s, ("cosarg",))
                    s.activation(sinr[0:32, :], frq[:], AF.Sin)
                    i = s.activation(cosr[0:32, :], kfb[:], AF.Sin)
                    SE.inc(i, 1)
                    mark(("cossin",), SE)
                    for tc in range(NT):
                        wait(s, ("xload", tc))
                        xb = xbuf[tc % 2]
                        i = s.activation(trash[:], xb[:], AF.Square, accum_out=colB[:])
                        s.drain()
                        SE.inc(i, 1)
                        mark(("sq", tc), SE)
                        wait(s, ("var", tc))
                        s.activation(lnc[:, 4 + tc % 2:5 + tc % 2], lnc[:, 2 + tc % 2:3 + tc % 2], AF.Ln)
                        s.drain()
                        i = s.activation(lnc[:, 4 + tc % 2:5 + tc % 2],
                                         lnc[:, 4 + tc % 2:5 + tc % 2], AF.Exp, scale=-0.5)
                        s.drain()
                        SE.inc(i, 1)
                        mark(("rstd", tc), SE)


                @block.gpsimd
                def _(g):
                    wait(g, ("cossin",))
                    g.dma_start(cosr[32:64, :], cosr[0:32, :]).then_inc(sems["dMISC"].h, 16)
                    sems["dMISC"].n += 16
                    i = g.dma_start(sinr[32:64, :], sinr[0:32, :])
                    sems["dMISC"].inc(i, 16)
                    mark(("cossinrep",), sems["dMISC"])

                @block.tensor
                def _(t):
                    wait(t, ("ident",))
                    for tc in range(NT):
                        wait(t, ("xn", tc))
                        for half in range(2):
                            pt = psT0 if half == 0 else psT1
                            if tc >= 1:
                                wait(t, ("xnTevac", tc - 1, half))
                            for j in range(8):
                                dc = half * 8 + j
                                i = t.transpose(pt[:, j * 128:j * 128 + 128],
                                                xnb[:, dc * 128:(dc + 1) * 128], ident[:])
                            PE.inc(i, 1)
                            mark(("xpose", tc, half), PE)

            with ExitStack() as P:
                EP = P.enter_context
                kvab = EP(nc.sbuf_tensor("kvab", [128, ND * 576], BF16))
                kpe_bf = EP(nc.sbuf_tensor("kpe_bf", [64, 2048], BF16))
                kpeO = EP(nc.sbuf_tensor("kpeO", [32, 2048], BF16))
                krO32 = EP(nc.sbuf_tensor("krO32", [32, 2048], BF16))
                r1 = EP(nc.sbuf_tensor("r1", [32, 2048], BF16))
                r2 = EP(nc.sbuf_tensor("r2", [32, 2048], BF16))
                rwb = EP(nc.sbuf_tensor("rwb", [128, ND * NEXP], BF16))
                eq1 = EP(nc.sbuf_tensor("eq1", [128, NEXP], I32))
                eq2 = EP(nc.sbuf_tensor("eq2", [128, NEXP], I32))
                rprob = EP(nc.sbuf_tensor("rprob", [128, NEXP], F32))
                rp2 = EP(nc.sbuf_tensor("rp2", [128, NEXP], F32))
                rtmp = EP(nc.sbuf_tensor("rtmp", [128, NEXP], F32))
                rneg = EP(nc.sbuf_tensor("rneg", [128, NEXP], F32))
                psDummy = EP(nc.psum_tensor("psDummy", [128, 2048], F32))
                psP = [EP(nc.psum_tensor(f"psP{i}", [128, 512], F32)) for i in range(4)]

                @block.gpsimd
                def _(g):
                    wait(g, ("xnTevac", NT - 1, 1))
                    wait(g, ("cossin",))
                    i = g.dma_start(kvab[:].rearrange("p (dc m) -> p dc m", m=576),
                                    wkva.rearrange("(dc p) m -> p dc m", p=128))
                    sems["dKVA"].inc(i, 16)
                    mark(("kva",), sems["dKVA"])
                    i = g.dma_start(rwb[:].rearrange("p (dc e) -> p dc e", e=NEXP),
                                    rw.rearrange("(dc p) e -> p dc e", p=128))
                    sems["dRW"].inc(i, 16)
                    mark(("rwl",), sems["dRW"])
                    i = g.dma_start(wqnb[:].rearrange("p (dc c) -> p dc c", c=NOPE),
                                    wqn.rearrange("(dc p) c -> p dc c", p=128)[:, :, 0:NOPE])
                    sems["dWQN"].inc(i, 16)
                    mark(("wqnl", 0), sems["dWQN"])
                    i = g.dma_start(wqrb[:].rearrange("p (dc c) -> p dc c", c=128),
                                    wqr.rearrange("(dc p) c -> p dc c", p=128)[:, :, 0:128])
                    sems["dWQR"].inc(i, 16)
                    mark(("wqrl", 0), sems["dWQR"])
                    i = g.dma_start(knb[:].rearrange("p (lc c) -> p lc c", c=128),
                                    wkvb.rearrange("(lc p) c -> p lc c", p=128)[:, :, 0:128])
                    sems["dWKB"].inc(i, 16)
                    i = g.dma_start(vwb[:].rearrange("p (lc c) -> p lc c", c=128),
                                    wkvb.rearrange("(lc p) c -> p lc c", p=128)[:, :, 128:256])
                    sems["dWKB"].inc(i, 16)
                    mark(("wkbl", 0), sems["dWKB"])
                    wait(g, ("ckvevac", 19))
                    i = g.dma_start(kpeO[:], kpe_bf[32:64, :])
                    sems["dQRS"].inc(i, 16)
                    mark(("kpeO",), sems["dQRS"])
                    wait(g, ("kropePart",))
                    i = g.dma_start(krope[32:64, :], krO32[:])
                    sems["dQRS"].inc(i, 16)
                    mark(("krope",), sems["dQRS"])

                @block.tensor
                def _(t):
                    wait(t, ("kva",))
                    wait(t, ("rwl",))
                    for nb in range(4):
                        for tc in range(nb * 4, nb * 4 + 4):
                            ps = psP[tc % 4]
                            wait(t, ("xnTevac", tc, 1))
                            if tc >= 4:
                                wait(t, ("rexp", tc - 4))
                            for dc in range(ND):
                                i = t.matmul(ps[:, 0:NEXP],
                                             xnt(dc)[:, tc * 128:(tc + 1) * 128],
                                             rwb[:, dc * NEXP:(dc + 1) * NEXP],
                                             start=(dc == 0), stop=(dc == ND - 1))
                            PE.inc(i, 1)
                            mark(("rscore", tc), PE)
                        for mc in range(5):
                            k = nb * 5 + mc
                            ps = psDummy[:, (k % 4) * 512:(k % 4 + 1) * 512]
                            if k >= 4:
                                wait(t, ("ckvevac", k - 4))
                            Mrows = 128 if mc < 4 else 64
                            for dc in range(ND):
                                i = t.matmul(ps[0:Mrows],
                                             kvab[:, dc * 576 + mc * 128: dc * 576 + mc * 128 + Mrows],
                                             xnt(dc)[:, nb * 512:(nb + 1) * 512],
                                             start=(dc == 0), stop=(dc == ND - 1))
                            PE.inc(i, 1)
                            mark(("ckv", k), PE)

                @block.vector
                def _(v):
                    def emit_ckvevac(k):
                        nb2, mc2 = divmod(k, 5)
                        ps2 = psDummy[:, (k % 4) * 512:(k % 4 + 1) * 512]
                        wait(v, ("ckv", k))
                        if mc2 < 4:
                            i2 = v.tensor_copy(cT[:, mc2 * 2048 + nb2 * 512: mc2 * 2048 + (nb2 + 1) * 512], ps2)
                        else:
                            i2 = v.tensor_copy(kpe_bf[:, nb2 * 512:(nb2 + 1) * 512], ps2[0:64])
                        VE.inc(i2, 1)
                        mark(("ckvevac", k), VE)

                    wait(v, ("cossin",))
                    v.memset(rneg[:], NEG_BIG)
                    for tc in range(NT):
                        wait(v, ("rscore", tc))
                        ps = psP[tc % 4]
                        i = v.reduce_max(colA[:], ps[:, 0:NEXP], axis=AX.X, negate=True)
                        v.drain()
                        VE.inc(i, 1)
                        mark(("rmax", tc), VE)
                        wait(v, ("rexps", tc))
                        v.reduce_max(colA[:], rprob[:], axis=AX.X)
                        v.drain()
                        v.tensor_scalar(out=eq1[:], in0=rprob[:], scalar1=colA[:], scalar2=None, op0=ALU.is_equal)
                        v.tensor_copy(rp2[:], rprob[:])
                        v.drain()
                        v.copy_predicated(out=rp2[:], mask=eq1[:], data=rneg[:])
                        v.drain()
                        v.reduce_max(colB[:], rp2[:], axis=AX.X)
                        v.drain()
                        v.tensor_scalar(out=eq2[:], in0=rp2[:], scalar1=colB[:], scalar2=None, op0=ALU.is_equal)
                        v.drain()
                        v.tensor_tensor(out=colC[:], in0=colA[:], in1=colB[:], op=ALU.add)
                        v.drain()
                        v.reciprocal(colC[:], colC[:])
                        v.drain()
                        v.tensor_tensor(out=colD[:], in0=colA[:], in1=colC[:], op=ALU.mult)
                        v.tensor_tensor(out=colE[:], in0=colB[:], in1=colC[:], op=ALU.mult)
                        v.drain()
                        v.tensor_scalar(out=rtmp[:], in0=rprob[:], scalar1=0.0, scalar2=colD[:],
                                        op0=ALU.mult, op1=ALU.add)
                        v.tensor_scalar(out=rp2[:], in0=rprob[:], scalar1=0.0, scalar2=colE[:],
                                        op0=ALU.mult, op1=ALU.add)
                        cslice = comb[:, tc * NEXP:(tc + 1) * NEXP]
                        v.memset(cslice, 0.0)
                        v.drain()
                        v.copy_predicated(out=cslice, mask=eq1[:], data=rtmp[:])
                        v.drain()
                        i = v.copy_predicated(out=cslice, mask=eq2[:], data=rp2[:])
                        v.drain()
                        VE.inc(i, 1)
                        mark(("comb", tc), VE)
                        mark(("rexp", tc), VE)
                        done = (tc * 20) // NT
                        upto = ((tc + 1) * 20) // NT
                        for k2 in range(done, upto):
                            emit_ckvevac(k2)
                    wait(v, ("cossinrep",))
                    wait(v, ("kpeO",))
                    v.tensor_tensor(out=r1[:], in0=kpe_bf[0:32, :], in1=cosr[0:32, :], op=ALU.mult)
                    v.tensor_tensor(out=r2[:], in0=kpeO[:], in1=sinr[0:32, :], op=ALU.mult)
                    v.drain()
                    v.tensor_tensor(out=krope[0:32, :], in0=r1[:], in1=r2[:], op=ALU.subtract)
                    v.drain()
                    v.tensor_tensor(out=r1[:], in0=kpeO[:], in1=cosr[0:32, :], op=ALU.mult)
                    v.tensor_tensor(out=r2[:], in0=kpe_bf[0:32, :], in1=sinr[0:32, :], op=ALU.mult)
                    v.drain()
                    i = v.tensor_tensor(out=krO32[:], in0=r1[:], in1=r2[:], op=ALU.add)
                    v.drain()
                    VE.inc(i, 1)
                    mark(("kropePart",), VE)

                @block.scalar
                def _(s):
                    for tc in range(NT):
                        wait(s, ("rmax", tc))
                        ps = psP[tc % 4]
                        if tc >= 1:
                            wait(s, ("comb", tc - 1))
                        i = s.activation(rprob[:], ps[:, 0:NEXP], AF.Exp, bias=colA[:])
                        s.drain()
                        SE.inc(i, 1)
                        mark(("rexps", tc), SE)

            with ExitStack() as P:
                EP = P.enter_context
                oT = EP(nc.sbuf_tensor("oT", [128, HPQ * 2048], BF16))
                woq = EP(nc.sbuf_tensor("woq", [128, HPQ * 2048], BF16))
                kns = EP(nc.sbuf_tensor("kns", [128, 2048], BF16))
                vs = EP(nc.sbuf_tensor("vs", [128, 2048], BF16))
                qn = EP(nc.sbuf_tensor("qn", [128, 2048], BF16))
                qrA = EP(nc.sbuf_tensor("qrA", [64, 2048], BF16))
                qrB = EP(nc.sbuf_tensor("qrB", [64, 2048], BF16))
                probs2 = [EP(nc.sbuf_tensor(f"probs{i}", [128, 2048], BF16)) for i in range(2)]
                probs = probs2[0]
                probsT2 = [EP(nc.sbuf_tensor(f"probsT{i}", [128, 2048], BF16)) for i in range(2)]
                probsT = probsT2[0]
                rtmp = EP(nc.sbuf_tensor("rtmp64", [64, 2048], F32))
                colM = EP(nc.sbuf_tensor("colM", [128, 4], F32))
                colB4 = EP(nc.sbuf_tensor("colB4", [128, 4], F32))
                stage = [EP(nc.sbuf_tensor(f"stage{i}", [128, 2048], F32)) for i in range(2)]
                xw = EP(nc.sbuf_tensor("xw", [128, 2048], F32))
                psA = EP(nc.psum_tensor("psA", [128, 2048], F32))
                psT0 = EP(nc.psum_tensor("psTa", [128, 1024], BF16))
                psT1 = EP(nc.psum_tensor("psTb", [128, 1024], BF16))
                psAV = [EP(nc.psum_tensor(f"psAV{i}", [128, 512], F32)) for i in range(2)]

                psT_last = {0: None, 1: None}
                psW_last = {0: None, 1: None}
                psAV_last = {0: None, 1: None}

                @block.gpsimd
                def _(g):
                    wait(g, ("krope",))
                    for h in range(H):
                        if h >= 1:
                            wait(g, ("qnMM", h - 1))
                            i = g.dma_start(wqnb[:].rearrange("p (dc c) -> p dc c", c=NOPE),
                                            wqn.rearrange("(dc p) c -> p dc c", p=128)[:, :, h * NOPE:(h + 1) * NOPE])
                            sems["dWQN"].inc(i, 16)
                            mark(("wqnl", h), sems["dWQN"])
                        if h % 2 == 0 and h >= 2:
                            hp = h // 2
                            wait(g, ("qrMM", hp - 1))
                            i = g.dma_start(wqrb[:].rearrange("p (dc c) -> p dc c", c=128),
                                            wqr.rearrange("(dc p) c -> p dc c", p=128)[:, :, hp * 128:(hp + 1) * 128])
                            sems["dWQR"].inc(i, 16)
                            mark(("wqrl", hp), sems["dWQR"])
                        if h >= 1:
                            wait(g, ("knvMM", h - 1))
                            i = g.dma_start(knb[:].rearrange("p (lc c) -> p lc c", c=128),
                                            wkvb.rearrange("(lc p) c -> p lc c", p=128)[:, :, h * 256:h * 256 + 128])
                            sems["dWKB"].inc(i, 16)
                            i = g.dma_start(vwb[:].rearrange("p (lc c) -> p lc c", c=128),
                                            wkvb.rearrange("(lc p) c -> p lc c", p=128)[:, :, h * 256 + 128:h * 256 + 256])
                            sems["dWKB"].inc(i, 16)
                            mark(("wkbl", h), sems["dWKB"])
                        if h % HPQ == 0:
                            q = h // HPQ
                            if q >= 1:
                                wait(g, ("woMM", q - 1))
                            for hh in range(HPQ):
                                i = g.dma_start(woq[:, hh * 2048:(hh + 1) * 2048],
                                                wo[(q * HPQ + hh) * 128:(q * HPQ + hh) * 128 + 128, :])
                                sems["dWO"].inc(i, 16)
                            mark(("wol", q), sems["dWO"])
                        if h % HPQ == HPQ - 1 and (h // HPQ) % 2 == 1:
                            q2 = h // HPQ
                            for tc in range(NT):
                                wait(g, ("stg", q2, tc))
                                wait(g, ("scrw", q2 - 1, tc))
                                i = g.dma_start(scr[q2 - 1][tc * 128:(tc + 1) * 128, :],
                                                stage[tc % 2][:], accum_op=ALU.add)
                                sems[f"dSCRA{tc % 2}"].inc(i, 16)
                                mark(("scrw", q2, tc), sems[f"dSCRA{tc % 2}"])
                        if h % 2 == 0:
                            hp = h // 2
                            wait(g, ("ropemath", hp))
                            g.dma_start(qrA[0:32, :], stage[0][0:32, :]).then_inc(sems["dQRS"].h, 16)
                            sems["dQRS"].n += 16
                            g.dma_start(qrA[32:64, :], stage[1][0:32, :]).then_inc(sems["dQRS"].h, 16)
                            sems["dQRS"].n += 16
                            g.dma_start(qrB[0:32, :], stage[0][32:64, :]).then_inc(sems["dQRS"].h, 16)
                            sems["dQRS"].n += 16
                            i = g.dma_start(qrB[32:64, :], stage[1][32:64, :])
                            sems["dQRS"].inc(i, 16)
                            mark(("qrready", hp), sems["dQRS"])

                @block.sync
                def _(sy):
                    for q in range(NQ):
                        for hp in (2 * q, 2 * q + 1):
                            wait(sy, ("qevac", hp))
                            i = sy.dma_start(probsT[0:64, :], probs[64:128, :])
                            sems["dQR"].inc(i, 16)
                            mark(("qodd", hp), sems["dQR"])
                        if q == 0:
                            i = sy.dma_start(xw[:], x[0:128, :])
                            sems["dX0"].inc(i, 16)
                            mark(("xw", 0), sems["dX0"])
                        for tc in range(NT):
                            if q % 2 == 0:
                                wait(sy, ("stg", q, tc))
                                i = sy.dma_start(scr[q][tc * 128:(tc + 1) * 128, :], stage[tc % 2][:])
                                sems[f"dSCR{tc % 2}"].inc(i, 16)
                                mark(("scrw", q, tc), sems[f"dSCR{tc % 2}"])
                            if q == 0 and tc + 1 < NT:
                                i = sy.dma_start(xw[:], x[(tc + 1) * 128:(tc + 2) * 128, :])
                                sems["dX0"].inc(i, 16)
                                mark(("xw", tc + 1), sems["dX0"])

                @block.tensor
                def _(t):
                    wait(t, ("krope",))
                    for h in range(H):
                        hp, q, hh = h // 2, h // HPQ, h % HPQ
                        wait(t, ("wqnl", h))
                        if h >= 1:
                            wait(t, ("vevac", h - 1, 3))
                            wait(t, ("exp", h - 1, NT - 1))
                        else:
                            wait(t, ("ckvevac", 19))
                        if h % 2 == 0:
                            wait(t, ("wqrl", hp))
                            for nb in range(4):
                                for dc in range(ND):
                                    i = t.matmul(psA[:, nb * 512:(nb + 1) * 512],
                                                 wqrb[:, dc * 128:(dc + 1) * 128],
                                                 xnt(dc)[:, nb * 512:(nb + 1) * 512],
                                                 start=(dc == 0), stop=(dc == ND - 1))
                                PE.inc(i, 1)
                                mark(("qrpsb", hp, nb), PE)
                            mark(("qrps", hp), PE)
                            mark(("qrMM", hp), PE)
                        for nb in range(4):
                            if h % 2 == 0:
                                wait(t, ("qevacb", hp, nb))
                            for dc in range(ND):
                                i = t.matmul(psA[:, nb * 512:(nb + 1) * 512],
                                             wqnb[:, dc * NOPE:(dc + 1) * NOPE],
                                             xnt(dc)[:, nb * 512:(nb + 1) * 512],
                                             start=(dc == 0), stop=(dc == ND - 1))
                            PE.inc(i, 1)
                            mark(("qnpsb", h, nb), PE)
                        mark(("qnps", h), PE)
                        mark(("qnMM", h), PE)
                        wait(t, ("wkbl", h))
                        if h == 0:
                            wait(t, ("ckvevac", 19))
                        for nb in range(4):
                            wait(t, ("qnevacb", h, nb))
                            for lc in range(4):
                                i = t.matmul(psA[:, nb * 512:(nb + 1) * 512],
                                             knb[:, lc * 128:(lc + 1) * 128],
                                             cT[:, lc * 2048 + nb * 512: lc * 2048 + (nb + 1) * 512],
                                             start=(lc == 0), stop=(lc == 3))
                            PE.inc(i, 1)
                            mark(("knpsb", h, nb), PE)
                        mark(("knps", h), PE)
                        for tc in range(NT):
                            if tc % 4 == 0:
                                wait(t, ("knevacb", h, tc // 4))
                            for lc in range(4):
                                i = t.matmul(psA[:, tc * 128:(tc + 1) * 128],
                                             cT[:, lc * 2048 + tc * 128: lc * 2048 + (tc + 1) * 128],
                                             vwb[:, lc * 128:(lc + 1) * 128],
                                             start=(lc == 0), stop=(lc == 3))
                            if tc % 4 == 3:
                                PE.inc(i, 1)
                                mark(("vps", h, tc // 4), PE)
                        mark(("knvMM", h), PE)
                        qr = qrA if h % 2 == 0 else qrB
                        wait(t, ("qrready", hp))
                        def emit_trp(j_qt):
                            wait(t, ("norm", h, j_qt))
                            pj = probs2[j_qt % 2]
                            nbat = j_qt // 8 + 1
                            for b in range(nbat):
                                pt = psT0 if b % 2 == 0 else psT1
                                ptidx = b % 2
                                if psT_last[ptidx] is not None:
                                    wait(t, psT_last[ptidx])
                                n_in_b = min(8, j_qt + 1 - b * 8)
                                for j in range(n_in_b):
                                    jj = b * 8 + j
                                    i2 = t.transpose(pt[:, j * 128:(j + 1) * 128],
                                                     pj[:, jj * 128:(jj + 1) * 128], ident[:])
                                PE.inc(i2, 1)
                                mark(("trp", h, j_qt, b), PE)
                                psT_last[ptidx] = ("trpevac", h, j_qt, b)

                        def emit_av(j_qt):
                            nbat = j_qt // 8 + 1
                            wait(t, ("trpevac", h, j_qt, nbat - 1))
                            if psAV_last[j_qt % 2] is not None:
                                wait(t, psAV_last[j_qt % 2])
                            pT = probsT2[j_qt % 2]
                            for j in range(j_qt + 1):
                                i2 = t.matmul(psAV[j_qt % 2][:, 0:128],
                                              vs[:, j * 128:(j + 1) * 128],
                                              pT[:, j * 128:(j + 1) * 128],
                                              start=(j == 0), stop=(j == j_qt))
                            PE.inc(i2, 1)
                            mark(("av", h, j_qt), PE)
                            psAV_last[j_qt % 2] = ("avevac", h, j_qt)

                        for qt in range(NT):
                            kv = (qt + 1) * 128
                            base = (qt % 2) * 1024 if qt <= 7 else 0
                            if qt == 0:
                                wait(t, ("vevac", h, 1))
                            elif qt == 1:
                                wait(t, ("vevac", h, 3))
                            nkb = (kv + 511) // 512
                            nkb_prev = (qt * 128 + 511) // 512
                            if qt == 8:
                                wait(t, ("exp", h, 6))
                                wait(t, ("exp", h, 7))
                            for kb in range(nkb):
                                wdt = min(512, kv - kb * 512)
                                if 2 <= qt <= 7:
                                    pk = (qt - 1) * 128
                                    wait(t, ("expb", h, qt - 2, min(kb, (pk + 511) // 512 - 1)))
                                elif qt >= 9:
                                    wait(t, ("expb", h, qt - 1, min(kb, nkb_prev - 1)))
                                t.matmul(psA[:, base + kb * 512: base + kb * 512 + wdt],
                                         qn[:, qt * 128:(qt + 1) * 128],
                                         kns[:, kb * 512: kb * 512 + wdt],
                                         start=True, stop=False)
                            for kb in range(nkb):
                                wdt = min(512, kv - kb * 512)
                                diag_here = (kb == nkb - 1)
                                i = t.matmul(psA[:, base + kb * 512: base + kb * 512 + wdt],
                                             qr[0:64, qt * 128:(qt + 1) * 128],
                                             krope[0:64, kb * 512: kb * 512 + wdt],
                                             start=False, stop=not diag_here)
                                if not diag_here:
                                    PE.inc(i, 1)
                                    mark(("sb", h, qt, kb), PE)
                            i = t.matmul(psA[:, base + qt * 128: base + qt * 128 + 128],
                                         identNeg[:], upperT[:],
                                         start=False, stop=True)
                            PE.inc(i, 1)
                            mark(("sb", h, qt, nkb - 1), PE)
                            mark(("scores", h, qt), PE)
                            if qt >= 1:
                                emit_trp(qt - 1)
                            if qt >= 2:
                                emit_av(qt - 2)
                        emit_trp(NT - 1)
                        emit_av(NT - 2)
                        emit_av(NT - 1)
                        if hh == HPQ - 1:
                            wait(t, ("wol", q))
                            wait(t, ("avevac", h, NT - 1))
                            for tc in range(NT):
                                for dcol in range(4):
                                    slot = (tc * 4 + dcol) % 2
                                    psW = psAV[slot][:]
                                    if psAV_last[slot] is not None:
                                        wait(t, psAV_last[slot])
                                    for h2 in range(HPQ):
                                        i = t.matmul(psW,
                                                     oT[:, h2 * 2048 + tc * 128: h2 * 2048 + (tc + 1) * 128],
                                                     woq[:, h2 * 2048 + dcol * 512: h2 * 2048 + (dcol + 1) * 512],
                                                     start=(h2 == 0), stop=(h2 == HPQ - 1))
                                    PE.inc(i, 1)
                                    mark(("wops", q, tc, dcol), PE)
                                    psAV_last[slot] = ("stgevac", q, tc, dcol)
                            mark(("woMM", q), PE)

                def emit_rope_math(v, hp, h):
                    if h % HPQ == 0 and h // HPQ >= 1:
                        wait(v, ("scrw", h // HPQ - 1, NT - 1))
                        wait(v, ("scrw", h // HPQ - 1, NT - 2))
                    if hp >= 1:
                        wait(v, ("qrready", hp - 1))
                    wait(v, ("qodd", hp))
                    v.tensor_tensor(out=stage[0][0:64, :], in0=probs[0:64, :], in1=cosr[:], op=ALU.mult)
                    v.tensor_tensor(out=stage[1][0:64, :], in0=probsT[0:64, :], in1=sinr[:], op=ALU.mult)
                    v.drain()
                    v.tensor_tensor(out=stage[0][0:64, :], in0=stage[0][0:64, :], in1=stage[1][0:64, :], op=ALU.subtract)
                    v.drain()
                    v.tensor_tensor(out=stage[1][0:64, :], in0=probsT[0:64, :], in1=cosr[:], op=ALU.mult)
                    v.tensor_tensor(out=rtmp[:], in0=probs[0:64, :], in1=sinr[:], op=ALU.mult)
                    v.drain()
                    v.tensor_tensor(out=stage[1][0:64, :], in0=stage[1][0:64, :], in1=rtmp[:], op=ALU.add)
                    v.drain()
                    v.tensor_scalar(out=stage[0][0:64, :], in0=stage[0][0:64, :],
                                    scalar1=SCALE, scalar2=None, op0=ALU.mult)
                    i = v.tensor_scalar(out=stage[1][0:64, :], in0=stage[1][0:64, :],
                                        scalar1=SCALE, scalar2=None, op0=ALU.mult)
                    v.drain()
                    VE.inc(i, 1)
                    mark(("ropemath", hp), VE)

                @block.vector
                def _(v):
                    for h in range(H):
                        hp, q, hh = h // 2, h // HPQ, h % HPQ
                        if h % 2 == 0:
                            emit_rope_math(v, hp, h)
                        def emit_trpevac(j_qt):
                            nbat = j_qt // 8 + 1
                            for b in range(nbat):
                                pt = psT0 if b % 2 == 0 else psT1
                                wait(v, ("trp", h, j_qt, b))
                                wdt = min(8, j_qt + 1 - b * 8) * 128
                                i2 = v.tensor_copy(probsT2[j_qt % 2][:, b * 1024: b * 1024 + wdt],
                                                   pt[:, 0:wdt])
                                VE.inc(i2, 1)
                                mark(("trpevac", h, j_qt, b), VE)

                        def emit_avevac(j_qt):
                            wait(v, ("av", h, j_qt))
                            i2 = v.tensor_copy(oT[:, hh * 2048 + j_qt * 128: hh * 2048 + (j_qt + 1) * 128],
                                               psAV[j_qt % 2][:, 0:128])
                            VE.inc(i2, 1)
                            mark(("avevac", h, j_qt), VE)

                        for qt in range(NT):
                            kv = (qt + 1) * 128
                            base = (qt % 2) * 1024 if qt <= 7 else 0
                            nkb = (kv + 511) // 512
                            if nkb == 1:
                                wait(v, ("sb", h, qt, 0))
                                i = v.reduce_max(colD[:], psA[:, base:base + kv], axis=AX.X, negate=True)
                                v.drain()
                            else:
                                for kb in range(nkb):
                                    wdt = min(512, kv - kb * 512)
                                    wait(v, ("sb", h, qt, kb))
                                    v.reduce_max(colM[:, kb:kb + 1],
                                                 psA[:, base + kb * 512: base + kb * 512 + wdt], axis=AX.X)
                                v.drain()
                                i = v.reduce_max(colD[:], colM[:, 0:nkb], axis=AX.X, negate=True)
                                v.drain()
                            VE.inc(i, 1)
                            mark(("smax", h, qt), VE)
                            wait(v, ("exp", h, qt))
                            nkb2 = (kv + 511) // 512
                            v.reduce_sum(colB[:], colB4[:, 0:nkb2], axis=AX.X)
                            v.drain()
                            v.reciprocal(colC[:], colB[:])
                            v.drain()
                            i = v.tensor_scalar(out=probs2[qt % 2][:, 0:kv], in0=probs2[qt % 2][:, 0:kv],
                                                scalar1=colC[:], scalar2=None, op0=ALU.mult)
                            v.drain()
                            VE.inc(i, 1)
                            mark(("norm", h, qt), VE)
                            if qt >= 1:
                                emit_trpevac(qt - 1)
                            if qt >= 2:
                                emit_avevac(qt - 2)
                        emit_trpevac(NT - 1)
                        emit_avevac(NT - 2)
                        emit_avevac(NT - 1)
                        if hh == HPQ - 1:
                            for tc in range(NT):
                                for dcol in range(4):
                                    slot = (tc * 4 + dcol) % 2
                                    wait(v, ("wops", q, tc, dcol))
                                    if tc == 0 and dcol == 0:
                                        wait(v, ("qrready", 2 * q + 1))
                                    if dcol == 0:
                                        if tc >= 2:
                                            wait(v, ("scrw", q, tc - 2))
                                        elif q >= 1:
                                            wait(v, ("scrw", q - 1, NT - 2 + tc))
                                    if q == 0:
                                        if dcol == 0:
                                            wait(v, ("xw", tc))
                                        i = v.tensor_tensor(out=stage[tc % 2][:, dcol * 512:(dcol + 1) * 512],
                                                            in0=psAV[slot][:],
                                                            in1=xw[:, dcol * 512:(dcol + 1) * 512],
                                                            op=ALU.add)
                                    else:
                                        i = v.tensor_copy(stage[tc % 2][:, dcol * 512:(dcol + 1) * 512],
                                                          psAV[slot][:])
                                    VE.inc(i, 1)
                                    mark(("stgevac", q, tc, dcol), VE)
                                mark(("stg", q, tc), VE)

                @block.scalar
                def _(s):
                    for h in range(H):
                        hp = h // 2
                        if h % 2 == 0:
                            if h > 0:
                                wait(s, ("av", h - 1, NT - 1))
                            for nb in range(4):
                                wait(s, ("qrpsb", hp, nb))
                                s.activation(probs[:, nb * 512:(nb + 1) * 512],
                                             psA[:, nb * 512:(nb + 1) * 512], AF.Copy)
                                i = s.drain(fusable=False)
                                SE.inc(i, 1)
                                mark(("qevacb", hp, nb), SE)
                            mark(("qevac", hp), SE)
                        for nb in range(4):
                            wait(s, ("qnpsb", h, nb))
                            s.activation(qn[:, nb * 512:(nb + 1) * 512],
                                         psA[:, nb * 512:(nb + 1) * 512], AF.Copy, scale=SCALE)
                            i = s.drain(fusable=False)
                            SE.inc(i, 1)
                            mark(("qnevacb", h, nb), SE)
                        mark(("qnevac", h), SE)
                        for nb in range(4):
                            wait(s, ("knpsb", h, nb))
                            s.activation(kns[:, nb * 512:(nb + 1) * 512],
                                         psA[:, nb * 512:(nb + 1) * 512], AF.Copy)
                            i = s.drain(fusable=False)
                            SE.inc(i, 1)
                            mark(("knevacb", h, nb), SE)
                        mark(("knevac", h), SE)
                        if h == 0:
                            wait(s, ("comb", NT - 1))
                        for g4 in range(4):
                            wait(s, ("vps", h, g4))
                            s.activation(vs[:, g4 * 512:(g4 + 1) * 512],
                                         psA[:, g4 * 512:(g4 + 1) * 512], AF.Copy)
                            i = s.drain(fusable=False)
                            SE.inc(i, 1)
                            mark(("vevac", h, g4), SE)
                        for qt in range(NT):
                            kv = (qt + 1) * 128
                            base = (qt % 2) * 1024 if qt <= 7 else 0
                            nkb = (kv + 511) // 512
                            wait(s, ("smax", h, qt))
                            if qt >= 2:
                                wait(s, ("trp", h, qt - 2, (qt - 2) // 8))
                            elif h >= 1:
                                pq = NT - 2 + qt
                                wait(s, ("trp", h - 1, pq, pq // 8))
                            for kb in range(nkb):
                                wdt = min(512, kv - kb * 512)
                                i = s.activation(probs2[qt % 2][:, kb * 512: kb * 512 + wdt],
                                                 psA[:, base + kb * 512: base + kb * 512 + wdt], AF.Exp,
                                                 bias=colD[:], accum_out=colB4[:, kb:kb + 1])
                                SE.inc(i, 1)
                                mark(("expb", h, qt, kb), SE)
                            i = s.drain(fusable=False)
                            SE.inc(i, 1)
                            mark(("exp", h, qt), SE)


        with ExitStack() as P:
            EP = P.enter_context
            acc = EP(nc.sbuf_tensor("acc", [128, NT * 2048], BF16))

            with ExitStack() as P5:
                E5 = P5.enter_context
                hT = E5(nc.sbuf_tensor("hT", [128, NFC * 2048], BF16))
                gb = [E5(nc.sbuf_tensor(f"gb{i}", [128, 2048], BF16)) for i in range(2)]
                ub = [E5(nc.sbuf_tensor(f"ub{i}", [128, 2048], BF16)) for i in range(2)]
                db = E5(nc.sbuf_tensor("db", [128, NFC * 2048], BF16))
                sgt = [E5(nc.sbuf_tensor(f"sgt{i}", [128, 512], F32)) for i in range(2)]
                psG = [E5(nc.psum_tensor(f"psG{i}", [128, 512], F32)) for i in range(2)]
                psU = [E5(nc.psum_tensor(f"psU{i}", [128, 512], F32)) for i in range(2)]
                psD = [E5(nc.psum_tensor(f"psD{i}", [128, 512], F32)) for i in range(2)]

                psG_last = {0: None, 1: None}
                psU_last = {0: None, 1: None}
                psD_last = {0: None, 1: None}
                gslot_last = {0: None, 1: None}
                uslot_last = {0: None, 1: None}

                @block.gpsimd
                def _(g):
                    wait(g, ("woMM", NQ - 1))
                    wait(g, ("scrw", NQ - 1, NT - 1))
                    wait(g, ("scrw", NQ - 1, NT - 2))
                    for e in range(NEXP):
                        for fc in range(NFC):
                            if fc == 2:
                                if e >= 1:
                                    wait(g, ("dwMM", e - 1))
                                i = g.dma_start(db[:].rearrange("p (fc d) -> p fc d", d=2048),
                                                dw[e].rearrange("(fc p) d -> p fc d", p=128))
                                sems["dDW"].inc(i, 16)
                                mark(("dwl", e), sems["dDW"])
                            sl = fc % 2
                            if gslot_last[sl] is not None:
                                wait(g, gslot_last[sl])
                            i = g.dma_start(gb[sl][:].rearrange("p (dc f) -> p dc f", f=128),
                                            gw[e].rearrange("(dc p) f -> p dc f", p=128)[:, :, fc * 128:(fc + 1) * 128])
                            sems[f"dG{sl}"].inc(i, 16)
                            mark(("gl", e, fc), sems[f"dG{sl}"])
                            i = g.dma_start(ub[sl][:].rearrange("p (dc f) -> p dc f", f=128),
                                            uw[e].rearrange("(dc p) f -> p dc f", p=128)[:, :, fc * 128:(fc + 1) * 128])
                            sems[f"dU{sl}"].inc(i, 16)
                            mark(("ul", e, fc), sems[f"dU{sl}"])
                            gslot_last[sl] = ("gMM", e, fc)
                            uslot_last[sl] = ("gMM", e, fc)

                @block.tensor
                def _(t):
                    wait(t, ("stgevac", NQ - 1, NT - 1, 3))
                    wait(t, ("scrw", NQ - 1, NT - 1))
                    wait(t, ("scrw", NQ - 1, NT - 2))
                    for e in range(NEXP):
                        for fc in range(NFC):
                            sl = fc % 2
                            wait(t, ("gl", e, fc))
                            wait(t, ("ul", e, fc))
                            for tb in range(4):
                                k = fc * 4 + tb
                                ps_g, ps_u = psG[k % 2], psU[k % 2]
                                if psG_last[k % 2] is not None:
                                    wait(t, psG_last[k % 2])
                                for dc in range(ND):
                                    i = t.matmul(ps_g[:], gb[sl][:, dc * 128:(dc + 1) * 128],
                                                 xnt(dc)[:, tb * 512:(tb + 1) * 512],
                                                 start=(dc == 0), stop=(dc == ND - 1))
                                PE.inc(i, 1)
                                mark(("gps", e, fc, tb), PE)
                                for dc in range(ND):
                                    i = t.matmul(ps_u[:], ub[sl][:, dc * 128:(dc + 1) * 128],
                                                 xnt(dc)[:, tb * 512:(tb + 1) * 512],
                                                 start=(dc == 0), stop=(dc == ND - 1))
                                PE.inc(i, 1)
                                mark(("ups", e, fc, tb), PE)
                                psG_last[k % 2] = ("hT", e, fc, tb)
                                psU_last[k % 2] = ("hT", e, fc, tb)
                            mark(("gMM", e, fc), PE)
                        wait(t, ("dwl", e))
                        wait(t, ("hT", e, NFC - 1, 3))
                        for tc in range(NT):
                            for dcol in range(4):
                                k = tc * 4 + dcol
                                ps_d = psD[k % 2]
                                if psD_last[k % 2] is not None:
                                    wait(t, psD_last[k % 2])
                                for fc in range(NFC):
                                    i = t.matmul(ps_d[:],
                                                 hT[:, fc * 2048 + tc * 128: fc * 2048 + (tc + 1) * 128],
                                                 db[:, fc * 2048 + dcol * 512: fc * 2048 + (dcol + 1) * 512],
                                                 start=(fc == 0), stop=(fc == NFC - 1))
                                PE.inc(i, 1)
                                mark(("dps", e, tc, dcol), PE)
                                psD_last[k % 2] = ("accw", e, tc, dcol)
                        mark(("dwMM", e), PE)

                @block.scalar
                def _(s):
                    for e in range(NEXP):
                        for fc in range(NFC):
                            for tb in range(4):
                                k = fc * 4 + tb
                                wait(s, ("gps", e, fc, tb))
                                if (e, fc, tb) != (0, 0, 0):
                                    pass
                                prev = (e, fc, tb - 2) if tb >= 2 else None
                                if k >= 2 or e > 0:
                                    pk = k - 2
                                    pe_, pfc, ptb = e, pk // 4, pk % 4
                                    if pk < 0:
                                        pe_, pfc, ptb = e - 1, (pk + 24) // 4, (pk + 24) % 4
                                    wait(s, ("hT", pe_, pfc, ptb))
                                i = s.activation(sgt[k % 2][:], psG[k % 2][:], AF.Silu)
                                s.drain()
                                SE.inc(i, 1)
                                mark(("silu", e, fc, tb), SE)

                @block.vector
                def _(v):
                    for e in range(NEXP):
                        for fc in range(NFC):
                            for tb in range(4):
                                k = fc * 4 + tb
                                wait(v, ("silu", e, fc, tb))
                                wait(v, ("ups", e, fc, tb))
                                i = v.tensor_tensor(out=hT[:, fc * 2048 + tb * 512: fc * 2048 + (tb + 1) * 512],
                                                    in0=sgt[k % 2][:], in1=psU[k % 2][:], op=ALU.mult)
                                VE.inc(i, 1)
                                mark(("hT", e, fc, tb), VE)
                        for tc in range(NT):
                            for dcol in range(4):
                                k = tc * 4 + dcol
                                wait(v, ("dps", e, tc, dcol))
                                cslice = comb[:, tc * NEXP + e: tc * NEXP + e + 1]
                                dst = acc[:, tc * 2048 + dcol * 512: tc * 2048 + (dcol + 1) * 512]
                                if e == 0:
                                    i = v.tensor_scalar(out=dst, in0=psD[k % 2][:], scalar1=cslice,
                                                        scalar2=None, op0=ALU.mult)
                                else:
                                    i = v.scalar_tensor_tensor(out=dst, in0=psD[k % 2][:], scalar=cslice,
                                                               in1=dst, op0=ALU.mult, op1=ALU.add)
                                VE.inc(i, 1)
                                mark(("accw", e, tc, dcol), VE)

            with ExitStack() as P6:
                E6 = P6.enter_context
                rb = [E6(nc.sbuf_tensor(f"rb{i}", [128, 2048], F32)) for i in range(4)]
                ost = [E6(nc.sbuf_tensor(f"ost{i}", [128, 2048], F32)) for i in range(2)]

                @block.sync
                def _(sy):
                    wait(sy, ("scrw", NQ - 1, NT - 1))
                    wait(sy, ("scrw", NQ - 1, NT - 2))
                    wait(sy, ("dwMM", NEXP - 1))
                    wait(sy, ("accw", NEXP - 1, NT - 1, 3))
                    for tc in range(NT):
                        if tc >= 1:
                            wait(sy, ("madd", tc - 1, 1))
                            i = sy.dma_start(out[(tc - 1) * 128:tc * 128, :], ost[(tc - 1) % 2][:])
                            sems[f"dOUT{(tc - 1) % 2}"].inc(i, 16)
                            mark(("outw", tc - 1), sems[f"dOUT{(tc - 1) % 2}"])
                        if tc >= 2:
                            wait(sy, ("madd", tc - 2, 0))
                        i = sy.dma_start(rb[2 * (tc % 2)][:], scr[0][tc * 128:(tc + 1) * 128, :])
                        sems[f"dMRG{tc % 2}"].inc(i, 16)
                        mark(("mld", tc, 0), sems[f"dMRG{tc % 2}"])
                    wait(sy, ("madd", NT - 1, 1))
                    i = sy.dma_start(out[(NT - 1) * 128:NT * 128, :], ost[(NT - 1) % 2][:])
                    sems[f"dOUT{(NT - 1) % 2}"].inc(i, 16)
                    mark(("outw", NT - 1), sems[f"dOUT{(NT - 1) % 2}"])
                    sy.wait_ge(sems["dOUT0"].h, sems["dOUT0"].n)
                    sy.wait_ge(sems["dOUT1"].h, sems["dOUT1"].n)

                @block.gpsimd
                def _(g):
                    wait(g, ("scrw", NQ - 1, NT - 1))
                    wait(g, ("scrw", NQ - 1, NT - 2))
                    wait(g, ("dwMM", NEXP - 1))
                    wait(g, ("accw", NEXP - 1, NT - 1, 3))
                    for tc in range(NT):
                        if tc >= 2:
                            wait(g, ("madd", tc - 2, 0))
                        i = g.dma_start(rb[2 * (tc % 2) + 1][:], scr[2][tc * 128:(tc + 1) * 128, :])
                        sems[f"dMG{tc % 2}"].inc(i, 16)
                        mark(("gld", tc, 0), sems[f"dMG{tc % 2}"])

                @block.vector
                def _(v):
                    for tc in range(NT):
                        wait(v, ("mld", tc, 0))
                        wait(v, ("gld", tc, 0))
                        if tc >= 2:
                            wait(v, ("outw", tc - 2))
                        v.tensor_tensor(out=ost[tc % 2][:], in0=rb[2 * (tc % 2)][:],
                                        in1=rb[2 * (tc % 2) + 1][:], op=ALU.add)
                        i = v.drain(fusable=False)
                        VE.inc(i, 1)
                        mark(("madd", tc, 0), VE)
                        wait(v, ("accw", NEXP - 1, tc, 3))
                        v.tensor_tensor(out=ost[tc % 2][:], in0=ost[tc % 2][:],
                                        in1=acc[:, tc * 2048:(tc + 1) * 2048], op=ALU.add)
                        i = v.drain(fusable=False)
                        VE.inc(i, 1)
                        mark(("madd", tc, 1), VE)

    return nc, marks


_CACHED = {}


def _get_nc():
    if "nc" not in _CACHED:
        _, marks = build_nc(None)
        nc, _ = build_nc(marks)
        _CACHED["nc"] = nc
    return _CACHED["nc"]


def _host_prep(inputs):
    wq = np.asarray(inputs["L0_attn_wq"], dtype=np.float32)
    wkv_a = np.asarray(inputs["L0_attn_wkv_a"], dtype=np.float32)
    nope_cols = np.concatenate([np.arange(h * 192, h * 192 + 128) for h in range(H)])
    wqn_ = np.ascontiguousarray(wq[:, nope_cols])
    rope_cols = []
    for hp in range(H // 2):
        b0 = (2 * hp) * 192 + 128
        b1 = (2 * hp + 1) * 192 + 128
        rope_cols += [b0 + 2 * i for i in range(32)]
        rope_cols += [b1 + 2 * i for i in range(32)]
        rope_cols += [b0 + 2 * i + 1 for i in range(32)]
        rope_cols += [b1 + 2 * i + 1 for i in range(32)]
    wqr_ = np.ascontiguousarray(wq[:, rope_cols])
    kpe_cols = [LORA + 2 * i for i in range(32)] + [LORA + 2 * i + 1 for i in range(32)]
    wkva_ = np.ascontiguousarray(
        np.concatenate([wkv_a[:, :LORA], wkv_a[:, kpe_cols]], axis=1))
    e = np.arange(32, dtype=np.float32) * np.float32(2.0 / 64.0)
    invf_ = (np.float32(1.0) / np.power(np.float32(10000.0), e, dtype=np.float32)
             ).astype(np.float32).reshape(32, 1)
    return wqn_, wqr_, wkva_, invf_


def kernel(L0_attn_x, L0_attn_wq, L0_attn_wkv_a, L0_attn_wkv_b, L0_attn_wo,
           L0_ffn_router_w, L0_ffn_gate_w, L0_ffn_up_w, L0_ffn_down_w):
    x = np.ascontiguousarray(np.asarray(L0_attn_x, dtype=np.float32))
    wqn_, wqr_, wkva_, invf_ = _host_prep({
        "L0_attn_wq": L0_attn_wq, "L0_attn_wkv_a": L0_attn_wkv_a})
    wkvb_ = np.ascontiguousarray(np.asarray(L0_attn_wkv_b, dtype=np.float32))
    wo_ = np.ascontiguousarray(np.asarray(L0_attn_wo, dtype=np.float32))
    rw_ = np.ascontiguousarray(np.asarray(L0_ffn_router_w, dtype=np.float32))
    gw_ = np.ascontiguousarray(np.asarray(L0_ffn_gate_w, dtype=np.float32))
    uw_ = np.ascontiguousarray(np.asarray(L0_ffn_up_w, dtype=np.float32))
    dw_ = np.ascontiguousarray(np.asarray(L0_ffn_down_w, dtype=np.float32))

    nc = _get_nc()
    in_maps = []
    for core in range(8):
        in_maps.append({
            "x": np.ascontiguousarray(x[core]),
            "wqn": wqn_, "wqr": wqr_, "wkva": wkva_, "wkvb": wkvb_,
            "wo": wo_, "rw": rw_, "gw": gw_, "uw": uw_, "dw": dw_,
            "invf": invf_,
        })
    res = run_bass_kernel_spmd(nc, in_maps, core_ids=list(range(8)))
    return np.stack([np.asarray(r["out"]) for r in res.results], axis=0)


# version 23
# speedup vs baseline: 1.0220x; 1.0220x over previous
import numpy as np
from contextlib import ExitStack

import concourse.bass as bass
import concourse.mybir as mybir
from concourse.bass_utils import run_bass_kernel_spmd

F32 = mybir.dt.float32
BF16 = mybir.dt.bfloat16
I32 = mybir.dt.int32
AF = mybir.ActivationFunctionType
ALU = mybir.AluOpType
AX = mybir.AxisListType

S = 2048
D = 2048
H = 16
NOPE = 128
ROPE = 64
VD = 128
LORA = 512
NEXP = 16
FF = 768
NT = S // 128
ND = D // 128
NFC = FF // 128
NQ = 4
HPQ = H // NQ
SCALE = float(1.0 / np.sqrt(NOPE + ROPE))
MAGIC = 12582912.0
TWO_PI = 6.283185307179586
CW1 = 6.28125
CW2 = float(np.float32(TWO_PI - CW1))
CW3 = float(TWO_PI - CW1 - np.float64(np.float32(TWO_PI - CW1)))
NEG_BIG = -1e30
LN_EPS = 1e-5


class DSem:
    def __init__(self, nc, stack, name):
        self.h = stack.enter_context(nc.semaphore(name))
        self.n = 0
        self.name = name

    def inc(self, instr, k):
        instr.then_inc(self.h, k)
        self.n += k
        return self.n


def build_nc(prev_marks):
    pass2 = prev_marks is not None
    marks = {}
    nc = bass.Bass("TRN2", target_bir_lowering=False)

    x = nc.dram_tensor("x", [S, D], F32, kind="ExternalInput").ap()
    wqn = nc.dram_tensor("wqn", [D, H * NOPE], F32, kind="ExternalInput").ap()
    wqr = nc.dram_tensor("wqr", [D, (H // 2) * 128], F32, kind="ExternalInput").ap()
    wkva = nc.dram_tensor("wkva", [D, LORA + ROPE], F32, kind="ExternalInput").ap()
    wkvb = nc.dram_tensor("wkvb", [LORA, H * 256], F32, kind="ExternalInput").ap()
    wo = nc.dram_tensor("wo", [H * VD, D], F32, kind="ExternalInput").ap()
    rw = nc.dram_tensor("rw", [D, NEXP], F32, kind="ExternalInput").ap()
    gw = nc.dram_tensor("gw", [NEXP, D, FF], F32, kind="ExternalInput").ap()
    uw = nc.dram_tensor("uw", [NEXP, D, FF], F32, kind="ExternalInput").ap()
    dw = nc.dram_tensor("dw", [NEXP, FF, D], F32, kind="ExternalInput").ap()
    invf = nc.dram_tensor("invf", [32, 1], F32, kind="ExternalInput").ap()
    out = nc.dram_tensor("out", [S, D], F32, kind="ExternalOutput").ap()
    scr = [nc.dram_tensor(f"scr{q}", [S, D], F32).ap() for q in range(NQ)]

    sem_by_name = {}

    def mark(key, sem):
        assert key not in marks, key
        marks[key] = (sem.name, sem.n)

    def wait(eng, key):
        if pass2:
            name, v = prev_marks[key]
            eng.wait_ge(sem_by_name[name], v)
        else:
            assert key not in marks or True
            eng.wait_ge(sem_by_name["PE"], 0)

    with ExitStack() as G:
        EG = G.enter_context
        sems = {}
        for nm in ["PE", "VE", "SE", "PO",
                   "dX0", "dX1", "dWQN", "dWQR", "dWKB", "dWO", "dKVA", "dRW", "dIF",
                   "dG0", "dG1", "dU0", "dU1", "dDW", "dSCR0", "dSCR1", "dOUT0", "dOUT1",
                   "dMISC", "dMRG0", "dMRG1", "dMG0", "dMG1", "dSCRA0", "dSCRA1", "dQR", "dQRS"]:
            sems[nm] = DSem(nc, G, nm)
            sem_by_name[nm] = sems[nm].h
        PE, VE, SE, PO = sems["PE"], sems["VE"], sems["SE"], sems["PO"]

        xnT = EG(nc.sbuf_tensor("xnT", [128, ND * 2048], BF16))
        comb = EG(nc.sbuf_tensor("comb", [128, NT * NEXP], F32))
        ident = EG(nc.sbuf_tensor("ident", [128, 128], BF16))
        identNeg = EG(nc.sbuf_tensor("identNeg", [128, 128], BF16))
        colA = EG(nc.sbuf_tensor("colA", [128, 1], F32))
        colB = EG(nc.sbuf_tensor("colB", [128, 1], F32))
        colC = EG(nc.sbuf_tensor("colC", [128, 1], F32))
        colD = EG(nc.sbuf_tensor("colD", [128, 1], F32))
        colE = EG(nc.sbuf_tensor("colE", [128, 1], F32))
        block = EG(nc.Block())

        def xnt(dc):
            return xnT[:, dc * 2048:(dc + 1) * 2048]

        with ExitStack() as M:
            EM = M.enter_context
            cosr = EM(nc.sbuf_tensor("cosr", [64, 2048], BF16))
            sinr = EM(nc.sbuf_tensor("sinr", [64, 2048], BF16))
            cT = EM(nc.sbuf_tensor("cT", [128, 4 * 2048], BF16))
            krope = EM(nc.sbuf_tensor("krope", [64, 2048], BF16))
            wqnb = EM(nc.sbuf_tensor("wqnb", [128, 2048], BF16))
            wqrb = EM(nc.sbuf_tensor("wqrb", [128, 2048], BF16))
            knb = EM(nc.sbuf_tensor("knb", [128, 512], BF16))
            vwb = EM(nc.sbuf_tensor("vwb", [128, 512], BF16))

            with ExitStack() as P:
                EP = P.enter_context
                xbuf = [EP(nc.sbuf_tensor(f"xbuf{i}", [128, 2048], F32)) for i in range(2)]
                trash = EP(nc.sbuf_tensor("trash", [128, 2048], BF16))
                xnb = EP(nc.sbuf_tensor("xnb", [128, 2048], BF16))
                tidx = EP(nc.sbuf_tensor("tidx", [32, 2048], F32))
                ivf = EP(nc.sbuf_tensor("ivf", [32, 1], F32))
                frq = EP(nc.sbuf_tensor("frq", [32, 2048], F32))
                kfb = EP(nc.sbuf_tensor("kfb", [32, 2048], F32))
                lnc = EP(nc.sbuf_tensor("lnc", [128, 8], F32))
                psT0 = EP(nc.psum_tensor("psT0", [128, 1024], BF16))
                psT1 = EP(nc.psum_tensor("psT1", [128, 1024], BF16))

                @block.sync
                def _(sy):
                    i = sy.dma_start(ivf[:], invf)
                    sems["dIF"].inc(i, 16)
                    mark(("ivf",), sems["dIF"])
                    for tc in range(NT):
                        if tc >= 2:
                            wait(sy, ("lnused", tc - 2))
                        i = sy.dma_start(xbuf[tc % 2][:], x[tc * 128:(tc + 1) * 128, :])
                        sems[f"dX{tc % 2}"].inc(i, 16)
                        mark(("xload", tc), sems[f"dX{tc % 2}"])

                @block.gpsimd
                def _(g):
                    g.memset(ident[:], 0.0)
                    g.drain()
                    i = g.affine_select(out=ident[:], in_=ident[:], compare_op=ALU.not_equal,
                                        fill=1.0, base=0, pattern=[[-1, 128]], channel_multiplier=1)
                    PO.inc(i, 1)
                    mark(("ident",), PO)
                    g.memset(identNeg[:], 0.0)
                    g.drain()
                    i = g.affine_select(out=identNeg[:], in_=identNeg[:], compare_op=ALU.is_ge,
                                        fill=NEG_BIG, base=0, pattern=[[1, 128]], channel_multiplier=-1)
                    PO.inc(i, 1)
                    mark(("cmask",), PO)
                    i = g.iota(tidx[:], pattern=[[1, 2048]], base=0, channel_multiplier=0,
                               allow_small_or_imprecise_dtypes=True)
                    PO.inc(i, 1)
                    mark(("tidx",), PO)

                @block.vector
                def _(v):
                    wait(v, ("tidx",))
                    wait(v, ("ivf",))
                    v.tensor_scalar(out=frq[:], in0=tidx[:], scalar1=ivf[:], scalar2=None, op0=ALU.mult)
                    v.drain()
                    v.tensor_scalar(out=kfb[:], in0=frq[:], scalar1=float(1.0 / TWO_PI),
                                    scalar2=MAGIC, op0=ALU.mult, op1=ALU.add)
                    v.drain()
                    v.tensor_scalar(out=kfb[:], in0=kfb[:], scalar1=MAGIC, scalar2=None, op0=ALU.subtract)
                    v.drain()
                    v.scalar_tensor_tensor(out=frq[:], in0=kfb[:], scalar=-CW1, in1=frq[:],
                                           op0=ALU.mult, op1=ALU.add)
                    v.drain()
                    v.scalar_tensor_tensor(out=frq[:], in0=kfb[:], scalar=-CW2, in1=frq[:],
                                           op0=ALU.mult, op1=ALU.add)
                    v.drain()
                    v.tensor_scalar(out=frq[:], in0=frq[:], scalar1=float(np.pi), scalar2=None, op0=ALU.min)
                    v.drain()
                    i = v.tensor_scalar(out=frq[:], in0=frq[:], scalar1=float(-np.pi), scalar2=None, op0=ALU.max)
                    v.drain()
                    VE.inc(i, 1)
                    mark(("ropearg",), VE)
                    v.tensor_scalar(out=kfb[:], in0=frq[:], scalar1=-1.0, scalar2=None, op0=ALU.mult)
                    v.drain()
                    v.tensor_tensor(out=kfb[:], in0=kfb[:], in1=frq[:], op=ALU.max)
                    v.drain()
                    i = v.tensor_scalar(out=kfb[:], in0=kfb[:], scalar1=-1.0, scalar2=float(np.pi / 2),
                                        op0=ALU.mult, op1=ALU.add)
                    v.drain()
                    VE.inc(i, 1)
                    mark(("cosarg",), VE)
                    def emit_xnb(j):
                        wait(v, ("rstd", j))
                        v.tensor_scalar(out=lnc[:, j % 2:j % 2 + 1], in0=lnc[:, j % 2:j % 2 + 1],
                                        scalar1=-1.0, scalar2=None, op0=ALU.mult)
                        v.drain()
                        i = v.tensor_scalar(out=xnb[:], in0=xbuf[j % 2][:],
                                            scalar1=lnc[:, j % 2:j % 2 + 1],
                                            scalar2=lnc[:, 4 + j % 2:5 + j % 2],
                                            op0=ALU.add, op1=ALU.mult)
                        v.drain()
                        VE.inc(i, 1)
                        mark(("xn", j), VE)
                        mark(("lnused", j), VE)

                    def emit_evac(j):
                        for half in range(2):
                            pt = psT0 if half == 0 else psT1
                            wait(v, ("xpose", j, half))
                            dst = xnT[:, half * 8 * 2048: half * 8 * 2048 + 8 * 2048].rearrange(
                                "p (dc c) -> p dc c", c=2048)[:, :, j * 128:j * 128 + 128]
                            src = pt[:].rearrange("p (j c) -> p j c", c=128)
                            i = v.tensor_copy(dst, src)
                            VE.inc(i, 1)
                            mark(("xnTevac", j, half), VE)

                    for tc in range(NT):
                        wait(v, ("xload", tc))
                        xb = xbuf[tc % 2]
                        v.reduce_sum(colA[:], xb[:], axis=AX.X)
                        v.drain()
                        v.tensor_scalar(out=lnc[:, tc % 2:tc % 2 + 1], in0=colA[:],
                                        scalar1=float(1.0 / D), scalar2=None, op0=ALU.mult)
                        v.drain()
                        wait(v, ("sq", tc))
                        v.tensor_scalar(out=lnc[:, 2 + tc % 2:3 + tc % 2], in0=colB[:],
                                        scalar1=float(1.0 / D), scalar2=None, op0=ALU.mult)
                        v.tensor_tensor(out=colA[:], in0=lnc[:, tc % 2:tc % 2 + 1],
                                        in1=lnc[:, tc % 2:tc % 2 + 1], op=ALU.mult)
                        v.drain()
                        v.tensor_tensor(out=lnc[:, 2 + tc % 2:3 + tc % 2],
                                        in0=lnc[:, 2 + tc % 2:3 + tc % 2], in1=colA[:], op=ALU.subtract)
                        v.drain()
                        i = v.tensor_scalar(out=lnc[:, 2 + tc % 2:3 + tc % 2],
                                            in0=lnc[:, 2 + tc % 2:3 + tc % 2], scalar1=LN_EPS,
                                            scalar2=None, op0=ALU.add)
                        v.drain()
                        VE.inc(i, 1)
                        mark(("var", tc), VE)
                        if tc >= 1:
                            emit_xnb(tc - 1)
                            emit_evac(tc - 1)
                    emit_xnb(NT - 1)
                    emit_evac(NT - 1)

                @block.scalar
                def _(s):
                    wait(s, ("ropearg",))
                    wait(s, ("cosarg",))
                    s.activation(sinr[0:32, :], frq[:], AF.Sin)
                    i = s.activation(cosr[0:32, :], kfb[:], AF.Sin)
                    SE.inc(i, 1)
                    mark(("cossin",), SE)
                    for tc in range(NT):
                        wait(s, ("xload", tc))
                        xb = xbuf[tc % 2]
                        i = s.activation(trash[:], xb[:], AF.Square, accum_out=colB[:])
                        s.drain()
                        SE.inc(i, 1)
                        mark(("sq", tc), SE)
                        wait(s, ("var", tc))
                        s.activation(lnc[:, 4 + tc % 2:5 + tc % 2], lnc[:, 2 + tc % 2:3 + tc % 2], AF.Ln)
                        s.drain()
                        i = s.activation(lnc[:, 4 + tc % 2:5 + tc % 2],
                                         lnc[:, 4 + tc % 2:5 + tc % 2], AF.Exp, scale=-0.5)
                        s.drain()
                        SE.inc(i, 1)
                        mark(("rstd", tc), SE)


                @block.gpsimd
                def _(g):
                    wait(g, ("cossin",))
                    g.dma_start(cosr[32:64, :], cosr[0:32, :]).then_inc(sems["dMISC"].h, 16)
                    sems["dMISC"].n += 16
                    i = g.dma_start(sinr[32:64, :], sinr[0:32, :])
                    sems["dMISC"].inc(i, 16)
                    mark(("cossinrep",), sems["dMISC"])

                @block.tensor
                def _(t):
                    wait(t, ("ident",))
                    for tc in range(NT):
                        wait(t, ("xn", tc))
                        for half in range(2):
                            pt = psT0 if half == 0 else psT1
                            if tc >= 1:
                                wait(t, ("xnTevac", tc - 1, half))
                            for j in range(8):
                                dc = half * 8 + j
                                i = t.transpose(pt[:, j * 128:j * 128 + 128],
                                                xnb[:, dc * 128:(dc + 1) * 128], ident[:])
                            PE.inc(i, 1)
                            mark(("xpose", tc, half), PE)

            with ExitStack() as P:
                EP = P.enter_context
                kvab = EP(nc.sbuf_tensor("kvab", [128, ND * 576], BF16))
                kpe_bf = EP(nc.sbuf_tensor("kpe_bf", [64, 2048], BF16))
                kpeO = EP(nc.sbuf_tensor("kpeO", [32, 2048], BF16))
                krO32 = EP(nc.sbuf_tensor("krO32", [32, 2048], BF16))
                r1 = EP(nc.sbuf_tensor("r1", [32, 2048], BF16))
                r2 = EP(nc.sbuf_tensor("r2", [32, 2048], BF16))
                rwb = EP(nc.sbuf_tensor("rwb", [128, ND * NEXP], BF16))
                eq1 = EP(nc.sbuf_tensor("eq1", [128, NEXP], I32))
                eq2 = EP(nc.sbuf_tensor("eq2", [128, NEXP], I32))
                rprob = EP(nc.sbuf_tensor("rprob", [128, NEXP], F32))
                rp2 = EP(nc.sbuf_tensor("rp2", [128, NEXP], F32))
                rtmp = EP(nc.sbuf_tensor("rtmp", [128, NEXP], F32))
                rneg = EP(nc.sbuf_tensor("rneg", [128, NEXP], F32))
                psDummy = EP(nc.psum_tensor("psDummy", [128, 2048], F32))
                psP = [EP(nc.psum_tensor(f"psP{i}", [128, 512], F32)) for i in range(4)]

                @block.gpsimd
                def _(g):
                    wait(g, ("xnTevac", NT - 1, 1))
                    wait(g, ("cossin",))
                    i = g.dma_start(kvab[:].rearrange("p (dc m) -> p dc m", m=576),
                                    wkva.rearrange("(dc p) m -> p dc m", p=128))
                    sems["dKVA"].inc(i, 16)
                    mark(("kva",), sems["dKVA"])
                    i = g.dma_start(rwb[:].rearrange("p (dc e) -> p dc e", e=NEXP),
                                    rw.rearrange("(dc p) e -> p dc e", p=128))
                    sems["dRW"].inc(i, 16)
                    mark(("rwl",), sems["dRW"])
                    i = g.dma_start(wqnb[:].rearrange("p (dc c) -> p dc c", c=NOPE),
                                    wqn.rearrange("(dc p) c -> p dc c", p=128)[:, :, 0:NOPE])
                    sems["dWQN"].inc(i, 16)
                    mark(("wqnl", 0), sems["dWQN"])
                    i = g.dma_start(wqrb[:].rearrange("p (dc c) -> p dc c", c=128),
                                    wqr.rearrange("(dc p) c -> p dc c", p=128)[:, :, 0:128])
                    sems["dWQR"].inc(i, 16)
                    mark(("wqrl", 0), sems["dWQR"])
                    i = g.dma_start(knb[:].rearrange("p (lc c) -> p lc c", c=128),
                                    wkvb.rearrange("(lc p) c -> p lc c", p=128)[:, :, 0:128])
                    sems["dWKB"].inc(i, 16)
                    i = g.dma_start(vwb[:].rearrange("p (lc c) -> p lc c", c=128),
                                    wkvb.rearrange("(lc p) c -> p lc c", p=128)[:, :, 128:256])
                    sems["dWKB"].inc(i, 16)
                    mark(("wkbl", 0), sems["dWKB"])
                    wait(g, ("ckvevac", 19))
                    i = g.dma_start(kpeO[:], kpe_bf[32:64, :])
                    sems["dQRS"].inc(i, 16)
                    mark(("kpeO",), sems["dQRS"])
                    wait(g, ("kropePart",))
                    i = g.dma_start(krope[32:64, :], krO32[:])
                    sems["dQRS"].inc(i, 16)
                    mark(("krope",), sems["dQRS"])

                @block.tensor
                def _(t):
                    wait(t, ("kva",))
                    wait(t, ("rwl",))
                    for nb in range(4):
                        for tc in range(nb * 4, nb * 4 + 4):
                            ps = psP[tc % 4]
                            wait(t, ("xnTevac", tc, 1))
                            if tc >= 4:
                                wait(t, ("rexp", tc - 4))
                            for dc in range(ND):
                                i = t.matmul(ps[:, 0:NEXP],
                                             xnt(dc)[:, tc * 128:(tc + 1) * 128],
                                             rwb[:, dc * NEXP:(dc + 1) * NEXP],
                                             start=(dc == 0), stop=(dc == ND - 1))
                            PE.inc(i, 1)
                            mark(("rscore", tc), PE)
                        for mc in range(5):
                            k = nb * 5 + mc
                            ps = psDummy[:, (k % 4) * 512:(k % 4 + 1) * 512]
                            if k >= 4:
                                wait(t, ("ckvevac", k - 4))
                            Mrows = 128 if mc < 4 else 64
                            for dc in range(ND):
                                i = t.matmul(ps[0:Mrows],
                                             kvab[:, dc * 576 + mc * 128: dc * 576 + mc * 128 + Mrows],
                                             xnt(dc)[:, nb * 512:(nb + 1) * 512],
                                             start=(dc == 0), stop=(dc == ND - 1))
                            PE.inc(i, 1)
                            mark(("ckv", k), PE)

                @block.vector
                def _(v):
                    def emit_ckvevac(k):
                        nb2, mc2 = divmod(k, 5)
                        ps2 = psDummy[:, (k % 4) * 512:(k % 4 + 1) * 512]
                        wait(v, ("ckv", k))
                        if mc2 < 4:
                            i2 = v.tensor_copy(cT[:, mc2 * 2048 + nb2 * 512: mc2 * 2048 + (nb2 + 1) * 512], ps2)
                        else:
                            i2 = v.tensor_copy(kpe_bf[:, nb2 * 512:(nb2 + 1) * 512], ps2[0:64])
                        VE.inc(i2, 1)
                        mark(("ckvevac", k), VE)

                    wait(v, ("cossin",))
                    v.memset(rneg[:], NEG_BIG)
                    for tc in range(NT):
                        wait(v, ("rscore", tc))
                        ps = psP[tc % 4]
                        i = v.reduce_max(colA[:], ps[:, 0:NEXP], axis=AX.X, negate=True)
                        v.drain()
                        VE.inc(i, 1)
                        mark(("rmax", tc), VE)
                        wait(v, ("rexps", tc))
                        v.reduce_max(colA[:], rprob[:], axis=AX.X)
                        v.drain()
                        v.tensor_scalar(out=eq1[:], in0=rprob[:], scalar1=colA[:], scalar2=None, op0=ALU.is_equal)
                        v.tensor_copy(rp2[:], rprob[:])
                        v.drain()
                        v.copy_predicated(out=rp2[:], mask=eq1[:], data=rneg[:])
                        v.drain()
                        v.reduce_max(colB[:], rp2[:], axis=AX.X)
                        v.drain()
                        v.tensor_scalar(out=eq2[:], in0=rp2[:], scalar1=colB[:], scalar2=None, op0=ALU.is_equal)
                        v.drain()
                        v.tensor_tensor(out=colC[:], in0=colA[:], in1=colB[:], op=ALU.add)
                        v.drain()
                        v.reciprocal(colC[:], colC[:])
                        v.drain()
                        v.tensor_tensor(out=colD[:], in0=colA[:], in1=colC[:], op=ALU.mult)
                        v.tensor_tensor(out=colE[:], in0=colB[:], in1=colC[:], op=ALU.mult)
                        v.drain()
                        v.tensor_scalar(out=rtmp[:], in0=rprob[:], scalar1=0.0, scalar2=colD[:],
                                        op0=ALU.mult, op1=ALU.add)
                        v.tensor_scalar(out=rp2[:], in0=rprob[:], scalar1=0.0, scalar2=colE[:],
                                        op0=ALU.mult, op1=ALU.add)
                        cslice = comb[:, tc * NEXP:(tc + 1) * NEXP]
                        v.memset(cslice, 0.0)
                        v.drain()
                        v.copy_predicated(out=cslice, mask=eq1[:], data=rtmp[:])
                        v.drain()
                        i = v.copy_predicated(out=cslice, mask=eq2[:], data=rp2[:])
                        v.drain()
                        VE.inc(i, 1)
                        mark(("comb", tc), VE)
                        mark(("rexp", tc), VE)
                        done = (tc * 20) // NT
                        upto = ((tc + 1) * 20) // NT
                        for k2 in range(done, upto):
                            emit_ckvevac(k2)
                    wait(v, ("cossinrep",))
                    wait(v, ("kpeO",))
                    v.tensor_tensor(out=r1[:], in0=kpe_bf[0:32, :], in1=cosr[0:32, :], op=ALU.mult)
                    v.tensor_tensor(out=r2[:], in0=kpeO[:], in1=sinr[0:32, :], op=ALU.mult)
                    v.drain()
                    v.tensor_tensor(out=krope[0:32, :], in0=r1[:], in1=r2[:], op=ALU.subtract)
                    v.drain()
                    v.tensor_tensor(out=r1[:], in0=kpeO[:], in1=cosr[0:32, :], op=ALU.mult)
                    v.tensor_tensor(out=r2[:], in0=kpe_bf[0:32, :], in1=sinr[0:32, :], op=ALU.mult)
                    v.drain()
                    i = v.tensor_tensor(out=krO32[:], in0=r1[:], in1=r2[:], op=ALU.add)
                    v.drain()
                    VE.inc(i, 1)
                    mark(("kropePart",), VE)

                @block.scalar
                def _(s):
                    for tc in range(NT):
                        wait(s, ("rmax", tc))
                        ps = psP[tc % 4]
                        if tc >= 1:
                            wait(s, ("comb", tc - 1))
                        i = s.activation(rprob[:], ps[:, 0:NEXP], AF.Exp, bias=colA[:])
                        s.drain()
                        SE.inc(i, 1)
                        mark(("rexps", tc), SE)

            with ExitStack() as P:
                EP = P.enter_context
                oT = EP(nc.sbuf_tensor("oT", [128, HPQ * 2048], BF16))
                woq = EP(nc.sbuf_tensor("woq", [128, HPQ * 2048], BF16))
                kns = EP(nc.sbuf_tensor("kns", [128, 2048], BF16))
                vs = EP(nc.sbuf_tensor("vs", [128, 2048], BF16))
                qn = EP(nc.sbuf_tensor("qn", [128, 2048], BF16))
                qrA = EP(nc.sbuf_tensor("qrA", [64, 2048], BF16))
                qrB = EP(nc.sbuf_tensor("qrB", [64, 2048], BF16))
                probs2 = [EP(nc.sbuf_tensor(f"probs{i}", [128, 2048], BF16)) for i in range(2)]
                probs = probs2[0]
                probsT2 = [EP(nc.sbuf_tensor(f"probsT{i}", [128, 2048], BF16)) for i in range(4)]
                probsT = probsT2[0]
                rtmp = EP(nc.sbuf_tensor("rtmp64", [64, 2048], BF16))
                colM = EP(nc.sbuf_tensor("colM", [128, 4], F32))
                colB4 = EP(nc.sbuf_tensor("colB4", [128, 4], F32))
                stage = [EP(nc.sbuf_tensor(f"stage{i}", [128, 2048], F32)) for i in range(2)]
                xw = EP(nc.sbuf_tensor("xw", [128, 2048], F32))
                psA = EP(nc.psum_tensor("psA", [128, 2048], F32))
                psT0 = EP(nc.psum_tensor("psTa", [128, 1024], BF16))
                psT1 = EP(nc.psum_tensor("psTb", [128, 1024], BF16))
                psAV = [EP(nc.psum_tensor(f"psAV{i}", [128, 512], F32)) for i in range(2)]

                psT_last = {0: None, 1: None}
                psW_last = {0: None, 1: None, 2: None, 3: None}
                psAV_last = {0: None, 1: None}

                @block.gpsimd
                def _(g):
                    wait(g, ("krope",))
                    for h in range(H):
                        if h >= 1:
                            wait(g, ("qnMM", h - 1))
                            i = g.dma_start(wqnb[:].rearrange("p (dc c) -> p dc c", c=NOPE),
                                            wqn.rearrange("(dc p) c -> p dc c", p=128)[:, :, h * NOPE:(h + 1) * NOPE])
                            sems["dWQN"].inc(i, 16)
                            mark(("wqnl", h), sems["dWQN"])
                        if h % 2 == 0 and h >= 2:
                            hp = h // 2
                            wait(g, ("qrMM", hp - 1))
                            i = g.dma_start(wqrb[:].rearrange("p (dc c) -> p dc c", c=128),
                                            wqr.rearrange("(dc p) c -> p dc c", p=128)[:, :, hp * 128:(hp + 1) * 128])
                            sems["dWQR"].inc(i, 16)
                            mark(("wqrl", hp), sems["dWQR"])
                        if h >= 1:
                            wait(g, ("knvMM", h - 1))
                            i = g.dma_start(knb[:].rearrange("p (lc c) -> p lc c", c=128),
                                            wkvb.rearrange("(lc p) c -> p lc c", p=128)[:, :, h * 256:h * 256 + 128])
                            sems["dWKB"].inc(i, 16)
                            i = g.dma_start(vwb[:].rearrange("p (lc c) -> p lc c", c=128),
                                            wkvb.rearrange("(lc p) c -> p lc c", p=128)[:, :, h * 256 + 128:h * 256 + 256])
                            sems["dWKB"].inc(i, 16)
                            mark(("wkbl", h), sems["dWKB"])
                        if h % HPQ == 0:
                            q = h // HPQ
                            if q >= 1:
                                wait(g, ("woMM", q - 1))
                            for hh in range(HPQ):
                                i = g.dma_start(woq[:, hh * 2048:(hh + 1) * 2048],
                                                wo[(q * HPQ + hh) * 128:(q * HPQ + hh) * 128 + 128, :])
                                sems["dWO"].inc(i, 16)
                            mark(("wol", q), sems["dWO"])
                        if h % HPQ == HPQ - 1 and (h // HPQ) % 2 == 1:
                            q2 = h // HPQ
                            for tc in range(NT):
                                wait(g, ("stg", q2, tc))
                                wait(g, ("scrw", q2 - 1, tc))
                                i = g.dma_start(scr[q2 - 1][tc * 128:(tc + 1) * 128, :],
                                                stage[tc % 2][:], accum_op=ALU.add)
                                sems[f"dSCRA{tc % 2}"].inc(i, 16)
                                mark(("scrw", q2, tc), sems[f"dSCRA{tc % 2}"])
                        if h % 2 == 0:
                            hp = h // 2
                            wait(g, ("ropemath", hp))
                            g.dma_start(qrA[0:32, :], stage[0][0:32, :]).then_inc(sems["dQRS"].h, 16)
                            sems["dQRS"].n += 16
                            g.dma_start(qrA[32:64, :], stage[1][0:32, :]).then_inc(sems["dQRS"].h, 16)
                            sems["dQRS"].n += 16
                            g.dma_start(qrB[0:32, :], stage[0][32:64, :]).then_inc(sems["dQRS"].h, 16)
                            sems["dQRS"].n += 16
                            i = g.dma_start(qrB[32:64, :], stage[1][32:64, :])
                            sems["dQRS"].inc(i, 16)
                            mark(("qrready", hp), sems["dQRS"])

                @block.sync
                def _(sy):
                    for q in range(NQ):
                        for hp in (2 * q, 2 * q + 1):
                            wait(sy, ("qevac", hp))
                            i = sy.dma_start(probsT[0:64, :], probs[64:128, :])
                            sems["dQR"].inc(i, 16)
                            mark(("qodd", hp), sems["dQR"])
                        if q == 0:
                            i = sy.dma_start(xw[:], x[0:128, :])
                            sems["dX0"].inc(i, 16)
                            mark(("xw", 0), sems["dX0"])
                        for tc in range(NT):
                            if q % 2 == 0:
                                wait(sy, ("stg", q, tc))
                                i = sy.dma_start(scr[q][tc * 128:(tc + 1) * 128, :], stage[tc % 2][:])
                                sems[f"dSCR{tc % 2}"].inc(i, 16)
                                mark(("scrw", q, tc), sems[f"dSCR{tc % 2}"])
                            if q == 0 and tc + 1 < NT:
                                i = sy.dma_start(xw[:], x[(tc + 1) * 128:(tc + 2) * 128, :])
                                sems["dX0"].inc(i, 16)
                                mark(("xw", tc + 1), sems["dX0"])

                @block.tensor
                def _(t):
                    wait(t, ("krope",))
                    for h in range(H):
                        hp, q, hh = h // 2, h // HPQ, h % HPQ
                        wait(t, ("wqnl", h))
                        if h >= 1:
                            wait(t, ("vevac", h - 1, 3))
                            wait(t, ("exp", h - 1, NT - 1))
                        else:
                            wait(t, ("ckvevac", 19))
                        if h % 2 == 0:
                            wait(t, ("wqrl", hp))
                            for nb in range(4):
                                if psW_last[nb] is not None:
                                    wait(t, psW_last[nb])
                                    psW_last[nb] = None
                                for dc in range(ND):
                                    i = t.matmul(psA[:, nb * 512:(nb + 1) * 512],
                                                 wqrb[:, dc * 128:(dc + 1) * 128],
                                                 xnt(dc)[:, nb * 512:(nb + 1) * 512],
                                                 start=(dc == 0), stop=(dc == ND - 1))
                                PE.inc(i, 1)
                                mark(("qrpsb", hp, nb), PE)
                            mark(("qrps", hp), PE)
                            mark(("qrMM", hp), PE)
                        for nb in range(4):
                            if psW_last[nb] is not None:
                                wait(t, psW_last[nb])
                                psW_last[nb] = None
                            if h % 2 == 0:
                                wait(t, ("qevacb", hp, nb))
                            for dc in range(ND):
                                i = t.matmul(psA[:, nb * 512:(nb + 1) * 512],
                                             wqnb[:, dc * NOPE:(dc + 1) * NOPE],
                                             xnt(dc)[:, nb * 512:(nb + 1) * 512],
                                             start=(dc == 0), stop=(dc == ND - 1))
                            PE.inc(i, 1)
                            mark(("qnpsb", h, nb), PE)
                        mark(("qnps", h), PE)
                        mark(("qnMM", h), PE)
                        wait(t, ("wkbl", h))
                        if h == 0:
                            wait(t, ("ckvevac", 19))
                        for nb in range(4):
                            wait(t, ("qnevacb", h, nb))
                            for lc in range(4):
                                i = t.matmul(psA[:, nb * 512:(nb + 1) * 512],
                                             knb[:, lc * 128:(lc + 1) * 128],
                                             cT[:, lc * 2048 + nb * 512: lc * 2048 + (nb + 1) * 512],
                                             start=(lc == 0), stop=(lc == 3))
                            PE.inc(i, 1)
                            mark(("knpsb", h, nb), PE)
                        mark(("knps", h), PE)
                        for tc in range(NT):
                            if tc % 4 == 0:
                                wait(t, ("knevacb", h, tc // 4))
                            for lc in range(4):
                                i = t.matmul(psA[:, tc * 128:(tc + 1) * 128],
                                             cT[:, lc * 2048 + tc * 128: lc * 2048 + (tc + 1) * 128],
                                             vwb[:, lc * 128:(lc + 1) * 128],
                                             start=(lc == 0), stop=(lc == 3))
                            if tc % 4 == 3:
                                PE.inc(i, 1)
                                mark(("vps", h, tc // 4), PE)
                        mark(("knvMM", h), PE)
                        qr = qrA if h % 2 == 0 else qrB
                        wait(t, ("qrready", hp))
                        def emit_trp(j_qt):
                            wait(t, ("norm", h, j_qt))
                            pj = probs2[j_qt % 2]
                            nbat = j_qt // 8 + 1
                            for b in range(nbat):
                                pt = psT0 if b % 2 == 0 else psT1
                                ptidx = b % 2
                                if psT_last[ptidx] is not None:
                                    wait(t, psT_last[ptidx])
                                n_in_b = min(8, j_qt + 1 - b * 8)
                                for j in range(n_in_b):
                                    jj = b * 8 + j
                                    i2 = t.transpose(pt[:, j * 128:(j + 1) * 128],
                                                     pj[:, jj * 128:(jj + 1) * 128], ident[:])
                                PE.inc(i2, 1)
                                mark(("trp", h, j_qt, b), PE)
                                psT_last[ptidx] = ("trpevac", h, j_qt, b)

                        def emit_av(j_qt):
                            nbat = j_qt // 8 + 1
                            wait(t, ("trpevac", h, j_qt, nbat - 1))
                            if psAV_last[j_qt % 2] is not None:
                                wait(t, psAV_last[j_qt % 2])
                            pT = probsT2[j_qt % 4]
                            for j in range(j_qt + 1):
                                i2 = t.matmul(psAV[j_qt % 2][:, 0:128],
                                              vs[:, j * 128:(j + 1) * 128],
                                              pT[:, j * 128:(j + 1) * 128],
                                              start=(j == 0), stop=(j == j_qt))
                            PE.inc(i2, 1)
                            mark(("av", h, j_qt), PE)
                            psAV_last[j_qt % 2] = ("avevac", h, j_qt)

                        for qt in range(NT):
                            kv = (qt + 1) * 128
                            base = (qt % 2) * 1024 if qt <= 7 else 0
                            if qt == 0:
                                wait(t, ("vevac", h, 1))
                            elif qt == 1:
                                wait(t, ("vevac", h, 3))
                            nkb = (kv + 511) // 512
                            nkb_prev = (qt * 128 + 511) // 512
                            if qt == 8:
                                wait(t, ("exp", h, 6))
                                wait(t, ("exp", h, 7))
                            for kb in range(nkb):
                                wdt = min(512, kv - kb * 512)
                                if 2 <= qt <= 7:
                                    pk = (qt - 1) * 128
                                    wait(t, ("expb", h, qt - 2, min(kb, (pk + 511) // 512 - 1)))
                                elif qt >= 9:
                                    wait(t, ("expb", h, qt - 1, min(kb, nkb_prev - 1)))
                                t.matmul(psA[:, base + kb * 512: base + kb * 512 + wdt],
                                         qn[:, qt * 128:(qt + 1) * 128],
                                         kns[:, kb * 512: kb * 512 + wdt],
                                         start=True, stop=False)
                            for kb in range(nkb):
                                wdt = min(512, kv - kb * 512)
                                diag_here = (kb == nkb - 1)
                                i = t.matmul(psA[:, base + kb * 512: base + kb * 512 + wdt],
                                             qr[0:64, qt * 128:(qt + 1) * 128],
                                             krope[0:64, kb * 512: kb * 512 + wdt],
                                             start=False, stop=not diag_here)
                                if not diag_here:
                                    PE.inc(i, 1)
                                    mark(("sb", h, qt, kb), PE)
                            i = t.matmul(psA[:, base + qt * 128: base + qt * 128 + 128],
                                         identNeg[:], ident[:],
                                         start=False, stop=True)
                            PE.inc(i, 1)
                            mark(("sb", h, qt, nkb - 1), PE)
                            mark(("scores", h, qt), PE)
                            if qt >= 1:
                                emit_trp(qt - 1)
                            if qt >= 3:
                                emit_av(qt - 3)
                        emit_trp(NT - 1)
                        emit_av(NT - 3)
                        emit_av(NT - 2)
                        emit_av(NT - 1)
                        if hh == HPQ - 1:
                            wait(t, ("wol", q))
                            wait(t, ("avevac", h, NT - 1))
                            for tc in range(NT):
                                for dcol in range(4):
                                    slot = (tc * 4 + dcol) % 6
                                    psW = psA[:, slot * 512:(slot + 1) * 512] if slot < 4 else psAV[slot - 4][:]
                                    if slot < 4:
                                        if psW_last[slot] is not None:
                                            wait(t, psW_last[slot])
                                    else:
                                        if psAV_last[slot - 4] is not None:
                                            wait(t, psAV_last[slot - 4])
                                    for h2 in range(HPQ):
                                        i = t.matmul(psW,
                                                     oT[:, h2 * 2048 + tc * 128: h2 * 2048 + (tc + 1) * 128],
                                                     woq[:, h2 * 2048 + dcol * 512: h2 * 2048 + (dcol + 1) * 512],
                                                     start=(h2 == 0), stop=(h2 == HPQ - 1))
                                    PE.inc(i, 1)
                                    mark(("wops", q, tc, dcol), PE)
                                    if slot < 4:
                                        psW_last[slot] = ("stgevac", q, tc, dcol)
                                    else:
                                        psAV_last[slot - 4] = ("stgevac", q, tc, dcol)
                            mark(("woMM", q), PE)

                def emit_rope_math(v, hp, h):
                    if h % HPQ == 0 and h // HPQ >= 1:
                        wait(v, ("scrw", h // HPQ - 1, NT - 1))
                        wait(v, ("scrw", h // HPQ - 1, NT - 2))
                    if hp >= 1:
                        wait(v, ("qrready", hp - 1))
                    wait(v, ("qodd", hp))
                    v.tensor_tensor(out=stage[0][0:64, :], in0=probs[0:64, :], in1=cosr[:], op=ALU.mult)
                    v.tensor_tensor(out=stage[1][0:64, :], in0=probsT[0:64, :], in1=sinr[:], op=ALU.mult)
                    v.drain()
                    v.tensor_tensor(out=stage[0][0:64, :], in0=stage[0][0:64, :], in1=stage[1][0:64, :], op=ALU.subtract)
                    v.drain()
                    v.tensor_tensor(out=stage[1][0:64, :], in0=probsT[0:64, :], in1=cosr[:], op=ALU.mult)
                    v.tensor_tensor(out=rtmp[:], in0=probs[0:64, :], in1=sinr[:], op=ALU.mult)
                    v.drain()
                    v.tensor_tensor(out=stage[1][0:64, :], in0=stage[1][0:64, :], in1=rtmp[:], op=ALU.add)
                    v.drain()
                    v.tensor_scalar(out=stage[0][0:64, :], in0=stage[0][0:64, :],
                                    scalar1=SCALE, scalar2=None, op0=ALU.mult)
                    i = v.tensor_scalar(out=stage[1][0:64, :], in0=stage[1][0:64, :],
                                        scalar1=SCALE, scalar2=None, op0=ALU.mult)
                    v.drain()
                    VE.inc(i, 1)
                    mark(("ropemath", hp), VE)

                @block.vector
                def _(v):
                    for h in range(H):
                        hp, q, hh = h // 2, h // HPQ, h % HPQ
                        if h % 2 == 0:
                            emit_rope_math(v, hp, h)
                        def emit_trpevac(j_qt):
                            nbat = j_qt // 8 + 1
                            for b in range(nbat):
                                pt = psT0 if b % 2 == 0 else psT1
                                wait(v, ("trp", h, j_qt, b))
                                wdt = min(8, j_qt + 1 - b * 8) * 128
                                i2 = v.tensor_copy(probsT2[j_qt % 4][:, b * 1024: b * 1024 + wdt],
                                                   pt[:, 0:wdt])
                                VE.inc(i2, 1)
                                mark(("trpevac", h, j_qt, b), VE)

                        def emit_avevac(j_qt):
                            wait(v, ("av", h, j_qt))
                            i2 = v.tensor_copy(oT[:, hh * 2048 + j_qt * 128: hh * 2048 + (j_qt + 1) * 128],
                                               psAV[j_qt % 2][:, 0:128])
                            VE.inc(i2, 1)
                            mark(("avevac", h, j_qt), VE)

                        for qt in range(NT):
                            kv = (qt + 1) * 128
                            base = (qt % 2) * 1024 if qt <= 7 else 0
                            nkb = (kv + 511) // 512
                            if nkb == 1:
                                wait(v, ("sb", h, qt, 0))
                                i = v.reduce_max(colD[:], psA[:, base:base + kv], axis=AX.X, negate=True)
                                v.drain()
                            else:
                                for kb in range(nkb):
                                    wdt = min(512, kv - kb * 512)
                                    wait(v, ("sb", h, qt, kb))
                                    v.reduce_max(colM[:, kb:kb + 1],
                                                 psA[:, base + kb * 512: base + kb * 512 + wdt], axis=AX.X)
                                v.drain()
                                i = v.reduce_max(colD[:], colM[:, 0:nkb], axis=AX.X, negate=True)
                                v.drain()
                            VE.inc(i, 1)
                            mark(("smax", h, qt), VE)
                            wait(v, ("exp", h, qt))
                            nkb2 = (kv + 511) // 512
                            v.reduce_sum(colB[:], colB4[:, 0:nkb2], axis=AX.X)
                            v.drain()
                            v.reciprocal(colC[:], colB[:])
                            v.drain()
                            i = v.tensor_scalar(out=probs2[qt % 2][:, 0:kv], in0=probs2[qt % 2][:, 0:kv],
                                                scalar1=colC[:], scalar2=None, op0=ALU.mult)
                            v.drain()
                            VE.inc(i, 1)
                            mark(("norm", h, qt), VE)
                            if qt >= 1:
                                emit_trpevac(qt - 1)
                            if qt >= 3:
                                emit_avevac(qt - 3)
                        emit_trpevac(NT - 1)
                        emit_avevac(NT - 3)
                        emit_avevac(NT - 2)
                        emit_avevac(NT - 1)
                        if hh == HPQ - 1:
                            for tc in range(NT):
                                for dcol in range(4):
                                    slot = (tc * 4 + dcol) % 6
                                    wait(v, ("wops", q, tc, dcol))
                                    if tc == 0 and dcol == 0:
                                        wait(v, ("qrready", 2 * q + 1))
                                    if dcol == 0:
                                        if tc >= 2:
                                            wait(v, ("scrw", q, tc - 2))
                                        elif q >= 1:
                                            wait(v, ("scrw", q - 1, NT - 2 + tc))
                                    srcW = psA[:, slot * 512:(slot + 1) * 512] if slot < 4 else psAV[slot - 4][:]
                                    if q == 0:
                                        if dcol == 0:
                                            wait(v, ("xw", tc))
                                        i = v.tensor_tensor(out=stage[tc % 2][:, dcol * 512:(dcol + 1) * 512],
                                                            in0=srcW,
                                                            in1=xw[:, dcol * 512:(dcol + 1) * 512],
                                                            op=ALU.add)
                                    else:
                                        i = v.tensor_copy(stage[tc % 2][:, dcol * 512:(dcol + 1) * 512],
                                                          srcW)
                                    VE.inc(i, 1)
                                    mark(("stgevac", q, tc, dcol), VE)
                                mark(("stg", q, tc), VE)

                @block.scalar
                def _(s):
                    for h in range(H):
                        hp = h // 2
                        if h % 2 == 0:
                            if h > 0:
                                wait(s, ("av", h - 1, NT - 1))
                            for nb in range(4):
                                wait(s, ("qrpsb", hp, nb))
                                s.activation(probs[:, nb * 512:(nb + 1) * 512],
                                             psA[:, nb * 512:(nb + 1) * 512], AF.Copy)
                                i = s.drain(fusable=False)
                                SE.inc(i, 1)
                                mark(("qevacb", hp, nb), SE)
                            mark(("qevac", hp), SE)
                        for nb in range(4):
                            wait(s, ("qnpsb", h, nb))
                            s.activation(qn[:, nb * 512:(nb + 1) * 512],
                                         psA[:, nb * 512:(nb + 1) * 512], AF.Copy, scale=SCALE)
                            i = s.drain(fusable=False)
                            SE.inc(i, 1)
                            mark(("qnevacb", h, nb), SE)
                        mark(("qnevac", h), SE)
                        for nb in range(4):
                            wait(s, ("knpsb", h, nb))
                            s.activation(kns[:, nb * 512:(nb + 1) * 512],
                                         psA[:, nb * 512:(nb + 1) * 512], AF.Copy)
                            i = s.drain(fusable=False)
                            SE.inc(i, 1)
                            mark(("knevacb", h, nb), SE)
                        mark(("knevac", h), SE)
                        if h == 0:
                            wait(s, ("comb", NT - 1))
                        for g4 in range(4):
                            wait(s, ("vps", h, g4))
                            s.activation(vs[:, g4 * 512:(g4 + 1) * 512],
                                         psA[:, g4 * 512:(g4 + 1) * 512], AF.Copy)
                            i = s.drain(fusable=False)
                            SE.inc(i, 1)
                            mark(("vevac", h, g4), SE)
                        for qt in range(NT):
                            kv = (qt + 1) * 128
                            base = (qt % 2) * 1024 if qt <= 7 else 0
                            nkb = (kv + 511) // 512
                            wait(s, ("smax", h, qt))
                            if qt >= 2:
                                wait(s, ("trp", h, qt - 2, (qt - 2) // 8))
                            elif h >= 1:
                                pq = NT - 2 + qt
                                wait(s, ("trp", h - 1, pq, pq // 8))
                            for kb in range(nkb):
                                wdt = min(512, kv - kb * 512)
                                i = s.activation(probs2[qt % 2][:, kb * 512: kb * 512 + wdt],
                                                 psA[:, base + kb * 512: base + kb * 512 + wdt], AF.Exp,
                                                 bias=colD[:], accum_out=colB4[:, kb:kb + 1])
                                SE.inc(i, 1)
                                mark(("expb", h, qt, kb), SE)
                            i = s.drain(fusable=False)
                            SE.inc(i, 1)
                            mark(("exp", h, qt), SE)


        with ExitStack() as P:
            EP = P.enter_context
            acc = EP(nc.sbuf_tensor("acc", [128, NT * 2048], BF16))

            with ExitStack() as P5:
                E5 = P5.enter_context
                hT = E5(nc.sbuf_tensor("hT", [128, NFC * 2048], BF16))
                gb = [E5(nc.sbuf_tensor(f"gb{i}", [128, 2048], BF16)) for i in range(2)]
                ub = [E5(nc.sbuf_tensor(f"ub{i}", [128, 2048], BF16)) for i in range(2)]
                db = E5(nc.sbuf_tensor("db", [128, NFC * 2048], BF16))
                sgt = [E5(nc.sbuf_tensor(f"sgt{i}", [128, 512], F32)) for i in range(2)]
                psG = [E5(nc.psum_tensor(f"psG{i}", [128, 512], F32)) for i in range(2)]
                psU = [E5(nc.psum_tensor(f"psU{i}", [128, 512], F32)) for i in range(2)]
                psD = [E5(nc.psum_tensor(f"psD{i}", [128, 512], F32)) for i in range(2)]

                psG_last = {0: None, 1: None}
                psU_last = {0: None, 1: None}
                psD_last = {0: None, 1: None}
                gslot_last = {0: None, 1: None}
                uslot_last = {0: None, 1: None}

                @block.gpsimd
                def _(g):
                    wait(g, ("woMM", NQ - 1))
                    wait(g, ("scrw", NQ - 1, NT - 1))
                    wait(g, ("scrw", NQ - 1, NT - 2))
                    for e in range(NEXP):
                        for fc in range(NFC):
                            if fc == 2:
                                if e >= 1:
                                    wait(g, ("dwMM", e - 1))
                                i = g.dma_start(db[:].rearrange("p (fc d) -> p fc d", d=2048),
                                                dw[e].rearrange("(fc p) d -> p fc d", p=128))
                                sems["dDW"].inc(i, 16)
                                mark(("dwl", e), sems["dDW"])
                            sl = fc % 2
                            if gslot_last[sl] is not None:
                                wait(g, gslot_last[sl])
                            i = g.dma_start(gb[sl][:].rearrange("p (dc f) -> p dc f", f=128),
                                            gw[e].rearrange("(dc p) f -> p dc f", p=128)[:, :, fc * 128:(fc + 1) * 128])
                            sems[f"dG{sl}"].inc(i, 16)
                            mark(("gl", e, fc), sems[f"dG{sl}"])
                            i = g.dma_start(ub[sl][:].rearrange("p (dc f) -> p dc f", f=128),
                                            uw[e].rearrange("(dc p) f -> p dc f", p=128)[:, :, fc * 128:(fc + 1) * 128])
                            sems[f"dU{sl}"].inc(i, 16)
                            mark(("ul", e, fc), sems[f"dU{sl}"])
                            gslot_last[sl] = ("gMM", e, fc)
                            uslot_last[sl] = ("gMM", e, fc)

                @block.tensor
                def _(t):
                    wait(t, ("stgevac", NQ - 1, NT - 1, 3))
                    wait(t, ("scrw", NQ - 1, NT - 1))
                    wait(t, ("scrw", NQ - 1, NT - 2))
                    for e in range(NEXP):
                        for fc in range(NFC):
                            sl = fc % 2
                            wait(t, ("gl", e, fc))
                            wait(t, ("ul", e, fc))
                            for tb in range(4):
                                k = fc * 4 + tb
                                ps_g, ps_u = psG[k % 2], psU[k % 2]
                                if psG_last[k % 2] is not None:
                                    wait(t, psG_last[k % 2])
                                for dc in range(ND):
                                    i = t.matmul(ps_g[:], gb[sl][:, dc * 128:(dc + 1) * 128],
                                                 xnt(dc)[:, tb * 512:(tb + 1) * 512],
                                                 start=(dc == 0), stop=(dc == ND - 1))
                                PE.inc(i, 1)
                                mark(("gps", e, fc, tb), PE)
                                for dc in range(ND):
                                    i = t.matmul(ps_u[:], ub[sl][:, dc * 128:(dc + 1) * 128],
                                                 xnt(dc)[:, tb * 512:(tb + 1) * 512],
                                                 start=(dc == 0), stop=(dc == ND - 1))
                                PE.inc(i, 1)
                                mark(("ups", e, fc, tb), PE)
                                psG_last[k % 2] = ("hT", e, fc, tb)
                                psU_last[k % 2] = ("hT", e, fc, tb)
                            mark(("gMM", e, fc), PE)
                        wait(t, ("dwl", e))
                        wait(t, ("hT", e, NFC - 1, 3))
                        for tc in range(NT):
                            for dcol in range(4):
                                k = tc * 4 + dcol
                                ps_d = psD[k % 2]
                                if psD_last[k % 2] is not None:
                                    wait(t, psD_last[k % 2])
                                for fc in range(NFC):
                                    i = t.matmul(ps_d[:],
                                                 hT[:, fc * 2048 + tc * 128: fc * 2048 + (tc + 1) * 128],
                                                 db[:, fc * 2048 + dcol * 512: fc * 2048 + (dcol + 1) * 512],
                                                 start=(fc == 0), stop=(fc == NFC - 1))
                                PE.inc(i, 1)
                                mark(("dps", e, tc, dcol), PE)
                                psD_last[k % 2] = ("accw", e, tc, dcol)
                        mark(("dwMM", e), PE)

                @block.scalar
                def _(s):
                    for e in range(NEXP):
                        for fc in range(NFC):
                            for tb in range(4):
                                k = fc * 4 + tb
                                wait(s, ("gps", e, fc, tb))
                                if (e, fc, tb) != (0, 0, 0):
                                    pass
                                prev = (e, fc, tb - 2) if tb >= 2 else None
                                if k >= 2 or e > 0:
                                    pk = k - 2
                                    pe_, pfc, ptb = e, pk // 4, pk % 4
                                    if pk < 0:
                                        pe_, pfc, ptb = e - 1, (pk + 24) // 4, (pk + 24) % 4
                                    wait(s, ("hT", pe_, pfc, ptb))
                                i = s.activation(sgt[k % 2][:], psG[k % 2][:], AF.Silu)
                                s.drain()
                                SE.inc(i, 1)
                                mark(("silu", e, fc, tb), SE)

                @block.vector
                def _(v):
                    for e in range(NEXP):
                        for fc in range(NFC):
                            for tb in range(4):
                                k = fc * 4 + tb
                                wait(v, ("silu", e, fc, tb))
                                wait(v, ("ups", e, fc, tb))
                                i = v.tensor_tensor(out=hT[:, fc * 2048 + tb * 512: fc * 2048 + (tb + 1) * 512],
                                                    in0=sgt[k % 2][:], in1=psU[k % 2][:], op=ALU.mult)
                                VE.inc(i, 1)
                                mark(("hT", e, fc, tb), VE)
                        for tc in range(NT):
                            for dcol in range(4):
                                k = tc * 4 + dcol
                                wait(v, ("dps", e, tc, dcol))
                                cslice = comb[:, tc * NEXP + e: tc * NEXP + e + 1]
                                dst = acc[:, tc * 2048 + dcol * 512: tc * 2048 + (dcol + 1) * 512]
                                if e == 0:
                                    i = v.tensor_scalar(out=dst, in0=psD[k % 2][:], scalar1=cslice,
                                                        scalar2=None, op0=ALU.mult)
                                else:
                                    i = v.scalar_tensor_tensor(out=dst, in0=psD[k % 2][:], scalar=cslice,
                                                               in1=dst, op0=ALU.mult, op1=ALU.add)
                                VE.inc(i, 1)
                                mark(("accw", e, tc, dcol), VE)

            with ExitStack() as P6:
                E6 = P6.enter_context
                rb = [E6(nc.sbuf_tensor(f"rb{i}", [128, 2048], F32)) for i in range(4)]
                ost = [E6(nc.sbuf_tensor(f"ost{i}", [128, 2048], F32)) for i in range(2)]

                @block.sync
                def _(sy):
                    wait(sy, ("scrw", NQ - 1, NT - 1))
                    wait(sy, ("scrw", NQ - 1, NT - 2))
                    wait(sy, ("dwMM", NEXP - 1))
                    wait(sy, ("accw", NEXP - 1, NT - 1, 3))
                    for tc in range(NT):
                        if tc >= 1:
                            wait(sy, ("madd", tc - 1, 1))
                            i = sy.dma_start(out[(tc - 1) * 128:tc * 128, :], ost[(tc - 1) % 2][:])
                            sems[f"dOUT{(tc - 1) % 2}"].inc(i, 16)
                            mark(("outw", tc - 1), sems[f"dOUT{(tc - 1) % 2}"])
                        if tc >= 2:
                            wait(sy, ("madd", tc - 2, 0))
                        i = sy.dma_start(rb[2 * (tc % 2)][:], scr[0][tc * 128:(tc + 1) * 128, :])
                        sems[f"dMRG{tc % 2}"].inc(i, 16)
                        mark(("mld", tc, 0), sems[f"dMRG{tc % 2}"])
                    wait(sy, ("madd", NT - 1, 1))
                    i = sy.dma_start(out[(NT - 1) * 128:NT * 128, :], ost[(NT - 1) % 2][:])
                    sems[f"dOUT{(NT - 1) % 2}"].inc(i, 16)
                    mark(("outw", NT - 1), sems[f"dOUT{(NT - 1) % 2}"])
                    sy.wait_ge(sems["dOUT0"].h, sems["dOUT0"].n)
                    sy.wait_ge(sems["dOUT1"].h, sems["dOUT1"].n)

                @block.gpsimd
                def _(g):
                    wait(g, ("scrw", NQ - 1, NT - 1))
                    wait(g, ("scrw", NQ - 1, NT - 2))
                    wait(g, ("dwMM", NEXP - 1))
                    wait(g, ("accw", NEXP - 1, NT - 1, 3))
                    for tc in range(NT):
                        if tc >= 2:
                            wait(g, ("madd", tc - 2, 0))
                        i = g.dma_start(rb[2 * (tc % 2) + 1][:], scr[2][tc * 128:(tc + 1) * 128, :])
                        sems[f"dMG{tc % 2}"].inc(i, 16)
                        mark(("gld", tc, 0), sems[f"dMG{tc % 2}"])

                @block.vector
                def _(v):
                    for tc in range(NT):
                        wait(v, ("mld", tc, 0))
                        wait(v, ("gld", tc, 0))
                        if tc >= 2:
                            wait(v, ("outw", tc - 2))
                        v.tensor_tensor(out=ost[tc % 2][:], in0=rb[2 * (tc % 2)][:],
                                        in1=rb[2 * (tc % 2) + 1][:], op=ALU.add)
                        i = v.drain(fusable=False)
                        VE.inc(i, 1)
                        mark(("madd", tc, 0), VE)
                        wait(v, ("accw", NEXP - 1, tc, 3))
                        v.tensor_tensor(out=ost[tc % 2][:], in0=ost[tc % 2][:],
                                        in1=acc[:, tc * 2048:(tc + 1) * 2048], op=ALU.add)
                        i = v.drain(fusable=False)
                        VE.inc(i, 1)
                        mark(("madd", tc, 1), VE)

    return nc, marks


_CACHED = {}


def _get_nc():
    if "nc" not in _CACHED:
        _, marks = build_nc(None)
        nc, _ = build_nc(marks)
        _CACHED["nc"] = nc
    return _CACHED["nc"]


def _host_prep(inputs):
    wq = np.asarray(inputs["L0_attn_wq"], dtype=np.float32)
    wkv_a = np.asarray(inputs["L0_attn_wkv_a"], dtype=np.float32)
    nope_cols = np.concatenate([np.arange(h * 192, h * 192 + 128) for h in range(H)])
    wqn_ = np.ascontiguousarray(wq[:, nope_cols])
    rope_cols = []
    for hp in range(H // 2):
        b0 = (2 * hp) * 192 + 128
        b1 = (2 * hp + 1) * 192 + 128
        rope_cols += [b0 + 2 * i for i in range(32)]
        rope_cols += [b1 + 2 * i for i in range(32)]
        rope_cols += [b0 + 2 * i + 1 for i in range(32)]
        rope_cols += [b1 + 2 * i + 1 for i in range(32)]
    wqr_ = np.ascontiguousarray(wq[:, rope_cols])
    kpe_cols = [LORA + 2 * i for i in range(32)] + [LORA + 2 * i + 1 for i in range(32)]
    wkva_ = np.ascontiguousarray(
        np.concatenate([wkv_a[:, :LORA], wkv_a[:, kpe_cols]], axis=1))
    e = np.arange(32, dtype=np.float32) * np.float32(2.0 / 64.0)
    invf_ = (np.float32(1.0) / np.power(np.float32(10000.0), e, dtype=np.float32)
             ).astype(np.float32).reshape(32, 1)
    return wqn_, wqr_, wkva_, invf_


def kernel(L0_attn_x, L0_attn_wq, L0_attn_wkv_a, L0_attn_wkv_b, L0_attn_wo,
           L0_ffn_router_w, L0_ffn_gate_w, L0_ffn_up_w, L0_ffn_down_w):
    x = np.ascontiguousarray(np.asarray(L0_attn_x, dtype=np.float32))
    wqn_, wqr_, wkva_, invf_ = _host_prep({
        "L0_attn_wq": L0_attn_wq, "L0_attn_wkv_a": L0_attn_wkv_a})
    wkvb_ = np.ascontiguousarray(np.asarray(L0_attn_wkv_b, dtype=np.float32))
    wo_ = np.ascontiguousarray(np.asarray(L0_attn_wo, dtype=np.float32))
    rw_ = np.ascontiguousarray(np.asarray(L0_ffn_router_w, dtype=np.float32))
    gw_ = np.ascontiguousarray(np.asarray(L0_ffn_gate_w, dtype=np.float32))
    uw_ = np.ascontiguousarray(np.asarray(L0_ffn_up_w, dtype=np.float32))
    dw_ = np.ascontiguousarray(np.asarray(L0_ffn_down_w, dtype=np.float32))

    nc = _get_nc()
    in_maps = []
    for core in range(8):
        in_maps.append({
            "x": np.ascontiguousarray(x[core]),
            "wqn": wqn_, "wqr": wqr_, "wkva": wkva_, "wkvb": wkvb_,
            "wo": wo_, "rw": rw_, "gw": gw_, "uw": uw_, "dw": dw_,
            "invf": invf_,
        })
    res = run_bass_kernel_spmd(nc, in_maps, core_ids=list(range(8)))
    return np.stack([np.asarray(r["out"]) for r in res.results], axis=0)


# version 24
# speedup vs baseline: 1.1406x; 1.1161x over previous
import numpy as np
from contextlib import ExitStack

import concourse.bass as bass
import concourse.mybir as mybir
from concourse.bass_utils import run_bass_kernel_spmd

F32 = mybir.dt.float32
BF16 = mybir.dt.bfloat16
I32 = mybir.dt.int32
AF = mybir.ActivationFunctionType
ALU = mybir.AluOpType
AX = mybir.AxisListType

S = 2048
D = 2048
H = 16
NOPE = 128
ROPE = 64
VD = 128
LORA = 512
NEXP = 16
FF = 768
NT = S // 128
ND = D // 128
NFC = FF // 128
NQ = 4
HPQ = H // NQ
SCALE = float(1.0 / np.sqrt(NOPE + ROPE))
MAGIC = 12582912.0
TWO_PI = 6.283185307179586
CW1 = 6.28125
CW2 = float(np.float32(TWO_PI - CW1))
CW3 = float(TWO_PI - CW1 - np.float64(np.float32(TWO_PI - CW1)))
NEG_BIG = -1e30
LN_EPS = 1e-5


class DSem:
    def __init__(self, nc, stack, name):
        self.h = stack.enter_context(nc.semaphore(name))
        self.n = 0
        self.name = name

    def inc(self, instr, k):
        instr.then_inc(self.h, k)
        self.n += k
        return self.n


def build_nc(prev_marks):
    pass2 = prev_marks is not None
    marks = {}
    nc = bass.Bass("TRN2", target_bir_lowering=False)

    x = nc.dram_tensor("x", [S, D], F32, kind="ExternalInput").ap()
    wqn = nc.dram_tensor("wqn", [D, H * NOPE], F32, kind="ExternalInput").ap()
    wqr = nc.dram_tensor("wqr", [D, (H // 2) * 128], F32, kind="ExternalInput").ap()
    wkva = nc.dram_tensor("wkva", [D, LORA + ROPE], F32, kind="ExternalInput").ap()
    wkvb = nc.dram_tensor("wkvb", [LORA, H * 256], F32, kind="ExternalInput").ap()
    wo = nc.dram_tensor("wo", [H * VD, D], F32, kind="ExternalInput").ap()
    rw = nc.dram_tensor("rw", [D, NEXP], F32, kind="ExternalInput").ap()
    gw = nc.dram_tensor("gw", [NEXP, D, FF], F32, kind="ExternalInput").ap()
    uw = nc.dram_tensor("uw", [NEXP, D, FF], F32, kind="ExternalInput").ap()
    dw = nc.dram_tensor("dw", [NEXP, FF, D], F32, kind="ExternalInput").ap()
    invf = nc.dram_tensor("invf", [32, 1], F32, kind="ExternalInput").ap()
    out = nc.dram_tensor("out", [S, D], F32, kind="ExternalOutput").ap()
    scr = [nc.dram_tensor(f"scr{q}", [S, D], F32).ap() for q in range(NQ)]

    sem_by_name = {}

    def mark(key, sem):
        assert key not in marks, key
        marks[key] = (sem.name, sem.n)

    def wait(eng, key):
        if pass2:
            name, v = prev_marks[key]
            eng.wait_ge(sem_by_name[name], v)
        else:
            assert key not in marks or True
            eng.wait_ge(sem_by_name["PE"], 0)

    with ExitStack() as G:
        EG = G.enter_context
        sems = {}
        for nm in ["PE", "VE", "SE", "PO",
                   "dX0", "dX1", "dWQN", "dWQR", "dWKB", "dWO", "dKVA", "dRW", "dIF",
                   "dG0", "dG1", "dU0", "dU1", "dDW", "dSCR0", "dSCR1", "dOUT0", "dOUT1",
                   "dMISC", "dMRG0", "dMRG1", "dMG0", "dMG1", "dSCRA0", "dSCRA1", "dQR", "dQRS"]:
            sems[nm] = DSem(nc, G, nm)
            sem_by_name[nm] = sems[nm].h
        PE, VE, SE, PO = sems["PE"], sems["VE"], sems["SE"], sems["PO"]

        xnT = EG(nc.sbuf_tensor("xnT", [128, ND * 2048], BF16))
        comb = EG(nc.sbuf_tensor("comb", [128, NT * NEXP], F32))
        ident = EG(nc.sbuf_tensor("ident", [128, 128], BF16))
        identNeg = EG(nc.sbuf_tensor("identNeg", [128, 128], BF16))
        colA = EG(nc.sbuf_tensor("colA", [128, 1], F32))
        colB = EG(nc.sbuf_tensor("colB", [128, 1], F32))
        colC = EG(nc.sbuf_tensor("colC", [128, 1], F32))
        colD = EG(nc.sbuf_tensor("colD", [128, 1], F32))
        colE = EG(nc.sbuf_tensor("colE", [128, 1], F32))
        block = EG(nc.Block())

        def xnt(dc):
            return xnT[:, dc * 2048:(dc + 1) * 2048]

        with ExitStack() as M:
            EM = M.enter_context
            cosr = EM(nc.sbuf_tensor("cosr", [64, 2048], BF16))
            sinr = EM(nc.sbuf_tensor("sinr", [64, 2048], BF16))
            cT = EM(nc.sbuf_tensor("cT", [128, 4 * 2048], BF16))
            krope = EM(nc.sbuf_tensor("krope", [64, 2048], BF16))
            wqnb = EM(nc.sbuf_tensor("wqnb", [128, 2048], BF16))
            wqrb = EM(nc.sbuf_tensor("wqrb", [128, 2048], BF16))
            knb = EM(nc.sbuf_tensor("knb", [128, 512], BF16))
            vwb = EM(nc.sbuf_tensor("vwb", [128, 512], BF16))

            with ExitStack() as P:
                EP = P.enter_context
                xbuf = [EP(nc.sbuf_tensor(f"xbuf{i}", [128, 2048], F32)) for i in range(2)]
                trash = EP(nc.sbuf_tensor("trash", [128, 2048], BF16))
                xnb = EP(nc.sbuf_tensor("xnb", [128, 2048], BF16))
                tidx = EP(nc.sbuf_tensor("tidx", [32, 2048], F32))
                ivf = EP(nc.sbuf_tensor("ivf", [32, 1], F32))
                frq = EP(nc.sbuf_tensor("frq", [32, 2048], F32))
                kfb = EP(nc.sbuf_tensor("kfb", [32, 2048], F32))
                lnc = EP(nc.sbuf_tensor("lnc", [128, 8], F32))
                psT0 = EP(nc.psum_tensor("psT0", [128, 1024], BF16))
                psT1 = EP(nc.psum_tensor("psT1", [128, 1024], BF16))

                @block.sync
                def _(sy):
                    i = sy.dma_start(ivf[:], invf)
                    sems["dIF"].inc(i, 16)
                    mark(("ivf",), sems["dIF"])
                    for tc in range(NT):
                        if tc >= 2:
                            wait(sy, ("lnused", tc - 2))
                        i = sy.dma_start(xbuf[tc % 2][:], x[tc * 128:(tc + 1) * 128, :])
                        sems[f"dX{tc % 2}"].inc(i, 16)
                        mark(("xload", tc), sems[f"dX{tc % 2}"])

                @block.gpsimd
                def _(g):
                    g.memset(ident[:], 0.0)
                    g.drain()
                    i = g.affine_select(out=ident[:], in_=ident[:], compare_op=ALU.not_equal,
                                        fill=1.0, base=0, pattern=[[-1, 128]], channel_multiplier=1)
                    PO.inc(i, 1)
                    mark(("ident",), PO)
                    g.memset(identNeg[:], 0.0)
                    g.drain()
                    i = g.affine_select(out=identNeg[:], in_=identNeg[:], compare_op=ALU.is_ge,
                                        fill=NEG_BIG, base=0, pattern=[[1, 128]], channel_multiplier=-1)
                    PO.inc(i, 1)
                    mark(("cmask",), PO)
                    i = g.iota(tidx[:], pattern=[[1, 2048]], base=0, channel_multiplier=0,
                               allow_small_or_imprecise_dtypes=True)
                    PO.inc(i, 1)
                    mark(("tidx",), PO)

                @block.vector
                def _(v):
                    wait(v, ("tidx",))
                    wait(v, ("ivf",))
                    v.tensor_scalar(out=frq[:], in0=tidx[:], scalar1=ivf[:], scalar2=None, op0=ALU.mult)
                    v.drain()
                    v.tensor_scalar(out=kfb[:], in0=frq[:], scalar1=float(1.0 / TWO_PI),
                                    scalar2=MAGIC, op0=ALU.mult, op1=ALU.add)
                    v.drain()
                    v.tensor_scalar(out=kfb[:], in0=kfb[:], scalar1=MAGIC, scalar2=None, op0=ALU.subtract)
                    v.drain()
                    v.scalar_tensor_tensor(out=frq[:], in0=kfb[:], scalar=-CW1, in1=frq[:],
                                           op0=ALU.mult, op1=ALU.add)
                    v.drain()
                    v.scalar_tensor_tensor(out=frq[:], in0=kfb[:], scalar=-CW2, in1=frq[:],
                                           op0=ALU.mult, op1=ALU.add)
                    v.drain()
                    v.tensor_scalar(out=frq[:], in0=frq[:], scalar1=float(np.pi), scalar2=None, op0=ALU.min)
                    v.drain()
                    i = v.tensor_scalar(out=frq[:], in0=frq[:], scalar1=float(-np.pi), scalar2=None, op0=ALU.max)
                    v.drain()
                    VE.inc(i, 1)
                    mark(("ropearg",), VE)
                    v.tensor_scalar(out=kfb[:], in0=frq[:], scalar1=-1.0, scalar2=None, op0=ALU.mult)
                    v.drain()
                    v.tensor_tensor(out=kfb[:], in0=kfb[:], in1=frq[:], op=ALU.max)
                    v.drain()
                    i = v.tensor_scalar(out=kfb[:], in0=kfb[:], scalar1=-1.0, scalar2=float(np.pi / 2),
                                        op0=ALU.mult, op1=ALU.add)
                    v.drain()
                    VE.inc(i, 1)
                    mark(("cosarg",), VE)
                    def emit_xnb(j):
                        wait(v, ("rstd", j))
                        v.tensor_scalar(out=lnc[:, j % 2:j % 2 + 1], in0=lnc[:, j % 2:j % 2 + 1],
                                        scalar1=-1.0, scalar2=None, op0=ALU.mult)
                        v.drain()
                        i = v.tensor_scalar(out=xnb[:], in0=xbuf[j % 2][:],
                                            scalar1=lnc[:, j % 2:j % 2 + 1],
                                            scalar2=lnc[:, 4 + j % 2:5 + j % 2],
                                            op0=ALU.add, op1=ALU.mult)
                        v.drain()
                        VE.inc(i, 1)
                        mark(("xn", j), VE)
                        mark(("lnused", j), VE)

                    def emit_evac(j):
                        for half in range(2):
                            pt = psT0 if half == 0 else psT1
                            wait(v, ("xpose", j, half))
                            dst = xnT[:, half * 8 * 2048: half * 8 * 2048 + 8 * 2048].rearrange(
                                "p (dc c) -> p dc c", c=2048)[:, :, j * 128:j * 128 + 128]
                            src = pt[:].rearrange("p (j c) -> p j c", c=128)
                            i = v.tensor_copy(dst, src)
                            VE.inc(i, 1)
                            mark(("xnTevac", j, half), VE)

                    for tc in range(NT):
                        wait(v, ("xload", tc))
                        xb = xbuf[tc % 2]
                        v.reduce_sum(colA[:], xb[:], axis=AX.X)
                        v.drain()
                        v.tensor_scalar(out=lnc[:, tc % 2:tc % 2 + 1], in0=colA[:],
                                        scalar1=float(1.0 / D), scalar2=None, op0=ALU.mult)
                        v.drain()
                        wait(v, ("sq", tc))
                        v.tensor_scalar(out=lnc[:, 2 + tc % 2:3 + tc % 2], in0=colB[:],
                                        scalar1=float(1.0 / D), scalar2=None, op0=ALU.mult)
                        v.tensor_tensor(out=colA[:], in0=lnc[:, tc % 2:tc % 2 + 1],
                                        in1=lnc[:, tc % 2:tc % 2 + 1], op=ALU.mult)
                        v.drain()
                        v.tensor_tensor(out=lnc[:, 2 + tc % 2:3 + tc % 2],
                                        in0=lnc[:, 2 + tc % 2:3 + tc % 2], in1=colA[:], op=ALU.subtract)
                        v.drain()
                        i = v.tensor_scalar(out=lnc[:, 2 + tc % 2:3 + tc % 2],
                                            in0=lnc[:, 2 + tc % 2:3 + tc % 2], scalar1=LN_EPS,
                                            scalar2=None, op0=ALU.add)
                        v.drain()
                        VE.inc(i, 1)
                        mark(("var", tc), VE)
                        if tc >= 1:
                            emit_xnb(tc - 1)
                            emit_evac(tc - 1)
                    emit_xnb(NT - 1)
                    emit_evac(NT - 1)

                @block.scalar
                def _(s):
                    wait(s, ("ropearg",))
                    wait(s, ("cosarg",))
                    s.activation(sinr[0:32, :], frq[:], AF.Sin)
                    i = s.activation(cosr[0:32, :], kfb[:], AF.Sin)
                    SE.inc(i, 1)
                    mark(("cossin",), SE)
                    for tc in range(NT):
                        wait(s, ("xload", tc))
                        xb = xbuf[tc % 2]
                        i = s.activation(trash[:], xb[:], AF.Square, accum_out=colB[:])
                        s.drain()
                        SE.inc(i, 1)
                        mark(("sq", tc), SE)
                        wait(s, ("var", tc))
                        s.activation(lnc[:, 4 + tc % 2:5 + tc % 2], lnc[:, 2 + tc % 2:3 + tc % 2], AF.Ln)
                        s.drain()
                        i = s.activation(lnc[:, 4 + tc % 2:5 + tc % 2],
                                         lnc[:, 4 + tc % 2:5 + tc % 2], AF.Exp, scale=-0.5)
                        s.drain()
                        SE.inc(i, 1)
                        mark(("rstd", tc), SE)


                @block.gpsimd
                def _(g):
                    wait(g, ("cossin",))
                    g.dma_start(cosr[32:64, :], cosr[0:32, :]).then_inc(sems["dMISC"].h, 16)
                    sems["dMISC"].n += 16
                    i = g.dma_start(sinr[32:64, :], sinr[0:32, :])
                    sems["dMISC"].inc(i, 16)
                    mark(("cossinrep",), sems["dMISC"])

                @block.tensor
                def _(t):
                    wait(t, ("ident",))
                    for tc in range(NT):
                        wait(t, ("xn", tc))
                        for half in range(2):
                            pt = psT0 if half == 0 else psT1
                            if tc >= 1:
                                wait(t, ("xnTevac", tc - 1, half))
                            for j in range(8):
                                dc = half * 8 + j
                                i = t.transpose(pt[:, j * 128:j * 128 + 128],
                                                xnb[:, dc * 128:(dc + 1) * 128], ident[:])
                            PE.inc(i, 1)
                            mark(("xpose", tc, half), PE)

            with ExitStack() as P:
                EP = P.enter_context
                kvab = EP(nc.sbuf_tensor("kvab", [128, ND * 576], BF16))
                kpe_bf = EP(nc.sbuf_tensor("kpe_bf", [64, 2048], BF16))
                kpeO = EP(nc.sbuf_tensor("kpeO", [32, 2048], BF16))
                krO32 = EP(nc.sbuf_tensor("krO32", [32, 2048], BF16))
                r1 = EP(nc.sbuf_tensor("r1", [32, 2048], BF16))
                r2 = EP(nc.sbuf_tensor("r2", [32, 2048], BF16))
                rwb = EP(nc.sbuf_tensor("rwb", [128, ND * NEXP], BF16))
                eq1 = EP(nc.sbuf_tensor("eq1", [128, NEXP], I32))
                eq2 = EP(nc.sbuf_tensor("eq2", [128, NEXP], I32))
                rprob = EP(nc.sbuf_tensor("rprob", [128, NEXP], F32))
                rp2 = EP(nc.sbuf_tensor("rp2", [128, NEXP], F32))
                rtmp = EP(nc.sbuf_tensor("rtmp", [128, NEXP], F32))
                rneg = EP(nc.sbuf_tensor("rneg", [128, NEXP], F32))
                psDummy = EP(nc.psum_tensor("psDummy", [128, 2048], F32))
                psP = [EP(nc.psum_tensor(f"psP{i}", [128, 512], F32)) for i in range(4)]

                @block.gpsimd
                def _(g):
                    wait(g, ("xnTevac", NT - 1, 1))
                    wait(g, ("cossin",))
                    i = g.dma_start(kvab[:].rearrange("p (dc m) -> p dc m", m=576),
                                    wkva.rearrange("(dc p) m -> p dc m", p=128))
                    sems["dKVA"].inc(i, 16)
                    mark(("kva",), sems["dKVA"])
                    i = g.dma_start(rwb[:].rearrange("p (dc e) -> p dc e", e=NEXP),
                                    rw.rearrange("(dc p) e -> p dc e", p=128))
                    sems["dRW"].inc(i, 16)
                    mark(("rwl",), sems["dRW"])
                    i = g.dma_start(wqnb[:].rearrange("p (dc c) -> p dc c", c=NOPE),
                                    wqn.rearrange("(dc p) c -> p dc c", p=128)[:, :, 0:NOPE])
                    sems["dWQN"].inc(i, 16)
                    mark(("wqnl", 0), sems["dWQN"])
                    i = g.dma_start(wqrb[:].rearrange("p (dc c) -> p dc c", c=128),
                                    wqr.rearrange("(dc p) c -> p dc c", p=128)[:, :, 0:128])
                    sems["dWQR"].inc(i, 16)
                    mark(("wqrl", 0), sems["dWQR"])
                    i = g.dma_start(knb[:].rearrange("p (lc c) -> p lc c", c=128),
                                    wkvb.rearrange("(lc p) c -> p lc c", p=128)[:, :, 0:128])
                    sems["dWKB"].inc(i, 16)
                    i = g.dma_start(vwb[:].rearrange("p (lc c) -> p lc c", c=128),
                                    wkvb.rearrange("(lc p) c -> p lc c", p=128)[:, :, 128:256])
                    sems["dWKB"].inc(i, 16)
                    mark(("wkbl", 0), sems["dWKB"])
                    wait(g, ("ckvevac", 19))
                    i = g.dma_start(kpeO[:], kpe_bf[32:64, :])
                    sems["dQRS"].inc(i, 16)
                    mark(("kpeO",), sems["dQRS"])
                    wait(g, ("kropePart",))
                    i = g.dma_start(krope[32:64, :], krO32[:])
                    sems["dQRS"].inc(i, 16)
                    mark(("krope",), sems["dQRS"])

                @block.tensor
                def _(t):
                    wait(t, ("kva",))
                    wait(t, ("rwl",))
                    for nb in range(4):
                        for tc in range(nb * 4, nb * 4 + 4):
                            ps = psP[tc % 4]
                            wait(t, ("xnTevac", tc, 1))
                            if tc >= 4:
                                wait(t, ("rexp", tc - 4))
                            for dc in range(ND):
                                i = t.matmul(ps[:, 0:NEXP],
                                             xnt(dc)[:, tc * 128:(tc + 1) * 128],
                                             rwb[:, dc * NEXP:(dc + 1) * NEXP],
                                             start=(dc == 0), stop=(dc == ND - 1))
                            PE.inc(i, 1)
                            mark(("rscore", tc), PE)
                        for mc in range(5):
                            k = nb * 5 + mc
                            ps = psDummy[:, (k % 4) * 512:(k % 4 + 1) * 512]
                            if k >= 4:
                                wait(t, ("ckvevac", k - 4))
                            Mrows = 128 if mc < 4 else 64
                            for dc in range(ND):
                                i = t.matmul(ps[0:Mrows],
                                             kvab[:, dc * 576 + mc * 128: dc * 576 + mc * 128 + Mrows],
                                             xnt(dc)[:, nb * 512:(nb + 1) * 512],
                                             start=(dc == 0), stop=(dc == ND - 1))
                            PE.inc(i, 1)
                            mark(("ckv", k), PE)

                @block.vector
                def _(v):
                    def emit_ckvevac(k):
                        nb2, mc2 = divmod(k, 5)
                        ps2 = psDummy[:, (k % 4) * 512:(k % 4 + 1) * 512]
                        wait(v, ("ckv", k))
                        if mc2 < 4:
                            i2 = v.tensor_copy(cT[:, mc2 * 2048 + nb2 * 512: mc2 * 2048 + (nb2 + 1) * 512], ps2)
                        else:
                            i2 = v.tensor_copy(kpe_bf[:, nb2 * 512:(nb2 + 1) * 512], ps2[0:64])
                        VE.inc(i2, 1)
                        mark(("ckvevac", k), VE)

                    wait(v, ("cossin",))
                    v.memset(rneg[:], NEG_BIG)
                    for tc in range(NT):
                        wait(v, ("rscore", tc))
                        ps = psP[tc % 4]
                        i = v.reduce_max(colA[:], ps[:, 0:NEXP], axis=AX.X, negate=True)
                        v.drain()
                        VE.inc(i, 1)
                        mark(("rmax", tc), VE)
                        wait(v, ("rexps", tc))
                        v.reduce_max(colA[:], rprob[:], axis=AX.X)
                        v.drain()
                        v.tensor_scalar(out=eq1[:], in0=rprob[:], scalar1=colA[:], scalar2=None, op0=ALU.is_equal)
                        v.tensor_copy(rp2[:], rprob[:])
                        v.drain()
                        v.copy_predicated(out=rp2[:], mask=eq1[:], data=rneg[:])
                        v.drain()
                        v.reduce_max(colB[:], rp2[:], axis=AX.X)
                        v.drain()
                        v.tensor_scalar(out=eq2[:], in0=rp2[:], scalar1=colB[:], scalar2=None, op0=ALU.is_equal)
                        v.drain()
                        v.tensor_tensor(out=colC[:], in0=colA[:], in1=colB[:], op=ALU.add)
                        v.drain()
                        v.reciprocal(colC[:], colC[:])
                        v.drain()
                        v.tensor_tensor(out=colD[:], in0=colA[:], in1=colC[:], op=ALU.mult)
                        v.tensor_tensor(out=colE[:], in0=colB[:], in1=colC[:], op=ALU.mult)
                        v.drain()
                        v.tensor_scalar(out=rtmp[:], in0=rprob[:], scalar1=0.0, scalar2=colD[:],
                                        op0=ALU.mult, op1=ALU.add)
                        v.tensor_scalar(out=rp2[:], in0=rprob[:], scalar1=0.0, scalar2=colE[:],
                                        op0=ALU.mult, op1=ALU.add)
                        cslice = comb[:, tc * NEXP:(tc + 1) * NEXP]
                        v.memset(cslice, 0.0)
                        v.drain()
                        v.copy_predicated(out=cslice, mask=eq1[:], data=rtmp[:])
                        v.drain()
                        i = v.copy_predicated(out=cslice, mask=eq2[:], data=rp2[:])
                        v.drain()
                        VE.inc(i, 1)
                        mark(("comb", tc), VE)
                        mark(("rexp", tc), VE)
                        done = (tc * 20) // NT
                        upto = ((tc + 1) * 20) // NT
                        for k2 in range(done, upto):
                            emit_ckvevac(k2)
                    wait(v, ("cossinrep",))
                    wait(v, ("kpeO",))
                    v.tensor_tensor(out=r1[:], in0=kpe_bf[0:32, :], in1=cosr[0:32, :], op=ALU.mult)
                    v.tensor_tensor(out=r2[:], in0=kpeO[:], in1=sinr[0:32, :], op=ALU.mult)
                    v.drain()
                    v.tensor_tensor(out=krope[0:32, :], in0=r1[:], in1=r2[:], op=ALU.subtract)
                    v.drain()
                    v.tensor_tensor(out=r1[:], in0=kpeO[:], in1=cosr[0:32, :], op=ALU.mult)
                    v.tensor_tensor(out=r2[:], in0=kpe_bf[0:32, :], in1=sinr[0:32, :], op=ALU.mult)
                    v.drain()
                    i = v.tensor_tensor(out=krO32[:], in0=r1[:], in1=r2[:], op=ALU.add)
                    v.drain()
                    VE.inc(i, 1)
                    mark(("kropePart",), VE)

                @block.scalar
                def _(s):
                    for tc in range(NT):
                        wait(s, ("rmax", tc))
                        ps = psP[tc % 4]
                        if tc >= 1:
                            wait(s, ("comb", tc - 1))
                        i = s.activation(rprob[:], ps[:, 0:NEXP], AF.Exp, bias=colA[:])
                        s.drain()
                        SE.inc(i, 1)
                        mark(("rexps", tc), SE)

            with ExitStack() as P:
                EP = P.enter_context
                oT = EP(nc.sbuf_tensor("oT", [128, HPQ * 2048], BF16))
                woq = EP(nc.sbuf_tensor("woq", [128, HPQ * 2048], BF16))
                kns = EP(nc.sbuf_tensor("kns", [128, 2048], BF16))
                vs = EP(nc.sbuf_tensor("vs", [128, 2048], BF16))
                qn = EP(nc.sbuf_tensor("qn", [128, 2048], BF16))
                qrA = EP(nc.sbuf_tensor("qrA", [64, 2048], BF16))
                qrB = EP(nc.sbuf_tensor("qrB", [64, 2048], BF16))
                probs2 = [EP(nc.sbuf_tensor(f"probs{i}", [128, 2048], BF16)) for i in range(2)]
                probs = probs2[0]
                probsT2 = [EP(nc.sbuf_tensor(f"probsT{i}", [128, 2048], BF16)) for i in range(4)]
                probsT = probsT2[0]
                rtmp = EP(nc.sbuf_tensor("rtmp64", [64, 2048], BF16))
                colM = EP(nc.sbuf_tensor("colM", [128, 4], F32))
                colB4 = EP(nc.sbuf_tensor("colB4", [128, 4], F32))
                colT = EP(nc.sbuf_tensor("colT", [128, 4], F32))
                stage = [EP(nc.sbuf_tensor(f"stage{i}", [128, 2048], F32)) for i in range(2)]
                xw = EP(nc.sbuf_tensor("xw", [128, 2048], F32))
                psA = EP(nc.psum_tensor("psA", [128, 2048], F32))
                psT0 = EP(nc.psum_tensor("psTa", [128, 1024], BF16))
                psT1 = EP(nc.psum_tensor("psTb", [128, 1024], BF16))
                psAV = [EP(nc.psum_tensor(f"psAV{i}", [128, 512], F32)) for i in range(2)]

                psT_last = {0: None, 1: None}
                psW_last = {0: None, 1: None, 2: None, 3: None}
                psAV_last = {0: None, 1: None}

                @block.gpsimd
                def _(g):
                    wait(g, ("krope",))
                    for h in range(H):
                        if h >= 1:
                            wait(g, ("qnMM", h - 1))
                            i = g.dma_start(wqnb[:].rearrange("p (dc c) -> p dc c", c=NOPE),
                                            wqn.rearrange("(dc p) c -> p dc c", p=128)[:, :, h * NOPE:(h + 1) * NOPE])
                            sems["dWQN"].inc(i, 16)
                            mark(("wqnl", h), sems["dWQN"])
                        if h % 2 == 0 and h >= 2:
                            hp = h // 2
                            wait(g, ("qrMM", hp - 1))
                            i = g.dma_start(wqrb[:].rearrange("p (dc c) -> p dc c", c=128),
                                            wqr.rearrange("(dc p) c -> p dc c", p=128)[:, :, hp * 128:(hp + 1) * 128])
                            sems["dWQR"].inc(i, 16)
                            mark(("wqrl", hp), sems["dWQR"])
                        if h >= 1:
                            wait(g, ("knvMM", h - 1))
                            i = g.dma_start(knb[:].rearrange("p (lc c) -> p lc c", c=128),
                                            wkvb.rearrange("(lc p) c -> p lc c", p=128)[:, :, h * 256:h * 256 + 128])
                            sems["dWKB"].inc(i, 16)
                            i = g.dma_start(vwb[:].rearrange("p (lc c) -> p lc c", c=128),
                                            wkvb.rearrange("(lc p) c -> p lc c", p=128)[:, :, h * 256 + 128:h * 256 + 256])
                            sems["dWKB"].inc(i, 16)
                            mark(("wkbl", h), sems["dWKB"])
                        if h % HPQ == 0:
                            q = h // HPQ
                            if q >= 1:
                                wait(g, ("woMM", q - 1))
                            for hh in range(HPQ):
                                i = g.dma_start(woq[:, hh * 2048:(hh + 1) * 2048],
                                                wo[(q * HPQ + hh) * 128:(q * HPQ + hh) * 128 + 128, :])
                                sems["dWO"].inc(i, 16)
                            mark(("wol", q), sems["dWO"])
                        if h % HPQ == HPQ - 1 and (h // HPQ) % 2 == 1:
                            q2 = h // HPQ
                            for tc in range(NT):
                                wait(g, ("stg", q2, tc))
                                wait(g, ("scrw", q2 - 1, tc))
                                i = g.dma_start(scr[q2 - 1][tc * 128:(tc + 1) * 128, :],
                                                stage[tc % 2][:], accum_op=ALU.add)
                                sems[f"dSCRA{tc % 2}"].inc(i, 16)
                                mark(("scrw", q2, tc), sems[f"dSCRA{tc % 2}"])
                        if h % 2 == 0:
                            hp = h // 2
                            wait(g, ("ropemath", hp))
                            g.dma_start(qrA[0:32, :], stage[0][0:32, :]).then_inc(sems["dQRS"].h, 16)
                            sems["dQRS"].n += 16
                            g.dma_start(qrA[32:64, :], stage[1][0:32, :]).then_inc(sems["dQRS"].h, 16)
                            sems["dQRS"].n += 16
                            g.dma_start(qrB[0:32, :], stage[0][32:64, :]).then_inc(sems["dQRS"].h, 16)
                            sems["dQRS"].n += 16
                            i = g.dma_start(qrB[32:64, :], stage[1][32:64, :])
                            sems["dQRS"].inc(i, 16)
                            mark(("qrready", hp), sems["dQRS"])

                @block.sync
                def _(sy):
                    for q in range(NQ):
                        for hp in (2 * q, 2 * q + 1):
                            wait(sy, ("qevac", hp))
                            i = sy.dma_start(probsT[0:64, :], probs[64:128, :])
                            sems["dQR"].inc(i, 16)
                            mark(("qodd", hp), sems["dQR"])
                        if q == 0:
                            i = sy.dma_start(xw[:], x[0:128, :])
                            sems["dX0"].inc(i, 16)
                            mark(("xw", 0), sems["dX0"])
                        for tc in range(NT):
                            if q % 2 == 0:
                                wait(sy, ("stg", q, tc))
                                i = sy.dma_start(scr[q][tc * 128:(tc + 1) * 128, :], stage[tc % 2][:])
                                sems[f"dSCR{tc % 2}"].inc(i, 16)
                                mark(("scrw", q, tc), sems[f"dSCR{tc % 2}"])
                            if q == 0 and tc + 1 < NT:
                                i = sy.dma_start(xw[:], x[(tc + 1) * 128:(tc + 2) * 128, :])
                                sems["dX0"].inc(i, 16)
                                mark(("xw", tc + 1), sems["dX0"])

                @block.tensor
                def _(t):
                    wait(t, ("krope",))
                    for h in range(H):
                        hp, q, hh = h // 2, h // HPQ, h % HPQ
                        wait(t, ("wqnl", h))
                        if h >= 1:
                            wait(t, ("vevac", h - 1, 3))
                            wait(t, ("exp", h - 1, NT - 1))
                        else:
                            wait(t, ("ckvevac", 19))
                        if h % 2 == 0:
                            wait(t, ("wqrl", hp))
                            for nb in range(4):
                                if psW_last[nb] is not None:
                                    wait(t, psW_last[nb])
                                    psW_last[nb] = None
                                for dc in range(ND):
                                    i = t.matmul(psA[:, nb * 512:(nb + 1) * 512],
                                                 wqrb[:, dc * 128:(dc + 1) * 128],
                                                 xnt(dc)[:, nb * 512:(nb + 1) * 512],
                                                 start=(dc == 0), stop=(dc == ND - 1))
                                PE.inc(i, 1)
                                mark(("qrpsb", hp, nb), PE)
                            mark(("qrps", hp), PE)
                            mark(("qrMM", hp), PE)
                        for nb in range(4):
                            if psW_last[nb] is not None:
                                wait(t, psW_last[nb])
                                psW_last[nb] = None
                            if h % 2 == 0:
                                wait(t, ("qevacb", hp, nb))
                            for dc in range(ND):
                                i = t.matmul(psA[:, nb * 512:(nb + 1) * 512],
                                             wqnb[:, dc * NOPE:(dc + 1) * NOPE],
                                             xnt(dc)[:, nb * 512:(nb + 1) * 512],
                                             start=(dc == 0), stop=(dc == ND - 1))
                            PE.inc(i, 1)
                            mark(("qnpsb", h, nb), PE)
                        mark(("qnps", h), PE)
                        mark(("qnMM", h), PE)
                        wait(t, ("wkbl", h))
                        if h == 0:
                            wait(t, ("ckvevac", 19))
                        for nb in range(4):
                            wait(t, ("qnevacb", h, nb))
                            for lc in range(4):
                                i = t.matmul(psA[:, nb * 512:(nb + 1) * 512],
                                             knb[:, lc * 128:(lc + 1) * 128],
                                             cT[:, lc * 2048 + nb * 512: lc * 2048 + (nb + 1) * 512],
                                             start=(lc == 0), stop=(lc == 3))
                            PE.inc(i, 1)
                            mark(("knpsb", h, nb), PE)
                        mark(("knps", h), PE)
                        for tc in range(NT):
                            if tc % 4 == 0:
                                wait(t, ("knevacb", h, tc // 4))
                            for lc in range(4):
                                i = t.matmul(psA[:, tc * 128:(tc + 1) * 128],
                                             cT[:, lc * 2048 + tc * 128: lc * 2048 + (tc + 1) * 128],
                                             vwb[:, lc * 128:(lc + 1) * 128],
                                             start=(lc == 0), stop=(lc == 3))
                            if tc % 4 == 3:
                                PE.inc(i, 1)
                                mark(("vps", h, tc // 4), PE)
                        mark(("knvMM", h), PE)
                        qr = qrA if h % 2 == 0 else qrB
                        wait(t, ("qrready", hp))
                        def emit_trp(j_qt):
                            wait(t, ("norm", h, j_qt))
                            pj = probs2[j_qt % 2]
                            nbat = j_qt // 8 + 1
                            for b in range(nbat):
                                pt = psT0 if b % 2 == 0 else psT1
                                ptidx = b % 2
                                if psT_last[ptidx] is not None:
                                    wait(t, psT_last[ptidx])
                                n_in_b = min(8, j_qt + 1 - b * 8)
                                for j in range(n_in_b):
                                    jj = b * 8 + j
                                    i2 = t.transpose(pt[:, j * 128:(j + 1) * 128],
                                                     pj[:, jj * 128:(jj + 1) * 128], ident[:])
                                PE.inc(i2, 1)
                                mark(("trp", h, j_qt, b), PE)
                                psT_last[ptidx] = ("trpevac", h, j_qt, b)

                        def emit_av(j_qt):
                            nbat = j_qt // 8 + 1
                            wait(t, ("trpevac", h, j_qt, nbat - 1))
                            if psAV_last[j_qt % 2] is not None:
                                wait(t, psAV_last[j_qt % 2])
                            pT = probsT2[j_qt % 4]
                            for j in range(j_qt + 1):
                                i2 = t.matmul(psAV[j_qt % 2][:, 0:128],
                                              vs[:, j * 128:(j + 1) * 128],
                                              pT[:, j * 128:(j + 1) * 128],
                                              start=(j == 0), stop=(j == j_qt))
                            PE.inc(i2, 1)
                            mark(("av", h, j_qt), PE)
                            psAV_last[j_qt % 2] = ("avevac", h, j_qt)

                        for qt in range(NT):
                            kv = (qt + 1) * 128
                            base = (qt % 2) * 1024 if qt <= 7 else 0
                            if qt == 0:
                                wait(t, ("vevac", h, 1))
                            elif qt == 1:
                                wait(t, ("vevac", h, 3))
                            nkb = (kv + 511) // 512
                            nkb_prev = (qt * 128 + 511) // 512
                            if qt == 8:
                                wait(t, ("exp", h, 6))
                                wait(t, ("exp", h, 7))
                            for kb in range(nkb):
                                wdt = min(512, kv - kb * 512)
                                if 2 <= qt <= 7:
                                    pk = (qt - 1) * 128
                                    wait(t, ("expb", h, qt - 2, min(kb, (pk + 511) // 512 - 1)))
                                elif qt >= 9:
                                    wait(t, ("expb", h, qt - 1, min(kb, nkb_prev - 1)))
                                t.matmul(psA[:, base + kb * 512: base + kb * 512 + wdt],
                                         qn[:, qt * 128:(qt + 1) * 128],
                                         kns[:, kb * 512: kb * 512 + wdt],
                                         start=True, stop=False)
                            for kb in range(nkb):
                                wdt = min(512, kv - kb * 512)
                                diag_here = (kb == nkb - 1)
                                i = t.matmul(psA[:, base + kb * 512: base + kb * 512 + wdt],
                                             qr[0:64, qt * 128:(qt + 1) * 128],
                                             krope[0:64, kb * 512: kb * 512 + wdt],
                                             start=False, stop=not diag_here)
                                if not diag_here:
                                    PE.inc(i, 1)
                                    mark(("sb", h, qt, kb), PE)
                            i = t.matmul(psA[:, base + qt * 128: base + qt * 128 + 128],
                                         identNeg[:], ident[:],
                                         start=False, stop=True)
                            PE.inc(i, 1)
                            mark(("sb", h, qt, nkb - 1), PE)
                            mark(("scores", h, qt), PE)
                            if qt >= 1:
                                emit_trp(qt - 1)
                            if qt >= 3:
                                emit_av(qt - 3)
                        emit_trp(NT - 1)
                        emit_av(NT - 3)
                        emit_av(NT - 2)
                        emit_av(NT - 1)
                        if hh == HPQ - 1:
                            wait(t, ("wol", q))
                            wait(t, ("avevac", h, NT - 1))
                            for tc in range(NT):
                                for dcol in range(4):
                                    slot = (tc * 4 + dcol) % 6
                                    psW = psA[:, slot * 512:(slot + 1) * 512] if slot < 4 else psAV[slot - 4][:]
                                    if slot < 4:
                                        if psW_last[slot] is not None:
                                            wait(t, psW_last[slot])
                                    else:
                                        if psAV_last[slot - 4] is not None:
                                            wait(t, psAV_last[slot - 4])
                                    for h2 in range(HPQ):
                                        i = t.matmul(psW,
                                                     oT[:, h2 * 2048 + tc * 128: h2 * 2048 + (tc + 1) * 128],
                                                     woq[:, h2 * 2048 + dcol * 512: h2 * 2048 + (dcol + 1) * 512],
                                                     start=(h2 == 0), stop=(h2 == HPQ - 1))
                                    PE.inc(i, 1)
                                    mark(("wops", q, tc, dcol), PE)
                                    if slot < 4:
                                        psW_last[slot] = ("stgevac", q, tc, dcol)
                                    else:
                                        psAV_last[slot - 4] = ("stgevac", q, tc, dcol)
                            mark(("woMM", q), PE)

                def emit_rope_math(v, hp, h):
                    if h % HPQ == 0 and h // HPQ >= 1:
                        wait(v, ("scrw", h // HPQ - 1, NT - 1))
                        wait(v, ("scrw", h // HPQ - 1, NT - 2))
                    if hp >= 1:
                        wait(v, ("qrready", hp - 1))
                    wait(v, ("qodd", hp))
                    v.tensor_tensor(out=stage[0][0:64, :], in0=probs[0:64, :], in1=cosr[:], op=ALU.mult)
                    v.tensor_tensor(out=stage[1][0:64, :], in0=probsT[0:64, :], in1=sinr[:], op=ALU.mult)
                    v.drain()
                    v.tensor_tensor(out=stage[0][0:64, :], in0=stage[0][0:64, :], in1=stage[1][0:64, :], op=ALU.subtract)
                    v.drain()
                    v.tensor_tensor(out=stage[1][0:64, :], in0=probsT[0:64, :], in1=cosr[:], op=ALU.mult)
                    v.tensor_tensor(out=rtmp[:], in0=probs[0:64, :], in1=sinr[:], op=ALU.mult)
                    v.drain()
                    v.tensor_tensor(out=stage[1][0:64, :], in0=stage[1][0:64, :], in1=rtmp[:], op=ALU.add)
                    v.drain()
                    v.tensor_scalar(out=stage[0][0:64, :], in0=stage[0][0:64, :],
                                    scalar1=SCALE, scalar2=None, op0=ALU.mult)
                    i = v.tensor_scalar(out=stage[1][0:64, :], in0=stage[1][0:64, :],
                                        scalar1=SCALE, scalar2=None, op0=ALU.mult)
                    v.drain()
                    VE.inc(i, 1)
                    mark(("ropemath", hp), VE)

                @block.vector
                def _(v):
                    for h in range(H):
                        hp, q, hh = h // 2, h // HPQ, h % HPQ
                        if h % 2 == 0:
                            emit_rope_math(v, hp, h)
                        for qt in range(NT):
                            kv = (qt + 1) * 128
                            base = (qt % 2) * 1024 if qt <= 7 else 0
                            nkb = (kv + 511) // 512
                            if nkb == 1:
                                wait(v, ("sb", h, qt, 0))
                                i = v.reduce_max(colD[:], psA[:, base:base + kv], axis=AX.X, negate=True)
                                v.drain()
                            else:
                                for kb in range(nkb):
                                    wdt = min(512, kv - kb * 512)
                                    wait(v, ("sb", h, qt, kb))
                                    v.reduce_max(colM[:, kb:kb + 1],
                                                 psA[:, base + kb * 512: base + kb * 512 + wdt], axis=AX.X)
                                v.drain()
                                i = v.reduce_max(colD[:], colM[:, 0:nkb], axis=AX.X, negate=True)
                                v.drain()
                            VE.inc(i, 1)
                            mark(("smax", h, qt), VE)
                            wait(v, ("exp", h, qt))
                            v.reciprocal(colC[:], colB[:])
                            v.drain()
                            i = v.tensor_scalar(out=probs2[qt % 2][:, 0:kv], in0=probs2[qt % 2][:, 0:kv],
                                                scalar1=colC[:], scalar2=None, op0=ALU.mult)
                            v.drain()
                            VE.inc(i, 1)
                            mark(("norm", h, qt), VE)
                        if hh == HPQ - 1:
                            for tc in range(NT):
                                for dcol in range(4):
                                    slot = (tc * 4 + dcol) % 6
                                    wait(v, ("wops", q, tc, dcol))
                                    if tc == 0 and dcol == 0:
                                        wait(v, ("qrready", 2 * q + 1))
                                    if dcol == 0:
                                        if tc >= 2:
                                            wait(v, ("scrw", q, tc - 2))
                                        elif q >= 1:
                                            wait(v, ("scrw", q - 1, NT - 2 + tc))
                                    srcW = psA[:, slot * 512:(slot + 1) * 512] if slot < 4 else psAV[slot - 4][:]
                                    if q == 0:
                                        if dcol == 0:
                                            wait(v, ("xw", tc))
                                        i = v.tensor_tensor(out=stage[tc % 2][:, dcol * 512:(dcol + 1) * 512],
                                                            in0=srcW,
                                                            in1=xw[:, dcol * 512:(dcol + 1) * 512],
                                                            op=ALU.add)
                                    else:
                                        i = v.tensor_copy(stage[tc % 2][:, dcol * 512:(dcol + 1) * 512],
                                                          srcW)
                                    VE.inc(i, 1)
                                    mark(("stgevac", q, tc, dcol), VE)
                                mark(("stg", q, tc), VE)

                @block.scalar
                def _(s):
                    for h in range(H):
                        hp, hh = h // 2, h % HPQ
                        if h % 2 == 0:
                            if h > 0:
                                wait(s, ("av", h - 1, NT - 1))
                            for nb in range(4):
                                wait(s, ("qrpsb", hp, nb))
                                s.activation(probs[:, nb * 512:(nb + 1) * 512],
                                             psA[:, nb * 512:(nb + 1) * 512], AF.Copy)
                                i = s.drain(fusable=False)
                                SE.inc(i, 1)
                                mark(("qevacb", hp, nb), SE)
                            mark(("qevac", hp), SE)
                        for nb in range(4):
                            wait(s, ("qnpsb", h, nb))
                            s.activation(qn[:, nb * 512:(nb + 1) * 512],
                                         psA[:, nb * 512:(nb + 1) * 512], AF.Copy, scale=SCALE)
                            i = s.drain(fusable=False)
                            SE.inc(i, 1)
                            mark(("qnevacb", h, nb), SE)
                        mark(("qnevac", h), SE)
                        for nb in range(4):
                            wait(s, ("knpsb", h, nb))
                            s.activation(kns[:, nb * 512:(nb + 1) * 512],
                                         psA[:, nb * 512:(nb + 1) * 512], AF.Copy)
                            i = s.drain(fusable=False)
                            SE.inc(i, 1)
                            mark(("knevacb", h, nb), SE)
                        mark(("knevac", h), SE)
                        if h == 0:
                            wait(s, ("comb", NT - 1))
                        for g4 in range(4):
                            wait(s, ("vps", h, g4))
                            s.activation(vs[:, g4 * 512:(g4 + 1) * 512],
                                         psA[:, g4 * 512:(g4 + 1) * 512], AF.Copy)
                            i = s.drain(fusable=False)
                            SE.inc(i, 1)
                            mark(("vevac", h, g4), SE)
                        for qt in range(NT):
                            kv = (qt + 1) * 128
                            base = (qt % 2) * 1024 if qt <= 7 else 0
                            nkb = (kv + 511) // 512
                            wait(s, ("smax", h, qt))
                            if qt >= 2:
                                wait(s, ("trp", h, qt - 2, (qt - 2) // 8))
                            elif h >= 1:
                                pq = NT - 2 + qt
                                wait(s, ("trp", h - 1, pq, pq // 8))
                            for kb in range(nkb):
                                wdt = min(512, kv - kb * 512)
                                i = s.activation(probs2[qt % 2][:, kb * 512: kb * 512 + wdt],
                                                 psA[:, base + kb * 512: base + kb * 512 + wdt], AF.Exp,
                                                 bias=colD[:], accum_out=colB4[:, kb:kb + 1])
                                SE.inc(i, 1)
                                mark(("expb", h, qt, kb), SE)
                            s.drain()
                            s.activation(colT[:, 0:nkb], colB4[:, 0:nkb], AF.Copy, accum_out=colB[:])
                            i = s.drain(fusable=False)
                            SE.inc(i, 1)
                            mark(("exp", h, qt), SE)
                            def s_trpevac(j_qt):
                                nbat2 = j_qt // 8 + 1
                                for b in range(nbat2):
                                    pt = psT0 if b % 2 == 0 else psT1
                                    wait(s, ("trp", h, j_qt, b))
                                    wdt2 = min(8, j_qt + 1 - b * 8) * 128
                                    i2 = s.activation(probsT2[j_qt % 4][:, b * 1024: b * 1024 + wdt2],
                                                      pt[:, 0:wdt2], AF.Copy)
                                    SE.inc(i2, 1)
                                    mark(("trpevac", h, j_qt, b), SE)
                            def s_avevac(j_qt):
                                wait(s, ("av", h, j_qt))
                                i2 = s.activation(oT[:, hh * 2048 + j_qt * 128: hh * 2048 + (j_qt + 1) * 128],
                                                  psAV[j_qt % 2][:, 0:128], AF.Copy)
                                SE.inc(i2, 1)
                                mark(("avevac", h, j_qt), SE)
                            if qt >= 1:
                                s_trpevac(qt - 1)
                            if qt >= 3:
                                s_avevac(qt - 3)
                            if qt == NT - 1:
                                s_trpevac(NT - 1)
                                s_avevac(NT - 3)
                                s_avevac(NT - 2)
                                s_avevac(NT - 1)


        with ExitStack() as P:
            EP = P.enter_context
            acc = EP(nc.sbuf_tensor("acc", [128, NT * 2048], BF16))

            with ExitStack() as P5:
                E5 = P5.enter_context
                hT = E5(nc.sbuf_tensor("hT", [128, NFC * 2048], BF16))
                gb = [E5(nc.sbuf_tensor(f"gb{i}", [128, 2048], BF16)) for i in range(2)]
                ub = [E5(nc.sbuf_tensor(f"ub{i}", [128, 2048], BF16)) for i in range(2)]
                db = E5(nc.sbuf_tensor("db", [128, NFC * 2048], BF16))
                sgt = [E5(nc.sbuf_tensor(f"sgt{i}", [128, 512], F32)) for i in range(2)]
                psG = [E5(nc.psum_tensor(f"psG{i}", [128, 512], F32)) for i in range(2)]
                psU = [E5(nc.psum_tensor(f"psU{i}", [128, 512], F32)) for i in range(2)]
                psD = [E5(nc.psum_tensor(f"psD{i}", [128, 512], F32)) for i in range(2)]

                psG_last = {0: None, 1: None}
                psU_last = {0: None, 1: None}
                psD_last = {0: None, 1: None}
                gslot_last = {0: None, 1: None}
                uslot_last = {0: None, 1: None}

                @block.gpsimd
                def _(g):
                    wait(g, ("woMM", NQ - 1))
                    wait(g, ("scrw", NQ - 1, NT - 1))
                    wait(g, ("scrw", NQ - 1, NT - 2))
                    for e in range(NEXP):
                        for fc in range(NFC):
                            if fc == 2:
                                if e >= 1:
                                    wait(g, ("dwMM", e - 1))
                                i = g.dma_start(db[:].rearrange("p (fc d) -> p fc d", d=2048),
                                                dw[e].rearrange("(fc p) d -> p fc d", p=128))
                                sems["dDW"].inc(i, 16)
                                mark(("dwl", e), sems["dDW"])
                            sl = fc % 2
                            if gslot_last[sl] is not None:
                                wait(g, gslot_last[sl])
                            i = g.dma_start(gb[sl][:].rearrange("p (dc f) -> p dc f", f=128),
                                            gw[e].rearrange("(dc p) f -> p dc f", p=128)[:, :, fc * 128:(fc + 1) * 128])
                            sems[f"dG{sl}"].inc(i, 16)
                            mark(("gl", e, fc), sems[f"dG{sl}"])
                            i = g.dma_start(ub[sl][:].rearrange("p (dc f) -> p dc f", f=128),
                                            uw[e].rearrange("(dc p) f -> p dc f", p=128)[:, :, fc * 128:(fc + 1) * 128])
                            sems[f"dU{sl}"].inc(i, 16)
                            mark(("ul", e, fc), sems[f"dU{sl}"])
                            gslot_last[sl] = ("gMM", e, fc)
                            uslot_last[sl] = ("gMM", e, fc)

                @block.tensor
                def _(t):
                    wait(t, ("stgevac", NQ - 1, NT - 1, 3))
                    wait(t, ("scrw", NQ - 1, NT - 1))
                    wait(t, ("scrw", NQ - 1, NT - 2))
                    for e in range(NEXP):
                        for fc in range(NFC):
                            sl = fc % 2
                            wait(t, ("gl", e, fc))
                            wait(t, ("ul", e, fc))
                            for tb in range(4):
                                k = fc * 4 + tb
                                ps_g, ps_u = psG[k % 2], psU[k % 2]
                                if psG_last[k % 2] is not None:
                                    wait(t, psG_last[k % 2])
                                for dc in range(ND):
                                    i = t.matmul(ps_g[:], gb[sl][:, dc * 128:(dc + 1) * 128],
                                                 xnt(dc)[:, tb * 512:(tb + 1) * 512],
                                                 start=(dc == 0), stop=(dc == ND - 1))
                                PE.inc(i, 1)
                                mark(("gps", e, fc, tb), PE)
                                for dc in range(ND):
                                    i = t.matmul(ps_u[:], ub[sl][:, dc * 128:(dc + 1) * 128],
                                                 xnt(dc)[:, tb * 512:(tb + 1) * 512],
                                                 start=(dc == 0), stop=(dc == ND - 1))
                                PE.inc(i, 1)
                                mark(("ups", e, fc, tb), PE)
                                psG_last[k % 2] = ("hT", e, fc, tb)
                                psU_last[k % 2] = ("hT", e, fc, tb)
                            mark(("gMM", e, fc), PE)
                        wait(t, ("dwl", e))
                        wait(t, ("hT", e, NFC - 1, 3))
                        for tc in range(NT):
                            for dcol in range(4):
                                k = tc * 4 + dcol
                                ps_d = psD[k % 2]
                                if psD_last[k % 2] is not None:
                                    wait(t, psD_last[k % 2])
                                for fc in range(NFC):
                                    i = t.matmul(ps_d[:],
                                                 hT[:, fc * 2048 + tc * 128: fc * 2048 + (tc + 1) * 128],
                                                 db[:, fc * 2048 + dcol * 512: fc * 2048 + (dcol + 1) * 512],
                                                 start=(fc == 0), stop=(fc == NFC - 1))
                                PE.inc(i, 1)
                                mark(("dps", e, tc, dcol), PE)
                                psD_last[k % 2] = ("accw", e, tc, dcol)
                        mark(("dwMM", e), PE)

                @block.scalar
                def _(s):
                    for e in range(NEXP):
                        for fc in range(NFC):
                            for tb in range(4):
                                k = fc * 4 + tb
                                wait(s, ("gps", e, fc, tb))
                                if (e, fc, tb) != (0, 0, 0):
                                    pass
                                prev = (e, fc, tb - 2) if tb >= 2 else None
                                if k >= 2 or e > 0:
                                    pk = k - 2
                                    pe_, pfc, ptb = e, pk // 4, pk % 4
                                    if pk < 0:
                                        pe_, pfc, ptb = e - 1, (pk + 24) // 4, (pk + 24) % 4
                                    wait(s, ("hT", pe_, pfc, ptb))
                                i = s.activation(sgt[k % 2][:], psG[k % 2][:], AF.Silu)
                                s.drain()
                                SE.inc(i, 1)
                                mark(("silu", e, fc, tb), SE)

                @block.vector
                def _(v):
                    for e in range(NEXP):
                        for fc in range(NFC):
                            for tb in range(4):
                                k = fc * 4 + tb
                                wait(v, ("silu", e, fc, tb))
                                wait(v, ("ups", e, fc, tb))
                                i = v.tensor_tensor(out=hT[:, fc * 2048 + tb * 512: fc * 2048 + (tb + 1) * 512],
                                                    in0=sgt[k % 2][:], in1=psU[k % 2][:], op=ALU.mult)
                                VE.inc(i, 1)
                                mark(("hT", e, fc, tb), VE)
                        for tc in range(NT):
                            for dcol in range(4):
                                k = tc * 4 + dcol
                                wait(v, ("dps", e, tc, dcol))
                                cslice = comb[:, tc * NEXP + e: tc * NEXP + e + 1]
                                dst = acc[:, tc * 2048 + dcol * 512: tc * 2048 + (dcol + 1) * 512]
                                if e == 0:
                                    i = v.tensor_scalar(out=dst, in0=psD[k % 2][:], scalar1=cslice,
                                                        scalar2=None, op0=ALU.mult)
                                else:
                                    i = v.scalar_tensor_tensor(out=dst, in0=psD[k % 2][:], scalar=cslice,
                                                               in1=dst, op0=ALU.mult, op1=ALU.add)
                                VE.inc(i, 1)
                                mark(("accw", e, tc, dcol), VE)

            with ExitStack() as P6:
                E6 = P6.enter_context
                rb = [E6(nc.sbuf_tensor(f"rb{i}", [128, 2048], F32)) for i in range(4)]
                ost = [E6(nc.sbuf_tensor(f"ost{i}", [128, 2048], F32)) for i in range(2)]

                @block.sync
                def _(sy):
                    wait(sy, ("scrw", NQ - 1, NT - 1))
                    wait(sy, ("scrw", NQ - 1, NT - 2))
                    wait(sy, ("dwMM", NEXP - 1))
                    wait(sy, ("accw", NEXP - 1, NT - 1, 3))
                    for tc in range(NT):
                        if tc >= 1:
                            wait(sy, ("madd", tc - 1, 1))
                            i = sy.dma_start(out[(tc - 1) * 128:tc * 128, :], ost[(tc - 1) % 2][:])
                            sems[f"dOUT{(tc - 1) % 2}"].inc(i, 16)
                            mark(("outw", tc - 1), sems[f"dOUT{(tc - 1) % 2}"])
                        if tc >= 2:
                            wait(sy, ("madd", tc - 2, 0))
                        i = sy.dma_start(rb[2 * (tc % 2)][:], scr[0][tc * 128:(tc + 1) * 128, :])
                        sems[f"dMRG{tc % 2}"].inc(i, 16)
                        mark(("mld", tc, 0), sems[f"dMRG{tc % 2}"])
                    wait(sy, ("madd", NT - 1, 1))
                    i = sy.dma_start(out[(NT - 1) * 128:NT * 128, :], ost[(NT - 1) % 2][:])
                    sems[f"dOUT{(NT - 1) % 2}"].inc(i, 16)
                    mark(("outw", NT - 1), sems[f"dOUT{(NT - 1) % 2}"])
                    sy.wait_ge(sems["dOUT0"].h, sems["dOUT0"].n)
                    sy.wait_ge(sems["dOUT1"].h, sems["dOUT1"].n)

                @block.gpsimd
                def _(g):
                    wait(g, ("scrw", NQ - 1, NT - 1))
                    wait(g, ("scrw", NQ - 1, NT - 2))
                    wait(g, ("dwMM", NEXP - 1))
                    wait(g, ("accw", NEXP - 1, NT - 1, 3))
                    for tc in range(NT):
                        if tc >= 2:
                            wait(g, ("madd", tc - 2, 0))
                        i = g.dma_start(rb[2 * (tc % 2) + 1][:], scr[2][tc * 128:(tc + 1) * 128, :])
                        sems[f"dMG{tc % 2}"].inc(i, 16)
                        mark(("gld", tc, 0), sems[f"dMG{tc % 2}"])

                @block.vector
                def _(v):
                    for tc in range(NT):
                        wait(v, ("mld", tc, 0))
                        wait(v, ("gld", tc, 0))
                        if tc >= 2:
                            wait(v, ("outw", tc - 2))
                        v.tensor_tensor(out=ost[tc % 2][:], in0=rb[2 * (tc % 2)][:],
                                        in1=rb[2 * (tc % 2) + 1][:], op=ALU.add)
                        i = v.drain(fusable=False)
                        VE.inc(i, 1)
                        mark(("madd", tc, 0), VE)
                        wait(v, ("accw", NEXP - 1, tc, 3))
                        v.tensor_tensor(out=ost[tc % 2][:], in0=ost[tc % 2][:],
                                        in1=acc[:, tc * 2048:(tc + 1) * 2048], op=ALU.add)
                        i = v.drain(fusable=False)
                        VE.inc(i, 1)
                        mark(("madd", tc, 1), VE)

    return nc, marks


_CACHED = {}


def _get_nc():
    if "nc" not in _CACHED:
        _, marks = build_nc(None)
        nc, _ = build_nc(marks)
        _CACHED["nc"] = nc
    return _CACHED["nc"]


def _host_prep(inputs):
    wq = np.asarray(inputs["L0_attn_wq"], dtype=np.float32)
    wkv_a = np.asarray(inputs["L0_attn_wkv_a"], dtype=np.float32)
    nope_cols = np.concatenate([np.arange(h * 192, h * 192 + 128) for h in range(H)])
    wqn_ = np.ascontiguousarray(wq[:, nope_cols])
    rope_cols = []
    for hp in range(H // 2):
        b0 = (2 * hp) * 192 + 128
        b1 = (2 * hp + 1) * 192 + 128
        rope_cols += [b0 + 2 * i for i in range(32)]
        rope_cols += [b1 + 2 * i for i in range(32)]
        rope_cols += [b0 + 2 * i + 1 for i in range(32)]
        rope_cols += [b1 + 2 * i + 1 for i in range(32)]
    wqr_ = np.ascontiguousarray(wq[:, rope_cols])
    kpe_cols = [LORA + 2 * i for i in range(32)] + [LORA + 2 * i + 1 for i in range(32)]
    wkva_ = np.ascontiguousarray(
        np.concatenate([wkv_a[:, :LORA], wkv_a[:, kpe_cols]], axis=1))
    e = np.arange(32, dtype=np.float32) * np.float32(2.0 / 64.0)
    invf_ = (np.float32(1.0) / np.power(np.float32(10000.0), e, dtype=np.float32)
             ).astype(np.float32).reshape(32, 1)
    return wqn_, wqr_, wkva_, invf_


def kernel(L0_attn_x, L0_attn_wq, L0_attn_wkv_a, L0_attn_wkv_b, L0_attn_wo,
           L0_ffn_router_w, L0_ffn_gate_w, L0_ffn_up_w, L0_ffn_down_w):
    x = np.ascontiguousarray(np.asarray(L0_attn_x, dtype=np.float32))
    wqn_, wqr_, wkva_, invf_ = _host_prep({
        "L0_attn_wq": L0_attn_wq, "L0_attn_wkv_a": L0_attn_wkv_a})
    wkvb_ = np.ascontiguousarray(np.asarray(L0_attn_wkv_b, dtype=np.float32))
    wo_ = np.ascontiguousarray(np.asarray(L0_attn_wo, dtype=np.float32))
    rw_ = np.ascontiguousarray(np.asarray(L0_ffn_router_w, dtype=np.float32))
    gw_ = np.ascontiguousarray(np.asarray(L0_ffn_gate_w, dtype=np.float32))
    uw_ = np.ascontiguousarray(np.asarray(L0_ffn_up_w, dtype=np.float32))
    dw_ = np.ascontiguousarray(np.asarray(L0_ffn_down_w, dtype=np.float32))

    nc = _get_nc()
    in_maps = []
    for core in range(8):
        in_maps.append({
            "x": np.ascontiguousarray(x[core]),
            "wqn": wqn_, "wqr": wqr_, "wkva": wkva_, "wkvb": wkvb_,
            "wo": wo_, "rw": rw_, "gw": gw_, "uw": uw_, "dw": dw_,
            "invf": invf_,
        })
    res = run_bass_kernel_spmd(nc, in_maps, core_ids=list(range(8)))
    return np.stack([np.asarray(r["out"]) for r in res.results], axis=0)


# version 26
# speedup vs baseline: 1.1455x; 1.0043x over previous
import numpy as np
from contextlib import ExitStack

import concourse.bass as bass
import concourse.mybir as mybir
from concourse.bass_utils import run_bass_kernel_spmd

F32 = mybir.dt.float32
BF16 = mybir.dt.bfloat16
I32 = mybir.dt.int32
AF = mybir.ActivationFunctionType
ALU = mybir.AluOpType
AX = mybir.AxisListType

S = 2048
D = 2048
H = 16
NOPE = 128
ROPE = 64
VD = 128
LORA = 512
NEXP = 16
FF = 768
NT = S // 128
ND = D // 128
NFC = FF // 128
NQ = 4
HPQ = H // NQ
SCALE = float(1.0 / np.sqrt(NOPE + ROPE))
MAGIC = 12582912.0
TWO_PI = 6.283185307179586
CW1 = 6.28125
CW2 = float(np.float32(TWO_PI - CW1))
CW3 = float(TWO_PI - CW1 - np.float64(np.float32(TWO_PI - CW1)))
NEG_BIG = -1e30
LN_EPS = 1e-5


class DSem:
    def __init__(self, nc, stack, name):
        self.h = stack.enter_context(nc.semaphore(name))
        self.n = 0
        self.name = name

    def inc(self, instr, k):
        instr.then_inc(self.h, k)
        self.n += k
        return self.n


def build_nc(prev_marks):
    pass2 = prev_marks is not None
    marks = {}
    nc = bass.Bass("TRN2", target_bir_lowering=False)

    x = nc.dram_tensor("x", [S, D], F32, kind="ExternalInput").ap()
    wqn = nc.dram_tensor("wqn", [D, H * NOPE], F32, kind="ExternalInput").ap()
    wqr = nc.dram_tensor("wqr", [D, (H // 2) * 128], F32, kind="ExternalInput").ap()
    wkva = nc.dram_tensor("wkva", [D, LORA + ROPE], F32, kind="ExternalInput").ap()
    wkvb = nc.dram_tensor("wkvb", [LORA, H * 256], F32, kind="ExternalInput").ap()
    wo = nc.dram_tensor("wo", [H * VD, D], F32, kind="ExternalInput").ap()
    rw = nc.dram_tensor("rw", [D, NEXP], F32, kind="ExternalInput").ap()
    gw = nc.dram_tensor("gw", [NEXP, D, FF], F32, kind="ExternalInput").ap()
    uw = nc.dram_tensor("uw", [NEXP, D, FF], F32, kind="ExternalInput").ap()
    dw = nc.dram_tensor("dw", [NEXP, FF, D], F32, kind="ExternalInput").ap()
    invf = nc.dram_tensor("invf", [32, 1], F32, kind="ExternalInput").ap()
    out = nc.dram_tensor("out", [S, D], F32, kind="ExternalOutput").ap()
    scr = [nc.dram_tensor(f"scr{q}", [S, D], F32).ap() for q in range(NQ)]

    sem_by_name = {}

    def mark(key, sem):
        assert key not in marks, key
        marks[key] = (sem.name, sem.n)

    def wait(eng, key):
        if pass2:
            name, v = prev_marks[key]
            eng.wait_ge(sem_by_name[name], v)
        else:
            assert key not in marks or True
            eng.wait_ge(sem_by_name["PE"], 0)

    with ExitStack() as G:
        EG = G.enter_context
        sems = {}
        for nm in ["PE", "VE", "SE", "PO",
                   "dX0", "dX1", "dWQN", "dWQR", "dWKB", "dWO", "dKVA", "dRW", "dIF",
                   "dG0", "dG1", "dU0", "dU1", "dDW", "dSCR0", "dSCR1", "dOUT0", "dOUT1",
                   "dMISC", "dMRG0", "dMRG1", "dMG0", "dMG1", "dSCRA0", "dSCRA1", "dQR", "dQRS"]:
            sems[nm] = DSem(nc, G, nm)
            sem_by_name[nm] = sems[nm].h
        PE, VE, SE, PO = sems["PE"], sems["VE"], sems["SE"], sems["PO"]

        xnT = EG(nc.sbuf_tensor("xnT", [128, ND * 2048], BF16))
        comb = EG(nc.sbuf_tensor("comb", [128, NT * NEXP], F32))
        ident = EG(nc.sbuf_tensor("ident", [128, 128], BF16))
        identNeg = EG(nc.sbuf_tensor("identNeg", [128, 128], BF16))
        colA = EG(nc.sbuf_tensor("colA", [128, 1], F32))
        colB = EG(nc.sbuf_tensor("colB", [128, 1], F32))
        colC = EG(nc.sbuf_tensor("colC", [128, 1], F32))
        colD = EG(nc.sbuf_tensor("colD", [128, 1], F32))
        colE = EG(nc.sbuf_tensor("colE", [128, 1], F32))
        block = EG(nc.Block())

        def xnt(dc):
            return xnT[:, dc * 2048:(dc + 1) * 2048]

        with ExitStack() as M:
            EM = M.enter_context
            cosr = EM(nc.sbuf_tensor("cosr", [64, 2048], BF16))
            sinr = EM(nc.sbuf_tensor("sinr", [64, 2048], BF16))
            cT = EM(nc.sbuf_tensor("cT", [128, 4 * 2048], BF16))
            krope = EM(nc.sbuf_tensor("krope", [128, 2048], BF16))
            wqnb = EM(nc.sbuf_tensor("wqnb", [128, 2048], BF16))
            wqrb = EM(nc.sbuf_tensor("wqrb", [128, 2048], BF16))
            knb = EM(nc.sbuf_tensor("knb", [128, 512], BF16))
            vwb = EM(nc.sbuf_tensor("vwb", [128, 512], BF16))

            with ExitStack() as P:
                EP = P.enter_context
                xbuf = [EP(nc.sbuf_tensor(f"xbuf{i}", [128, 2048], F32)) for i in range(2)]
                trash = EP(nc.sbuf_tensor("trash", [128, 2048], BF16))
                xnb = EP(nc.sbuf_tensor("xnb", [128, 2048], BF16))
                tidx = EP(nc.sbuf_tensor("tidx", [32, 2048], F32))
                ivf = EP(nc.sbuf_tensor("ivf", [32, 1], F32))
                frq = EP(nc.sbuf_tensor("frq", [32, 2048], F32))
                kfb = EP(nc.sbuf_tensor("kfb", [32, 2048], F32))
                lnc = EP(nc.sbuf_tensor("lnc", [128, 8], F32))
                psT0 = EP(nc.psum_tensor("psT0", [128, 1024], BF16))
                psT1 = EP(nc.psum_tensor("psT1", [128, 1024], BF16))

                @block.sync
                def _(sy):
                    i = sy.dma_start(ivf[:], invf)
                    sems["dIF"].inc(i, 16)
                    mark(("ivf",), sems["dIF"])
                    for tc in range(NT):
                        if tc >= 2:
                            wait(sy, ("lnused", tc - 2))
                        i = sy.dma_start(xbuf[tc % 2][:], x[tc * 128:(tc + 1) * 128, :])
                        sems[f"dX{tc % 2}"].inc(i, 16)
                        mark(("xload", tc), sems[f"dX{tc % 2}"])

                @block.gpsimd
                def _(g):
                    g.memset(krope[64:128, :], 0.0)
                    g.drain()
                    g.memset(ident[:], 0.0)
                    g.drain()
                    i = g.affine_select(out=ident[:], in_=ident[:], compare_op=ALU.not_equal,
                                        fill=1.0, base=0, pattern=[[-1, 128]], channel_multiplier=1)
                    PO.inc(i, 1)
                    mark(("ident",), PO)
                    g.memset(identNeg[:], 0.0)
                    g.drain()
                    i = g.affine_select(out=identNeg[:], in_=identNeg[:], compare_op=ALU.is_ge,
                                        fill=NEG_BIG, base=0, pattern=[[1, 128]], channel_multiplier=-1)
                    PO.inc(i, 1)
                    mark(("cmask",), PO)
                    i = g.iota(tidx[:], pattern=[[1, 2048]], base=0, channel_multiplier=0,
                               allow_small_or_imprecise_dtypes=True)
                    PO.inc(i, 1)
                    mark(("tidx",), PO)

                @block.vector
                def _(v):
                    wait(v, ("tidx",))
                    wait(v, ("ivf",))
                    v.tensor_scalar(out=frq[:], in0=tidx[:], scalar1=ivf[:], scalar2=None, op0=ALU.mult)
                    v.drain()
                    v.tensor_scalar(out=kfb[:], in0=frq[:], scalar1=float(1.0 / TWO_PI),
                                    scalar2=MAGIC, op0=ALU.mult, op1=ALU.add)
                    v.drain()
                    v.tensor_scalar(out=kfb[:], in0=kfb[:], scalar1=MAGIC, scalar2=None, op0=ALU.subtract)
                    v.drain()
                    v.scalar_tensor_tensor(out=frq[:], in0=kfb[:], scalar=-CW1, in1=frq[:],
                                           op0=ALU.mult, op1=ALU.add)
                    v.drain()
                    v.scalar_tensor_tensor(out=frq[:], in0=kfb[:], scalar=-CW2, in1=frq[:],
                                           op0=ALU.mult, op1=ALU.add)
                    v.drain()
                    v.tensor_scalar(out=frq[:], in0=frq[:], scalar1=float(np.pi), scalar2=None, op0=ALU.min)
                    v.drain()
                    i = v.tensor_scalar(out=frq[:], in0=frq[:], scalar1=float(-np.pi), scalar2=None, op0=ALU.max)
                    v.drain()
                    VE.inc(i, 1)
                    mark(("ropearg",), VE)
                    v.tensor_scalar(out=kfb[:], in0=frq[:], scalar1=-1.0, scalar2=None, op0=ALU.mult)
                    v.drain()
                    v.tensor_tensor(out=kfb[:], in0=kfb[:], in1=frq[:], op=ALU.max)
                    v.drain()
                    i = v.tensor_scalar(out=kfb[:], in0=kfb[:], scalar1=-1.0, scalar2=float(np.pi / 2),
                                        op0=ALU.mult, op1=ALU.add)
                    v.drain()
                    VE.inc(i, 1)
                    mark(("cosarg",), VE)
                    def emit_xnb(j):
                        wait(v, ("rstd", j))
                        v.tensor_scalar(out=lnc[:, j % 2:j % 2 + 1], in0=lnc[:, j % 2:j % 2 + 1],
                                        scalar1=-1.0, scalar2=None, op0=ALU.mult)
                        v.drain()
                        i = v.tensor_scalar(out=xnb[:], in0=xbuf[j % 2][:],
                                            scalar1=lnc[:, j % 2:j % 2 + 1],
                                            scalar2=lnc[:, 4 + j % 2:5 + j % 2],
                                            op0=ALU.add, op1=ALU.mult)
                        v.drain()
                        VE.inc(i, 1)
                        mark(("xn", j), VE)
                        mark(("lnused", j), VE)

                    def emit_evac(j):
                        for half in range(2):
                            pt = psT0 if half == 0 else psT1
                            wait(v, ("xpose", j, half))
                            dst = xnT[:, half * 8 * 2048: half * 8 * 2048 + 8 * 2048].rearrange(
                                "p (dc c) -> p dc c", c=2048)[:, :, j * 128:j * 128 + 128]
                            src = pt[:].rearrange("p (j c) -> p j c", c=128)
                            i = v.tensor_copy(dst, src)
                            VE.inc(i, 1)
                            mark(("xnTevac", j, half), VE)

                    for tc in range(NT):
                        wait(v, ("xload", tc))
                        xb = xbuf[tc % 2]
                        v.reduce_sum(colA[:], xb[:], axis=AX.X)
                        v.drain()
                        v.tensor_scalar(out=lnc[:, tc % 2:tc % 2 + 1], in0=colA[:],
                                        scalar1=float(1.0 / D), scalar2=None, op0=ALU.mult)
                        v.drain()
                        wait(v, ("sq", tc))
                        v.tensor_scalar(out=lnc[:, 2 + tc % 2:3 + tc % 2], in0=colB[:],
                                        scalar1=float(1.0 / D), scalar2=None, op0=ALU.mult)
                        v.tensor_tensor(out=colA[:], in0=lnc[:, tc % 2:tc % 2 + 1],
                                        in1=lnc[:, tc % 2:tc % 2 + 1], op=ALU.mult)
                        v.drain()
                        v.tensor_tensor(out=lnc[:, 2 + tc % 2:3 + tc % 2],
                                        in0=lnc[:, 2 + tc % 2:3 + tc % 2], in1=colA[:], op=ALU.subtract)
                        v.drain()
                        i = v.tensor_scalar(out=lnc[:, 2 + tc % 2:3 + tc % 2],
                                            in0=lnc[:, 2 + tc % 2:3 + tc % 2], scalar1=LN_EPS,
                                            scalar2=None, op0=ALU.add)
                        v.drain()
                        VE.inc(i, 1)
                        mark(("var", tc), VE)
                        if tc >= 1:
                            emit_xnb(tc - 1)
                            emit_evac(tc - 1)
                    emit_xnb(NT - 1)
                    emit_evac(NT - 1)

                @block.scalar
                def _(s):
                    wait(s, ("ropearg",))
                    wait(s, ("cosarg",))
                    s.activation(sinr[0:32, :], frq[:], AF.Sin)
                    i = s.activation(cosr[0:32, :], kfb[:], AF.Sin)
                    SE.inc(i, 1)
                    mark(("cossin",), SE)
                    for tc in range(NT):
                        wait(s, ("xload", tc))
                        xb = xbuf[tc % 2]
                        i = s.activation(trash[:], xb[:], AF.Square, accum_out=colB[:])
                        s.drain()
                        SE.inc(i, 1)
                        mark(("sq", tc), SE)
                        wait(s, ("var", tc))
                        s.activation(lnc[:, 4 + tc % 2:5 + tc % 2], lnc[:, 2 + tc % 2:3 + tc % 2], AF.Ln)
                        s.drain()
                        i = s.activation(lnc[:, 4 + tc % 2:5 + tc % 2],
                                         lnc[:, 4 + tc % 2:5 + tc % 2], AF.Exp, scale=-0.5)
                        s.drain()
                        SE.inc(i, 1)
                        mark(("rstd", tc), SE)


                @block.gpsimd
                def _(g):
                    wait(g, ("cossin",))
                    g.dma_start(cosr[32:64, :], cosr[0:32, :]).then_inc(sems["dMISC"].h, 16)
                    sems["dMISC"].n += 16
                    i = g.dma_start(sinr[32:64, :], sinr[0:32, :])
                    sems["dMISC"].inc(i, 16)
                    mark(("cossinrep",), sems["dMISC"])

                @block.tensor
                def _(t):
                    wait(t, ("ident",))
                    for tc in range(NT):
                        wait(t, ("xn", tc))
                        for half in range(2):
                            pt = psT0 if half == 0 else psT1
                            if tc >= 1:
                                wait(t, ("xnTevac", tc - 1, half))
                            for j in range(8):
                                dc = half * 8 + j
                                i = t.transpose(pt[:, j * 128:j * 128 + 128],
                                                xnb[:, dc * 128:(dc + 1) * 128], ident[:])
                            PE.inc(i, 1)
                            mark(("xpose", tc, half), PE)

            with ExitStack() as P:
                EP = P.enter_context
                kvab = EP(nc.sbuf_tensor("kvab", [128, ND * 576], BF16))
                kpe_bf = EP(nc.sbuf_tensor("kpe_bf", [64, 2048], BF16))
                kpeO = EP(nc.sbuf_tensor("kpeO", [32, 2048], BF16))
                krO32 = EP(nc.sbuf_tensor("krO32", [32, 2048], BF16))
                r1 = EP(nc.sbuf_tensor("r1", [32, 2048], BF16))
                r2 = EP(nc.sbuf_tensor("r2", [32, 2048], BF16))
                rwb = EP(nc.sbuf_tensor("rwb", [128, ND * NEXP], BF16))
                eq1 = EP(nc.sbuf_tensor("eq1", [128, NEXP], I32))
                eq2 = EP(nc.sbuf_tensor("eq2", [128, NEXP], I32))
                rprob = EP(nc.sbuf_tensor("rprob", [128, NEXP], F32))
                rp2 = EP(nc.sbuf_tensor("rp2", [128, NEXP], F32))
                rtmp = EP(nc.sbuf_tensor("rtmp", [128, NEXP], F32))
                rneg = EP(nc.sbuf_tensor("rneg", [128, NEXP], F32))
                psDummy = EP(nc.psum_tensor("psDummy", [128, 2048], F32))
                psP = [EP(nc.psum_tensor(f"psP{i}", [128, 512], F32)) for i in range(4)]

                @block.gpsimd
                def _(g):
                    wait(g, ("xnTevac", NT - 1, 1))
                    wait(g, ("cossin",))
                    i = g.dma_start(kvab[:].rearrange("p (dc m) -> p dc m", m=576),
                                    wkva.rearrange("(dc p) m -> p dc m", p=128))
                    sems["dKVA"].inc(i, 16)
                    mark(("kva",), sems["dKVA"])
                    i = g.dma_start(rwb[:].rearrange("p (dc e) -> p dc e", e=NEXP),
                                    rw.rearrange("(dc p) e -> p dc e", p=128))
                    sems["dRW"].inc(i, 16)
                    mark(("rwl",), sems["dRW"])
                    i = g.dma_start(wqnb[:].rearrange("p (dc c) -> p dc c", c=NOPE),
                                    wqn.rearrange("(dc p) c -> p dc c", p=128)[:, :, 0:NOPE])
                    sems["dWQN"].inc(i, 16)
                    mark(("wqnl", 0), sems["dWQN"])
                    i = g.dma_start(wqrb[:].rearrange("p (dc c) -> p dc c", c=128),
                                    wqr.rearrange("(dc p) c -> p dc c", p=128)[:, :, 0:128])
                    sems["dWQR"].inc(i, 16)
                    mark(("wqrl", 0), sems["dWQR"])
                    i = g.dma_start(knb[:].rearrange("p (lc c) -> p lc c", c=128),
                                    wkvb.rearrange("(lc p) c -> p lc c", p=128)[:, :, 0:128])
                    sems["dWKB"].inc(i, 16)
                    i = g.dma_start(vwb[:].rearrange("p (lc c) -> p lc c", c=128),
                                    wkvb.rearrange("(lc p) c -> p lc c", p=128)[:, :, 128:256])
                    sems["dWKB"].inc(i, 16)
                    mark(("wkbl", 0), sems["dWKB"])
                    wait(g, ("ckvevac", 19))
                    i = g.dma_start(kpeO[:], kpe_bf[32:64, :])
                    sems["dQRS"].inc(i, 16)
                    mark(("kpeO",), sems["dQRS"])
                    wait(g, ("kropePart",))
                    i = g.dma_start(krope[32:64, :], krO32[:])
                    sems["dQRS"].inc(i, 16)
                    mark(("krope",), sems["dQRS"])

                @block.tensor
                def _(t):
                    wait(t, ("kva",))
                    wait(t, ("rwl",))
                    for nb in range(4):
                        for tc in range(nb * 4, nb * 4 + 4):
                            ps = psP[tc % 4]
                            wait(t, ("xnTevac", tc, 1))
                            if tc >= 4:
                                wait(t, ("rexp", tc - 4))
                            for dc in range(ND):
                                i = t.matmul(ps[:, 0:NEXP],
                                             xnt(dc)[:, tc * 128:(tc + 1) * 128],
                                             rwb[:, dc * NEXP:(dc + 1) * NEXP],
                                             start=(dc == 0), stop=(dc == ND - 1))
                            PE.inc(i, 1)
                            mark(("rscore", tc), PE)
                        for mc in range(5):
                            k = nb * 5 + mc
                            ps = psDummy[:, (k % 4) * 512:(k % 4 + 1) * 512]
                            if k >= 4:
                                wait(t, ("ckvevac", k - 4))
                            Mrows = 128 if mc < 4 else 64
                            for dc in range(ND):
                                i = t.matmul(ps[0:Mrows],
                                             kvab[:, dc * 576 + mc * 128: dc * 576 + mc * 128 + Mrows],
                                             xnt(dc)[:, nb * 512:(nb + 1) * 512],
                                             start=(dc == 0), stop=(dc == ND - 1))
                            PE.inc(i, 1)
                            mark(("ckv", k), PE)

                @block.vector
                def _(v):
                    def emit_ckvevac(k):
                        nb2, mc2 = divmod(k, 5)
                        ps2 = psDummy[:, (k % 4) * 512:(k % 4 + 1) * 512]
                        wait(v, ("ckv", k))
                        if mc2 < 4:
                            i2 = v.tensor_copy(cT[:, mc2 * 2048 + nb2 * 512: mc2 * 2048 + (nb2 + 1) * 512], ps2)
                        else:
                            i2 = v.tensor_copy(kpe_bf[:, nb2 * 512:(nb2 + 1) * 512], ps2[0:64])
                        VE.inc(i2, 1)
                        mark(("ckvevac", k), VE)

                    wait(v, ("cossin",))
                    v.memset(rneg[:], NEG_BIG)
                    for tc in range(NT):
                        wait(v, ("rscore", tc))
                        ps = psP[tc % 4]
                        i = v.reduce_max(colA[:], ps[:, 0:NEXP], axis=AX.X, negate=True)
                        v.drain()
                        VE.inc(i, 1)
                        mark(("rmax", tc), VE)
                        wait(v, ("rexps", tc))
                        v.reduce_max(colA[:], rprob[:], axis=AX.X)
                        v.drain()
                        v.tensor_scalar(out=eq1[:], in0=rprob[:], scalar1=colA[:], scalar2=None, op0=ALU.is_equal)
                        v.tensor_copy(rp2[:], rprob[:])
                        v.drain()
                        v.copy_predicated(out=rp2[:], mask=eq1[:], data=rneg[:])
                        v.drain()
                        v.reduce_max(colB[:], rp2[:], axis=AX.X)
                        v.drain()
                        v.tensor_scalar(out=eq2[:], in0=rp2[:], scalar1=colB[:], scalar2=None, op0=ALU.is_equal)
                        v.drain()
                        v.tensor_tensor(out=colC[:], in0=colA[:], in1=colB[:], op=ALU.add)
                        v.drain()
                        v.reciprocal(colC[:], colC[:])
                        v.drain()
                        v.tensor_tensor(out=colD[:], in0=colA[:], in1=colC[:], op=ALU.mult)
                        v.tensor_tensor(out=colE[:], in0=colB[:], in1=colC[:], op=ALU.mult)
                        v.drain()
                        v.tensor_scalar(out=rtmp[:], in0=rprob[:], scalar1=0.0, scalar2=colD[:],
                                        op0=ALU.mult, op1=ALU.add)
                        v.tensor_scalar(out=rp2[:], in0=rprob[:], scalar1=0.0, scalar2=colE[:],
                                        op0=ALU.mult, op1=ALU.add)
                        cslice = comb[:, tc * NEXP:(tc + 1) * NEXP]
                        v.memset(cslice, 0.0)
                        v.drain()
                        v.copy_predicated(out=cslice, mask=eq1[:], data=rtmp[:])
                        v.drain()
                        i = v.copy_predicated(out=cslice, mask=eq2[:], data=rp2[:])
                        v.drain()
                        VE.inc(i, 1)
                        mark(("comb", tc), VE)
                        mark(("rexp", tc), VE)
                        done = (tc * 20) // NT
                        upto = ((tc + 1) * 20) // NT
                        for k2 in range(done, upto):
                            emit_ckvevac(k2)
                    wait(v, ("cossinrep",))
                    wait(v, ("kpeO",))
                    v.tensor_tensor(out=r1[:], in0=kpe_bf[0:32, :], in1=cosr[0:32, :], op=ALU.mult)
                    v.tensor_tensor(out=r2[:], in0=kpeO[:], in1=sinr[0:32, :], op=ALU.mult)
                    v.drain()
                    v.tensor_tensor(out=krope[0:32, :], in0=r1[:], in1=r2[:], op=ALU.subtract)
                    v.drain()
                    v.tensor_tensor(out=r1[:], in0=kpeO[:], in1=cosr[0:32, :], op=ALU.mult)
                    v.tensor_tensor(out=r2[:], in0=kpe_bf[0:32, :], in1=sinr[0:32, :], op=ALU.mult)
                    v.drain()
                    i = v.tensor_tensor(out=krO32[:], in0=r1[:], in1=r2[:], op=ALU.add)
                    v.drain()
                    VE.inc(i, 1)
                    mark(("kropePart",), VE)

                @block.scalar
                def _(s):
                    for tc in range(NT):
                        wait(s, ("rmax", tc))
                        ps = psP[tc % 4]
                        if tc >= 1:
                            wait(s, ("comb", tc - 1))
                        i = s.activation(rprob[:], ps[:, 0:NEXP], AF.Exp, bias=colA[:])
                        s.drain()
                        SE.inc(i, 1)
                        mark(("rexps", tc), SE)

            with ExitStack() as P:
                EP = P.enter_context
                oT = EP(nc.sbuf_tensor("oT", [128, HPQ * 2048], BF16))
                woq = EP(nc.sbuf_tensor("woq", [128, HPQ * 2048], BF16))
                kns = EP(nc.sbuf_tensor("kns", [128, 2048], BF16))
                vs = EP(nc.sbuf_tensor("vs", [128, 2048], BF16))
                qn = EP(nc.sbuf_tensor("qn", [128, 2048], BF16))
                qrA = EP(nc.sbuf_tensor("qrA", [128, 2048], BF16))
                qrB = EP(nc.sbuf_tensor("qrB", [128, 2048], BF16))
                probs2 = [EP(nc.sbuf_tensor(f"probs{i}", [128, 2048], BF16)) for i in range(2)]
                probs = probs2[0]
                probsT2 = [EP(nc.sbuf_tensor(f"probsT{i}", [128, 2048], BF16)) for i in range(4)]
                probsT = probsT2[0]
                rtmp = EP(nc.sbuf_tensor("rtmp64", [64, 2048], BF16))
                colM = EP(nc.sbuf_tensor("colM", [128, 4], F32))
                colB4 = EP(nc.sbuf_tensor("colB4", [128, 4], F32))
                colT = EP(nc.sbuf_tensor("colT", [128, 4], F32))
                stage = [EP(nc.sbuf_tensor(f"stage{i}", [128, 2048], F32)) for i in range(2)]
                xw = EP(nc.sbuf_tensor("xw", [128, 2048], F32))
                psA = EP(nc.psum_tensor("psA", [128, 2048], F32))
                psT0 = EP(nc.psum_tensor("psTa", [128, 1024], BF16))
                psT1 = EP(nc.psum_tensor("psTb", [128, 1024], BF16))
                psAV = [EP(nc.psum_tensor(f"psAV{i}", [128, 512], F32)) for i in range(2)]

                psT_last = {0: None, 1: None}
                psW_last = {0: None, 1: None, 2: None, 3: None}
                psAV_last = {0: None, 1: None}

                @block.gpsimd
                def _(g):
                    wait(g, ("krope",))
                    for h in range(H):
                        if h >= 1:
                            wait(g, ("qnMM", h - 1))
                            i = g.dma_start(wqnb[:].rearrange("p (dc c) -> p dc c", c=NOPE),
                                            wqn.rearrange("(dc p) c -> p dc c", p=128)[:, :, h * NOPE:(h + 1) * NOPE])
                            sems["dWQN"].inc(i, 16)
                            mark(("wqnl", h), sems["dWQN"])
                        if h % 2 == 0 and h >= 2:
                            hp = h // 2
                            wait(g, ("qrMM", hp - 1))
                            i = g.dma_start(wqrb[:].rearrange("p (dc c) -> p dc c", c=128),
                                            wqr.rearrange("(dc p) c -> p dc c", p=128)[:, :, hp * 128:(hp + 1) * 128])
                            sems["dWQR"].inc(i, 16)
                            mark(("wqrl", hp), sems["dWQR"])
                        if h >= 1:
                            wait(g, ("knvMM", h - 1))
                            i = g.dma_start(knb[:].rearrange("p (lc c) -> p lc c", c=128),
                                            wkvb.rearrange("(lc p) c -> p lc c", p=128)[:, :, h * 256:h * 256 + 128])
                            sems["dWKB"].inc(i, 16)
                            i = g.dma_start(vwb[:].rearrange("p (lc c) -> p lc c", c=128),
                                            wkvb.rearrange("(lc p) c -> p lc c", p=128)[:, :, h * 256 + 128:h * 256 + 256])
                            sems["dWKB"].inc(i, 16)
                            mark(("wkbl", h), sems["dWKB"])
                        if h % HPQ == 0:
                            q = h // HPQ
                            if q >= 1:
                                wait(g, ("woMM", q - 1))
                            for hh in range(HPQ):
                                i = g.dma_start(woq[:, hh * 2048:(hh + 1) * 2048],
                                                wo[(q * HPQ + hh) * 128:(q * HPQ + hh) * 128 + 128, :])
                                sems["dWO"].inc(i, 16)
                            mark(("wol", q), sems["dWO"])
                        if h % HPQ == HPQ - 1 and (h // HPQ) % 2 == 1:
                            q2 = h // HPQ
                            for tc in range(NT):
                                wait(g, ("stg", q2, tc))
                                wait(g, ("scrw", q2 - 1, tc))
                                i = g.dma_start(scr[q2 - 1][tc * 128:(tc + 1) * 128, :],
                                                stage[tc % 2][:], accum_op=ALU.add)
                                sems[f"dSCRA{tc % 2}"].inc(i, 16)
                                mark(("scrw", q2, tc), sems[f"dSCRA{tc % 2}"])
                        if h % 2 == 0:
                            hp = h // 2
                            wait(g, ("ropemath", hp))
                            if hp == 0:
                                g.memset(qrA[64:128, :], 0.0)
                                g.drain()
                                g.memset(qrB[64:128, :], 0.0)
                                g.drain()
                            g.dma_start(qrA[0:32, :], stage[0][0:32, :]).then_inc(sems["dQRS"].h, 16)
                            sems["dQRS"].n += 16
                            g.dma_start(qrA[32:64, :], stage[1][0:32, :]).then_inc(sems["dQRS"].h, 16)
                            sems["dQRS"].n += 16
                            g.dma_start(qrB[0:32, :], stage[0][32:64, :]).then_inc(sems["dQRS"].h, 16)
                            sems["dQRS"].n += 16
                            i = g.dma_start(qrB[32:64, :], stage[1][32:64, :])
                            sems["dQRS"].inc(i, 16)
                            mark(("qrready", hp), sems["dQRS"])

                @block.sync
                def _(sy):
                    for q in range(NQ):
                        for hp in (2 * q, 2 * q + 1):
                            wait(sy, ("qevac", hp))
                            i = sy.dma_start(probsT[0:64, :], probs[64:128, :])
                            sems["dQR"].inc(i, 16)
                            mark(("qodd", hp), sems["dQR"])
                        if q == 0:
                            i = sy.dma_start(xw[:], x[0:128, :])
                            sems["dX0"].inc(i, 16)
                            mark(("xw", 0), sems["dX0"])
                        for tc in range(NT):
                            if q % 2 == 0:
                                wait(sy, ("stg", q, tc))
                                i = sy.dma_start(scr[q][tc * 128:(tc + 1) * 128, :], stage[tc % 2][:])
                                sems[f"dSCR{tc % 2}"].inc(i, 16)
                                mark(("scrw", q, tc), sems[f"dSCR{tc % 2}"])
                            if q == 0 and tc + 1 < NT:
                                i = sy.dma_start(xw[:], x[(tc + 1) * 128:(tc + 2) * 128, :])
                                sems["dX0"].inc(i, 16)
                                mark(("xw", tc + 1), sems["dX0"])

                @block.tensor
                def _(t):
                    wait(t, ("krope",))
                    for h in range(H):
                        hp, q, hh = h // 2, h // HPQ, h % HPQ
                        wait(t, ("wqnl", h))
                        if h >= 1:
                            wait(t, ("vevac", h - 1, 3))
                            wait(t, ("exp", h - 1, NT - 1))
                        else:
                            wait(t, ("ckvevac", 19))
                        if h % 2 == 0:
                            wait(t, ("wqrl", hp))
                            for nb in range(4):
                                if psW_last[nb] is not None:
                                    wait(t, psW_last[nb])
                                    psW_last[nb] = None
                                for dc in range(ND):
                                    i = t.matmul(psA[:, nb * 512:(nb + 1) * 512],
                                                 wqrb[:, dc * 128:(dc + 1) * 128],
                                                 xnt(dc)[:, nb * 512:(nb + 1) * 512],
                                                 start=(dc == 0), stop=(dc == ND - 1))
                                PE.inc(i, 1)
                                mark(("qrpsb", hp, nb), PE)
                            mark(("qrps", hp), PE)
                            mark(("qrMM", hp), PE)
                        for nb in range(4):
                            if psW_last[nb] is not None:
                                wait(t, psW_last[nb])
                                psW_last[nb] = None
                            if h % 2 == 0:
                                wait(t, ("qevacb", hp, nb))
                            for dc in range(ND):
                                i = t.matmul(psA[:, nb * 512:(nb + 1) * 512],
                                             wqnb[:, dc * NOPE:(dc + 1) * NOPE],
                                             xnt(dc)[:, nb * 512:(nb + 1) * 512],
                                             start=(dc == 0), stop=(dc == ND - 1))
                            PE.inc(i, 1)
                            mark(("qnpsb", h, nb), PE)
                        mark(("qnps", h), PE)
                        mark(("qnMM", h), PE)
                        wait(t, ("wkbl", h))
                        if h == 0:
                            wait(t, ("ckvevac", 19))
                        for nb in range(4):
                            wait(t, ("qnevacb", h, nb))
                            for lc in range(4):
                                i = t.matmul(psA[:, nb * 512:(nb + 1) * 512],
                                             knb[:, lc * 128:(lc + 1) * 128],
                                             cT[:, lc * 2048 + nb * 512: lc * 2048 + (nb + 1) * 512],
                                             start=(lc == 0), stop=(lc == 3))
                            PE.inc(i, 1)
                            mark(("knpsb", h, nb), PE)
                        mark(("knps", h), PE)
                        for tc in range(NT):
                            if tc % 4 == 0:
                                wait(t, ("knevacb", h, tc // 4))
                            for lc in range(4):
                                i = t.matmul(psA[:, tc * 128:(tc + 1) * 128],
                                             cT[:, lc * 2048 + tc * 128: lc * 2048 + (tc + 1) * 128],
                                             vwb[:, lc * 128:(lc + 1) * 128],
                                             start=(lc == 0), stop=(lc == 3))
                            if tc % 4 == 3:
                                PE.inc(i, 1)
                                mark(("vps", h, tc // 4), PE)
                        mark(("knvMM", h), PE)
                        qr = qrA if h % 2 == 0 else qrB
                        wait(t, ("qrready", hp))
                        def emit_trp(j_qt):
                            wait(t, ("norm", h, j_qt))
                            pj = probs2[j_qt % 2]
                            nbat = j_qt // 8 + 1
                            for b in range(nbat):
                                pt = psT0 if b % 2 == 0 else psT1
                                ptidx = b % 2
                                if psT_last[ptidx] is not None:
                                    wait(t, psT_last[ptidx])
                                n_in_b = min(8, j_qt + 1 - b * 8)
                                for j in range(n_in_b):
                                    jj = b * 8 + j
                                    i2 = t.transpose(pt[:, j * 128:(j + 1) * 128],
                                                     pj[:, jj * 128:(jj + 1) * 128], ident[:])
                                PE.inc(i2, 1)
                                mark(("trp", h, j_qt, b), PE)
                                psT_last[ptidx] = ("trpevac", h, j_qt, b)

                        def emit_av(j_qt):
                            nbat = j_qt // 8 + 1
                            wait(t, ("trpevac", h, j_qt, nbat - 1))
                            if psAV_last[j_qt % 2] is not None:
                                wait(t, psAV_last[j_qt % 2])
                            pT = probsT2[j_qt % 4]
                            for j in range(j_qt + 1):
                                i2 = t.matmul(psAV[j_qt % 2][:, 0:128],
                                              vs[:, j * 128:(j + 1) * 128],
                                              pT[:, j * 128:(j + 1) * 128],
                                              start=(j == 0), stop=(j == j_qt))
                            PE.inc(i2, 1)
                            mark(("av", h, j_qt), PE)
                            psAV_last[j_qt % 2] = ("avevac", h, j_qt)

                        for qt in range(NT):
                            kv = (qt + 1) * 128
                            base = (qt % 2) * 1024 if qt <= 7 else 0
                            if qt == 0:
                                wait(t, ("vevac", h, 1))
                            elif qt == 1:
                                wait(t, ("vevac", h, 3))
                            nkb = (kv + 511) // 512
                            nkb_prev = (qt * 128 + 511) // 512
                            if qt == 8:
                                wait(t, ("exp", h, 6))
                                wait(t, ("exp", h, 7))
                            for kb in range(nkb):
                                wdt = min(512, kv - kb * 512)
                                if 2 <= qt <= 7:
                                    pk = (qt - 1) * 128
                                    wait(t, ("expb", h, qt - 2, min(kb, (pk + 511) // 512 - 1)))
                                elif qt >= 9:
                                    wait(t, ("expb", h, qt - 1, min(kb, nkb_prev - 1)))
                                t.matmul(psA[:, base + kb * 512: base + kb * 512 + wdt],
                                         qn[:, qt * 128:(qt + 1) * 128],
                                         kns[:, kb * 512: kb * 512 + wdt],
                                         start=True, stop=False)
                            for kb in range(nkb):
                                wdt = min(512, kv - kb * 512)
                                diag_here = (kb == nkb - 1)
                                i = t.matmul(psA[:, base + kb * 512: base + kb * 512 + wdt],
                                             qr[:, qt * 128:(qt + 1) * 128],
                                             krope[:, kb * 512: kb * 512 + wdt],
                                             start=False, stop=not diag_here)
                                if not diag_here:
                                    PE.inc(i, 1)
                                    mark(("sb", h, qt, kb), PE)
                            i = t.matmul(psA[:, base + qt * 128: base + qt * 128 + 128],
                                         identNeg[:], ident[:],
                                         start=False, stop=True)
                            PE.inc(i, 1)
                            mark(("sb", h, qt, nkb - 1), PE)
                            mark(("scores", h, qt), PE)
                            if qt >= 1:
                                emit_trp(qt - 1)
                            if qt >= 3:
                                emit_av(qt - 3)
                        emit_trp(NT - 1)
                        emit_av(NT - 3)
                        emit_av(NT - 2)
                        emit_av(NT - 1)
                        if hh == HPQ - 1:
                            wait(t, ("wol", q))
                            wait(t, ("avevac", h, NT - 1))
                            for tc in range(NT):
                                for dcol in range(4):
                                    slot = (tc * 4 + dcol) % 6
                                    psW = psA[:, slot * 512:(slot + 1) * 512] if slot < 4 else psAV[slot - 4][:]
                                    if slot < 4:
                                        if psW_last[slot] is not None:
                                            wait(t, psW_last[slot])
                                    else:
                                        if psAV_last[slot - 4] is not None:
                                            wait(t, psAV_last[slot - 4])
                                    for h2 in range(HPQ):
                                        i = t.matmul(psW,
                                                     oT[:, h2 * 2048 + tc * 128: h2 * 2048 + (tc + 1) * 128],
                                                     woq[:, h2 * 2048 + dcol * 512: h2 * 2048 + (dcol + 1) * 512],
                                                     start=(h2 == 0), stop=(h2 == HPQ - 1))
                                    PE.inc(i, 1)
                                    mark(("wops", q, tc, dcol), PE)
                                    if slot < 4:
                                        psW_last[slot] = ("stgevac", q, tc, dcol)
                                    else:
                                        psAV_last[slot - 4] = ("stgevac", q, tc, dcol)
                            mark(("woMM", q), PE)

                def emit_rope_math(v, hp, h):
                    if h % HPQ == 0 and h // HPQ >= 1:
                        wait(v, ("scrw", h // HPQ - 1, NT - 1))
                        wait(v, ("scrw", h // HPQ - 1, NT - 2))
                    if hp >= 1:
                        wait(v, ("qrready", hp - 1))
                    wait(v, ("qodd", hp))
                    v.tensor_tensor(out=stage[0][0:64, :], in0=probs[0:64, :], in1=cosr[:], op=ALU.mult)
                    v.tensor_tensor(out=stage[1][0:64, :], in0=probsT[0:64, :], in1=sinr[:], op=ALU.mult)
                    v.drain()
                    v.tensor_tensor(out=stage[0][0:64, :], in0=stage[0][0:64, :], in1=stage[1][0:64, :], op=ALU.subtract)
                    v.drain()
                    v.tensor_tensor(out=stage[1][0:64, :], in0=probsT[0:64, :], in1=cosr[:], op=ALU.mult)
                    v.tensor_tensor(out=rtmp[:], in0=probs[0:64, :], in1=sinr[:], op=ALU.mult)
                    v.drain()
                    v.tensor_tensor(out=stage[1][0:64, :], in0=stage[1][0:64, :], in1=rtmp[:], op=ALU.add)
                    v.drain()
                    v.tensor_scalar(out=stage[0][0:64, :], in0=stage[0][0:64, :],
                                    scalar1=SCALE, scalar2=None, op0=ALU.mult)
                    i = v.tensor_scalar(out=stage[1][0:64, :], in0=stage[1][0:64, :],
                                        scalar1=SCALE, scalar2=None, op0=ALU.mult)
                    v.drain()
                    VE.inc(i, 1)
                    mark(("ropemath", hp), VE)

                @block.vector
                def _(v):
                    for h in range(H):
                        hp, q, hh = h // 2, h // HPQ, h % HPQ
                        if h % 2 == 0:
                            emit_rope_math(v, hp, h)
                        for qt in range(NT):
                            kv = (qt + 1) * 128
                            base = (qt % 2) * 1024 if qt <= 7 else 0
                            nkb = (kv + 511) // 512
                            if nkb == 1:
                                wait(v, ("sb", h, qt, 0))
                                i = v.reduce_max(colD[:], psA[:, base:base + kv], axis=AX.X, negate=True)
                                v.drain()
                            else:
                                for kb in range(nkb):
                                    wdt = min(512, kv - kb * 512)
                                    wait(v, ("sb", h, qt, kb))
                                    v.reduce_max(colM[:, kb:kb + 1],
                                                 psA[:, base + kb * 512: base + kb * 512 + wdt], axis=AX.X)
                                v.drain()
                                i = v.reduce_max(colD[:], colM[:, 0:nkb], axis=AX.X, negate=True)
                                v.drain()
                            VE.inc(i, 1)
                            mark(("smax", h, qt), VE)
                            wait(v, ("exp", h, qt))
                            v.reciprocal(colC[:], colB[:])
                            v.drain()
                            i = v.tensor_scalar(out=probs2[qt % 2][:, 0:kv], in0=probs2[qt % 2][:, 0:kv],
                                                scalar1=colC[:], scalar2=None, op0=ALU.mult)
                            v.drain()
                            VE.inc(i, 1)
                            mark(("norm", h, qt), VE)
                        if hh == HPQ - 1:
                            for tc in range(NT):
                                for dcol in range(4):
                                    slot = (tc * 4 + dcol) % 6
                                    wait(v, ("wops", q, tc, dcol))
                                    if tc == 0 and dcol == 0:
                                        wait(v, ("qrready", 2 * q + 1))
                                    if dcol == 0:
                                        if tc >= 2:
                                            wait(v, ("scrw", q, tc - 2))
                                        elif q >= 1:
                                            wait(v, ("scrw", q - 1, NT - 2 + tc))
                                    srcW = psA[:, slot * 512:(slot + 1) * 512] if slot < 4 else psAV[slot - 4][:]
                                    if q == 0:
                                        if dcol == 0:
                                            wait(v, ("xw", tc))
                                        i = v.tensor_tensor(out=stage[tc % 2][:, dcol * 512:(dcol + 1) * 512],
                                                            in0=srcW,
                                                            in1=xw[:, dcol * 512:(dcol + 1) * 512],
                                                            op=ALU.add)
                                    else:
                                        i = v.tensor_copy(stage[tc % 2][:, dcol * 512:(dcol + 1) * 512],
                                                          srcW)
                                    VE.inc(i, 1)
                                    mark(("stgevac", q, tc, dcol), VE)
                                mark(("stg", q, tc), VE)

                @block.scalar
                def _(s):
                    for h in range(H):
                        hp, hh = h // 2, h % HPQ
                        if h % 2 == 0:
                            if h > 0:
                                wait(s, ("av", h - 1, NT - 1))
                            for nb in range(4):
                                wait(s, ("qrpsb", hp, nb))
                                s.activation(probs[:, nb * 512:(nb + 1) * 512],
                                             psA[:, nb * 512:(nb + 1) * 512], AF.Copy)
                                i = s.drain(fusable=False)
                                SE.inc(i, 1)
                                mark(("qevacb", hp, nb), SE)
                            mark(("qevac", hp), SE)
                        for nb in range(4):
                            wait(s, ("qnpsb", h, nb))
                            s.activation(qn[:, nb * 512:(nb + 1) * 512],
                                         psA[:, nb * 512:(nb + 1) * 512], AF.Copy, scale=SCALE)
                            i = s.drain(fusable=False)
                            SE.inc(i, 1)
                            mark(("qnevacb", h, nb), SE)
                        mark(("qnevac", h), SE)
                        for nb in range(4):
                            wait(s, ("knpsb", h, nb))
                            s.activation(kns[:, nb * 512:(nb + 1) * 512],
                                         psA[:, nb * 512:(nb + 1) * 512], AF.Copy)
                            i = s.drain(fusable=False)
                            SE.inc(i, 1)
                            mark(("knevacb", h, nb), SE)
                        mark(("knevac", h), SE)
                        if h == 0:
                            wait(s, ("comb", NT - 1))
                        for g4 in range(4):
                            wait(s, ("vps", h, g4))
                            s.activation(vs[:, g4 * 512:(g4 + 1) * 512],
                                         psA[:, g4 * 512:(g4 + 1) * 512], AF.Copy)
                            i = s.drain(fusable=False)
                            SE.inc(i, 1)
                            mark(("vevac", h, g4), SE)
                        for qt in range(NT):
                            kv = (qt + 1) * 128
                            base = (qt % 2) * 1024 if qt <= 7 else 0
                            nkb = (kv + 511) // 512
                            wait(s, ("smax", h, qt))
                            if qt >= 2:
                                wait(s, ("trp", h, qt - 2, (qt - 2) // 8))
                            elif h >= 1:
                                pq = NT - 2 + qt
                                wait(s, ("trp", h - 1, pq, pq // 8))
                            for kb in range(nkb):
                                wdt = min(512, kv - kb * 512)
                                i = s.activation(probs2[qt % 2][:, kb * 512: kb * 512 + wdt],
                                                 psA[:, base + kb * 512: base + kb * 512 + wdt], AF.Exp,
                                                 bias=colD[:], accum_out=colB4[:, kb:kb + 1])
                                SE.inc(i, 1)
                                mark(("expb", h, qt, kb), SE)
                            s.drain()
                            s.activation(colT[:, 0:nkb], colB4[:, 0:nkb], AF.Copy, accum_out=colB[:])
                            i = s.drain(fusable=False)
                            SE.inc(i, 1)
                            mark(("exp", h, qt), SE)
                            def s_trpevac(j_qt):
                                nbat2 = j_qt // 8 + 1
                                for b in range(nbat2):
                                    pt = psT0 if b % 2 == 0 else psT1
                                    wait(s, ("trp", h, j_qt, b))
                                    wdt2 = min(8, j_qt + 1 - b * 8) * 128
                                    i2 = s.activation(probsT2[j_qt % 4][:, b * 1024: b * 1024 + wdt2],
                                                      pt[:, 0:wdt2], AF.Copy)
                                    SE.inc(i2, 1)
                                    mark(("trpevac", h, j_qt, b), SE)
                            def s_avevac(j_qt):
                                wait(s, ("av", h, j_qt))
                                i2 = s.activation(oT[:, hh * 2048 + j_qt * 128: hh * 2048 + (j_qt + 1) * 128],
                                                  psAV[j_qt % 2][:, 0:128], AF.Copy)
                                SE.inc(i2, 1)
                                mark(("avevac", h, j_qt), SE)
                            if qt >= 1:
                                s_trpevac(qt - 1)
                            if qt >= 3:
                                s_avevac(qt - 3)
                            if qt == NT - 1:
                                s_trpevac(NT - 1)
                                s_avevac(NT - 3)
                                s_avevac(NT - 2)
                                s_avevac(NT - 1)


        with ExitStack() as P:
            EP = P.enter_context
            acc = EP(nc.sbuf_tensor("acc", [128, NT * 2048], BF16))

            with ExitStack() as P5:
                E5 = P5.enter_context
                hT = E5(nc.sbuf_tensor("hT", [128, NFC * 2048], BF16))
                gb = [E5(nc.sbuf_tensor(f"gb{i}", [128, 2048], BF16)) for i in range(2)]
                ub = [E5(nc.sbuf_tensor(f"ub{i}", [128, 2048], BF16)) for i in range(2)]
                db = E5(nc.sbuf_tensor("db", [128, NFC * 2048], BF16))
                sgt = [E5(nc.sbuf_tensor(f"sgt{i}", [128, 512], F32)) for i in range(2)]
                psG = [E5(nc.psum_tensor(f"psG{i}", [128, 512], F32)) for i in range(2)]
                psU = [E5(nc.psum_tensor(f"psU{i}", [128, 512], F32)) for i in range(2)]
                psD = [E5(nc.psum_tensor(f"psD{i}", [128, 512], F32)) for i in range(2)]

                psG_last = {0: None, 1: None}
                psU_last = {0: None, 1: None}
                psD_last = {0: None, 1: None}
                gslot_last = {0: None, 1: None}
                uslot_last = {0: None, 1: None}

                @block.gpsimd
                def _(g):
                    wait(g, ("woMM", NQ - 1))
                    wait(g, ("scrw", NQ - 1, NT - 1))
                    wait(g, ("scrw", NQ - 1, NT - 2))
                    for e in range(NEXP):
                        for fc in range(NFC):
                            if fc == 2:
                                if e >= 1:
                                    wait(g, ("dwMM", e - 1))
                                i = g.dma_start(db[:].rearrange("p (fc d) -> p fc d", d=2048),
                                                dw[e].rearrange("(fc p) d -> p fc d", p=128))
                                sems["dDW"].inc(i, 16)
                                mark(("dwl", e), sems["dDW"])
                            sl = fc % 2
                            if gslot_last[sl] is not None:
                                wait(g, gslot_last[sl])
                            i = g.dma_start(gb[sl][:].rearrange("p (dc f) -> p dc f", f=128),
                                            gw[e].rearrange("(dc p) f -> p dc f", p=128)[:, :, fc * 128:(fc + 1) * 128])
                            sems[f"dG{sl}"].inc(i, 16)
                            mark(("gl", e, fc), sems[f"dG{sl}"])
                            i = g.dma_start(ub[sl][:].rearrange("p (dc f) -> p dc f", f=128),
                                            uw[e].rearrange("(dc p) f -> p dc f", p=128)[:, :, fc * 128:(fc + 1) * 128])
                            sems[f"dU{sl}"].inc(i, 16)
                            mark(("ul", e, fc), sems[f"dU{sl}"])
                            gslot_last[sl] = ("gMM", e, fc)
                            uslot_last[sl] = ("gMM", e, fc)

                @block.tensor
                def _(t):
                    wait(t, ("stgevac", NQ - 1, NT - 1, 3))
                    wait(t, ("scrw", NQ - 1, NT - 1))
                    wait(t, ("scrw", NQ - 1, NT - 2))
                    for e in range(NEXP):
                        for fc in range(NFC):
                            sl = fc % 2
                            wait(t, ("gl", e, fc))
                            wait(t, ("ul", e, fc))
                            for tb in range(4):
                                k = fc * 4 + tb
                                ps_g, ps_u = psG[k % 2], psU[k % 2]
                                if psG_last[k % 2] is not None:
                                    wait(t, psG_last[k % 2])
                                for dc in range(ND):
                                    i = t.matmul(ps_g[:], gb[sl][:, dc * 128:(dc + 1) * 128],
                                                 xnt(dc)[:, tb * 512:(tb + 1) * 512],
                                                 start=(dc == 0), stop=(dc == ND - 1))
                                PE.inc(i, 1)
                                mark(("gps", e, fc, tb), PE)
                                for dc in range(ND):
                                    i = t.matmul(ps_u[:], ub[sl][:, dc * 128:(dc + 1) * 128],
                                                 xnt(dc)[:, tb * 512:(tb + 1) * 512],
                                                 start=(dc == 0), stop=(dc == ND - 1))
                                PE.inc(i, 1)
                                mark(("ups", e, fc, tb), PE)
                                psG_last[k % 2] = ("hT", e, fc, tb)
                                psU_last[k % 2] = ("hT", e, fc, tb)
                            mark(("gMM", e, fc), PE)
                        wait(t, ("dwl", e))
                        wait(t, ("hT", e, NFC - 1, 3))
                        for tc in range(NT):
                            for dcol in range(4):
                                k = tc * 4 + dcol
                                ps_d = psD[k % 2]
                                if psD_last[k % 2] is not None:
                                    wait(t, psD_last[k % 2])
                                for fc in range(NFC):
                                    i = t.matmul(ps_d[:],
                                                 hT[:, fc * 2048 + tc * 128: fc * 2048 + (tc + 1) * 128],
                                                 db[:, fc * 2048 + dcol * 512: fc * 2048 + (dcol + 1) * 512],
                                                 start=(fc == 0), stop=(fc == NFC - 1))
                                PE.inc(i, 1)
                                mark(("dps", e, tc, dcol), PE)
                                psD_last[k % 2] = ("accw", e, tc, dcol)
                        mark(("dwMM", e), PE)

                @block.scalar
                def _(s):
                    for e in range(NEXP):
                        for fc in range(NFC):
                            for tb in range(4):
                                k = fc * 4 + tb
                                wait(s, ("gps", e, fc, tb))
                                if (e, fc, tb) != (0, 0, 0):
                                    pass
                                prev = (e, fc, tb - 2) if tb >= 2 else None
                                if k >= 2 or e > 0:
                                    pk = k - 2
                                    pe_, pfc, ptb = e, pk // 4, pk % 4
                                    if pk < 0:
                                        pe_, pfc, ptb = e - 1, (pk + 24) // 4, (pk + 24) % 4
                                    wait(s, ("hT", pe_, pfc, ptb))
                                i = s.activation(sgt[k % 2][:], psG[k % 2][:], AF.Silu)
                                s.drain()
                                SE.inc(i, 1)
                                mark(("silu", e, fc, tb), SE)

                @block.vector
                def _(v):
                    for e in range(NEXP):
                        for fc in range(NFC):
                            for tb in range(4):
                                k = fc * 4 + tb
                                wait(v, ("silu", e, fc, tb))
                                wait(v, ("ups", e, fc, tb))
                                i = v.tensor_tensor(out=hT[:, fc * 2048 + tb * 512: fc * 2048 + (tb + 1) * 512],
                                                    in0=sgt[k % 2][:], in1=psU[k % 2][:], op=ALU.mult)
                                VE.inc(i, 1)
                                mark(("hT", e, fc, tb), VE)
                        for tc in range(NT):
                            for dcol in range(4):
                                k = tc * 4 + dcol
                                wait(v, ("dps", e, tc, dcol))
                                cslice = comb[:, tc * NEXP + e: tc * NEXP + e + 1]
                                dst = acc[:, tc * 2048 + dcol * 512: tc * 2048 + (dcol + 1) * 512]
                                if e == 0:
                                    i = v.tensor_scalar(out=dst, in0=psD[k % 2][:], scalar1=cslice,
                                                        scalar2=None, op0=ALU.mult)
                                else:
                                    i = v.scalar_tensor_tensor(out=dst, in0=psD[k % 2][:], scalar=cslice,
                                                               in1=dst, op0=ALU.mult, op1=ALU.add)
                                VE.inc(i, 1)
                                mark(("accw", e, tc, dcol), VE)

            with ExitStack() as P6:
                E6 = P6.enter_context
                rb = [E6(nc.sbuf_tensor(f"rb{i}", [128, 2048], F32)) for i in range(4)]
                ost = [E6(nc.sbuf_tensor(f"ost{i}", [128, 2048], F32)) for i in range(2)]

                @block.sync
                def _(sy):
                    wait(sy, ("scrw", NQ - 1, NT - 1))
                    wait(sy, ("scrw", NQ - 1, NT - 2))
                    wait(sy, ("dwMM", NEXP - 1))
                    wait(sy, ("accw", NEXP - 1, NT - 1, 3))
                    for tc in range(NT):
                        if tc >= 1:
                            wait(sy, ("madd", tc - 1, 1))
                            i = sy.dma_start(out[(tc - 1) * 128:tc * 128, :], ost[(tc - 1) % 2][:])
                            sems[f"dOUT{(tc - 1) % 2}"].inc(i, 16)
                            mark(("outw", tc - 1), sems[f"dOUT{(tc - 1) % 2}"])
                        if tc >= 2:
                            wait(sy, ("madd", tc - 2, 0))
                        i = sy.dma_start(rb[2 * (tc % 2)][:], scr[0][tc * 128:(tc + 1) * 128, :])
                        sems[f"dMRG{tc % 2}"].inc(i, 16)
                        mark(("mld", tc, 0), sems[f"dMRG{tc % 2}"])
                    wait(sy, ("madd", NT - 1, 1))
                    i = sy.dma_start(out[(NT - 1) * 128:NT * 128, :], ost[(NT - 1) % 2][:])
                    sems[f"dOUT{(NT - 1) % 2}"].inc(i, 16)
                    mark(("outw", NT - 1), sems[f"dOUT{(NT - 1) % 2}"])
                    sy.wait_ge(sems["dOUT0"].h, sems["dOUT0"].n)
                    sy.wait_ge(sems["dOUT1"].h, sems["dOUT1"].n)

                @block.gpsimd
                def _(g):
                    wait(g, ("scrw", NQ - 1, NT - 1))
                    wait(g, ("scrw", NQ - 1, NT - 2))
                    wait(g, ("dwMM", NEXP - 1))
                    wait(g, ("accw", NEXP - 1, NT - 1, 3))
                    for tc in range(NT):
                        if tc >= 2:
                            wait(g, ("madd", tc - 2, 0))
                        i = g.dma_start(rb[2 * (tc % 2) + 1][:], scr[2][tc * 128:(tc + 1) * 128, :])
                        sems[f"dMG{tc % 2}"].inc(i, 16)
                        mark(("gld", tc, 0), sems[f"dMG{tc % 2}"])

                @block.vector
                def _(v):
                    for tc in range(NT):
                        wait(v, ("mld", tc, 0))
                        wait(v, ("gld", tc, 0))
                        if tc >= 2:
                            wait(v, ("outw", tc - 2))
                        v.tensor_tensor(out=ost[tc % 2][:], in0=rb[2 * (tc % 2)][:],
                                        in1=rb[2 * (tc % 2) + 1][:], op=ALU.add)
                        i = v.drain(fusable=False)
                        VE.inc(i, 1)
                        mark(("madd", tc, 0), VE)
                        wait(v, ("accw", NEXP - 1, tc, 3))
                        v.tensor_tensor(out=ost[tc % 2][:], in0=ost[tc % 2][:],
                                        in1=acc[:, tc * 2048:(tc + 1) * 2048], op=ALU.add)
                        i = v.drain(fusable=False)
                        VE.inc(i, 1)
                        mark(("madd", tc, 1), VE)

    return nc, marks


_CACHED = {}


def _get_nc():
    if "nc" not in _CACHED:
        _, marks = build_nc(None)
        nc, _ = build_nc(marks)
        _CACHED["nc"] = nc
    return _CACHED["nc"]


def _host_prep(inputs):
    wq = np.asarray(inputs["L0_attn_wq"], dtype=np.float32)
    wkv_a = np.asarray(inputs["L0_attn_wkv_a"], dtype=np.float32)
    nope_cols = np.concatenate([np.arange(h * 192, h * 192 + 128) for h in range(H)])
    wqn_ = np.ascontiguousarray(wq[:, nope_cols])
    rope_cols = []
    for hp in range(H // 2):
        b0 = (2 * hp) * 192 + 128
        b1 = (2 * hp + 1) * 192 + 128
        rope_cols += [b0 + 2 * i for i in range(32)]
        rope_cols += [b1 + 2 * i for i in range(32)]
        rope_cols += [b0 + 2 * i + 1 for i in range(32)]
        rope_cols += [b1 + 2 * i + 1 for i in range(32)]
    wqr_ = np.ascontiguousarray(wq[:, rope_cols])
    kpe_cols = [LORA + 2 * i for i in range(32)] + [LORA + 2 * i + 1 for i in range(32)]
    wkva_ = np.ascontiguousarray(
        np.concatenate([wkv_a[:, :LORA], wkv_a[:, kpe_cols]], axis=1))
    e = np.arange(32, dtype=np.float32) * np.float32(2.0 / 64.0)
    invf_ = (np.float32(1.0) / np.power(np.float32(10000.0), e, dtype=np.float32)
             ).astype(np.float32).reshape(32, 1)
    return wqn_, wqr_, wkva_, invf_


def kernel(L0_attn_x, L0_attn_wq, L0_attn_wkv_a, L0_attn_wkv_b, L0_attn_wo,
           L0_ffn_router_w, L0_ffn_gate_w, L0_ffn_up_w, L0_ffn_down_w):
    x = np.ascontiguousarray(np.asarray(L0_attn_x, dtype=np.float32))
    wqn_, wqr_, wkva_, invf_ = _host_prep({
        "L0_attn_wq": L0_attn_wq, "L0_attn_wkv_a": L0_attn_wkv_a})
    wkvb_ = np.ascontiguousarray(np.asarray(L0_attn_wkv_b, dtype=np.float32))
    wo_ = np.ascontiguousarray(np.asarray(L0_attn_wo, dtype=np.float32))
    rw_ = np.ascontiguousarray(np.asarray(L0_ffn_router_w, dtype=np.float32))
    gw_ = np.ascontiguousarray(np.asarray(L0_ffn_gate_w, dtype=np.float32))
    uw_ = np.ascontiguousarray(np.asarray(L0_ffn_up_w, dtype=np.float32))
    dw_ = np.ascontiguousarray(np.asarray(L0_ffn_down_w, dtype=np.float32))

    nc = _get_nc()
    in_maps = []
    for core in range(8):
        in_maps.append({
            "x": np.ascontiguousarray(x[core]),
            "wqn": wqn_, "wqr": wqr_, "wkva": wkva_, "wkvb": wkvb_,
            "wo": wo_, "rw": rw_, "gw": gw_, "uw": uw_, "dw": dw_,
            "invf": invf_,
        })
    res = run_bass_kernel_spmd(nc, in_maps, core_ids=list(range(8)))
    return np.stack([np.asarray(r["out"]) for r in res.results], axis=0)


# version 29 (final)
# speedup vs baseline: 1.2045x; 1.0515x over previous
import numpy as np
from contextlib import ExitStack

import concourse.bass as bass
import concourse.mybir as mybir
from concourse.bass_utils import run_bass_kernel_spmd

F32 = mybir.dt.float32
BF16 = mybir.dt.bfloat16
I32 = mybir.dt.int32
AF = mybir.ActivationFunctionType
ALU = mybir.AluOpType
AX = mybir.AxisListType

S = 2048
D = 2048
H = 16
NOPE = 128
ROPE = 64
VD = 128
LORA = 512
NEXP = 16
FF = 768
NT = S // 128
ND = D // 128
NFC = FF // 128
NQ = 4
HPQ = H // NQ
SCALE = float(1.0 / np.sqrt(NOPE + ROPE))
MAGIC = 12582912.0
TWO_PI = 6.283185307179586
CW1 = 6.28125
CW2 = float(np.float32(TWO_PI - CW1))
CW3 = float(TWO_PI - CW1 - np.float64(np.float32(TWO_PI - CW1)))
NEG_BIG = -1e30
LN_EPS = 1e-5


class DSem:
    def __init__(self, nc, stack, name):
        self.h = stack.enter_context(nc.semaphore(name))
        self.n = 0
        self.name = name

    def inc(self, instr, k):
        instr.then_inc(self.h, k)
        self.n += k
        return self.n


def build_nc(prev_marks):
    pass2 = prev_marks is not None
    marks = {}
    nc = bass.Bass("TRN2", target_bir_lowering=False)

    x = nc.dram_tensor("x", [S, D], F32, kind="ExternalInput").ap()
    wqn = nc.dram_tensor("wqn", [D, H * NOPE], F32, kind="ExternalInput").ap()
    wqr = nc.dram_tensor("wqr", [D, (H // 2) * 128], F32, kind="ExternalInput").ap()
    wkva = nc.dram_tensor("wkva", [D, LORA + ROPE], F32, kind="ExternalInput").ap()
    wkvb = nc.dram_tensor("wkvb", [LORA, H * 256], F32, kind="ExternalInput").ap()
    wo = nc.dram_tensor("wo", [H * VD, D], F32, kind="ExternalInput").ap()
    rw = nc.dram_tensor("rw", [D, NEXP], F32, kind="ExternalInput").ap()
    gw = nc.dram_tensor("gw", [NEXP, D, FF], F32, kind="ExternalInput").ap()
    uw = nc.dram_tensor("uw", [NEXP, D, FF], F32, kind="ExternalInput").ap()
    dw = nc.dram_tensor("dw", [NEXP, FF, D], F32, kind="ExternalInput").ap()
    invf = nc.dram_tensor("invf", [32, 1], F32, kind="ExternalInput").ap()
    out = nc.dram_tensor("out", [S, D], F32, kind="ExternalOutput").ap()
    scr = [nc.dram_tensor(f"scr{q}", [S, D], F32).ap() for q in range(NQ)]

    sem_by_name = {}

    def mark(key, sem):
        assert key not in marks, key
        marks[key] = (sem.name, sem.n)

    def wait(eng, key):
        if pass2:
            name, v = prev_marks[key]
            eng.wait_ge(sem_by_name[name], v)
        else:
            assert key not in marks or True
            eng.wait_ge(sem_by_name["PE"], 0)

    with ExitStack() as G:
        EG = G.enter_context
        sems = {}
        for nm in ["PE", "VE", "SE", "PO",
                   "dX0", "dX1", "dWQN", "dWQR", "dWKB", "dWO", "dKVA", "dRW", "dIF",
                   "dG0", "dG1", "dU0", "dU1", "dDW", "dSCR0", "dSCR1", "dOUT0", "dOUT1",
                   "dMISC", "dMRG0", "dMRG1", "dMG0", "dMG1", "dSCRA0", "dSCRA1", "dQR", "dQRS", "dXLD0", "dXLD1"]:
            sems[nm] = DSem(nc, G, nm)
            sem_by_name[nm] = sems[nm].h
        PE, VE, SE, PO = sems["PE"], sems["VE"], sems["SE"], sems["PO"]

        xnT = EG(nc.sbuf_tensor("xnT", [128, ND * 2048], BF16))
        comb = EG(nc.sbuf_tensor("comb", [128, NT * NEXP], F32))
        ident = EG(nc.sbuf_tensor("ident", [128, 128], BF16))
        identNeg = EG(nc.sbuf_tensor("identNeg", [128, 128], BF16))
        colA = EG(nc.sbuf_tensor("colA", [128, 1], F32))
        colB = EG(nc.sbuf_tensor("colB", [128, 1], F32))
        colC = EG(nc.sbuf_tensor("colC", [128, 1], F32))
        colD = EG(nc.sbuf_tensor("colD", [128, 1], F32))
        colE = EG(nc.sbuf_tensor("colE", [128, 1], F32))
        block = EG(nc.Block())

        def xnt(dc):
            return xnT[:, dc * 2048:(dc + 1) * 2048]

        with ExitStack() as M:
            EM = M.enter_context
            cosr = EM(nc.sbuf_tensor("cosr", [64, 2048], BF16))
            sinr = EM(nc.sbuf_tensor("sinr", [64, 2048], BF16))
            cT = EM(nc.sbuf_tensor("cT", [128, 4 * 2048], BF16))
            krope = EM(nc.sbuf_tensor("krope", [128, 2048], BF16))
            wqnb = EM(nc.sbuf_tensor("wqnb", [128, 2048], BF16))
            wqrb = EM(nc.sbuf_tensor("wqrb", [128, 2048], BF16))
            knb = EM(nc.sbuf_tensor("knb", [128, 512], BF16))
            vwb = EM(nc.sbuf_tensor("vwb", [128, 512], BF16))

            with ExitStack() as P:
                EP = P.enter_context
                xbuf = [EP(nc.sbuf_tensor(f"xbuf{i}", [128, 2048], F32)) for i in range(2)]
                trash = EP(nc.sbuf_tensor("trash", [128, 2048], BF16))
                xnb = EP(nc.sbuf_tensor("xnb", [128, 2048], BF16))
                tidx = EP(nc.sbuf_tensor("tidx", [32, 2048], F32))
                ivf = EP(nc.sbuf_tensor("ivf", [32, 1], F32))
                frq = EP(nc.sbuf_tensor("frq", [32, 2048], F32))
                kfb = EP(nc.sbuf_tensor("kfb", [32, 2048], F32))
                lnc = EP(nc.sbuf_tensor("lnc", [128, 8], F32))
                psT0 = EP(nc.psum_tensor("psT0", [128, 1024], BF16))
                psT1 = EP(nc.psum_tensor("psT1", [128, 1024], BF16))

                @block.sync
                def _(sy):
                    i = sy.dma_start(ivf[:], invf)
                    sems["dIF"].inc(i, 16)
                    mark(("ivf",), sems["dIF"])
                    for tc in range(NT):
                        if tc >= 2:
                            wait(sy, ("lnused", tc - 2))
                        i = sy.dma_start(xbuf[tc % 2][:], x[tc * 128:(tc + 1) * 128, :])
                        sems[f"dX{tc % 2}"].inc(i, 16)
                        mark(("xload", tc), sems[f"dX{tc % 2}"])

                @block.gpsimd
                def _(g):
                    g.memset(krope[64:128, :], 0.0)
                    g.drain()
                    g.memset(ident[:], 0.0)
                    g.drain()
                    i = g.affine_select(out=ident[:], in_=ident[:], compare_op=ALU.not_equal,
                                        fill=1.0, base=0, pattern=[[-1, 128]], channel_multiplier=1)
                    PO.inc(i, 1)
                    mark(("ident",), PO)
                    g.memset(identNeg[:], 0.0)
                    g.drain()
                    i = g.affine_select(out=identNeg[:], in_=identNeg[:], compare_op=ALU.is_ge,
                                        fill=NEG_BIG, base=0, pattern=[[1, 128]], channel_multiplier=-1)
                    PO.inc(i, 1)
                    mark(("cmask",), PO)
                    i = g.iota(tidx[:], pattern=[[1, 2048]], base=0, channel_multiplier=0,
                               allow_small_or_imprecise_dtypes=True)
                    PO.inc(i, 1)
                    mark(("tidx",), PO)

                @block.vector
                def _(v):
                    wait(v, ("tidx",))
                    wait(v, ("ivf",))
                    v.tensor_scalar(out=frq[:], in0=tidx[:], scalar1=ivf[:], scalar2=None, op0=ALU.mult)
                    v.drain()
                    v.tensor_scalar(out=kfb[:], in0=frq[:], scalar1=float(1.0 / TWO_PI),
                                    scalar2=MAGIC, op0=ALU.mult, op1=ALU.add)
                    v.drain()
                    v.tensor_scalar(out=kfb[:], in0=kfb[:], scalar1=MAGIC, scalar2=None, op0=ALU.subtract)
                    v.drain()
                    v.scalar_tensor_tensor(out=frq[:], in0=kfb[:], scalar=-CW1, in1=frq[:],
                                           op0=ALU.mult, op1=ALU.add)
                    v.drain()
                    v.scalar_tensor_tensor(out=frq[:], in0=kfb[:], scalar=-CW2, in1=frq[:],
                                           op0=ALU.mult, op1=ALU.add)
                    v.drain()
                    v.tensor_scalar(out=frq[:], in0=frq[:], scalar1=float(np.pi), scalar2=None, op0=ALU.min)
                    v.drain()
                    i = v.tensor_scalar(out=frq[:], in0=frq[:], scalar1=float(-np.pi), scalar2=None, op0=ALU.max)
                    v.drain()
                    VE.inc(i, 1)
                    mark(("ropearg",), VE)
                    v.tensor_scalar(out=kfb[:], in0=frq[:], scalar1=-1.0, scalar2=None, op0=ALU.mult)
                    v.drain()
                    v.tensor_tensor(out=kfb[:], in0=kfb[:], in1=frq[:], op=ALU.max)
                    v.drain()
                    i = v.tensor_scalar(out=kfb[:], in0=kfb[:], scalar1=-1.0, scalar2=float(np.pi / 2),
                                        op0=ALU.mult, op1=ALU.add)
                    v.drain()
                    VE.inc(i, 1)
                    mark(("cosarg",), VE)
                    def emit_xnb(j):
                        wait(v, ("rstd", j))
                        v.tensor_scalar(out=lnc[:, j % 2:j % 2 + 1], in0=lnc[:, j % 2:j % 2 + 1],
                                        scalar1=-1.0, scalar2=None, op0=ALU.mult)
                        v.drain()
                        i = v.tensor_scalar(out=xnb[:], in0=xbuf[j % 2][:],
                                            scalar1=lnc[:, j % 2:j % 2 + 1],
                                            scalar2=lnc[:, 4 + j % 2:5 + j % 2],
                                            op0=ALU.add, op1=ALU.mult)
                        v.drain()
                        VE.inc(i, 1)
                        mark(("xn", j), VE)
                        mark(("lnused", j), VE)

                    def emit_evac(j):
                        for half in range(2):
                            pt = psT0 if half == 0 else psT1
                            wait(v, ("xpose", j, half))
                            dst = xnT[:, half * 8 * 2048: half * 8 * 2048 + 8 * 2048].rearrange(
                                "p (dc c) -> p dc c", c=2048)[:, :, j * 128:j * 128 + 128]
                            src = pt[:].rearrange("p (j c) -> p j c", c=128)
                            i = v.tensor_copy(dst, src)
                            VE.inc(i, 1)
                            mark(("xnTevac", j, half), VE)

                    for tc in range(NT):
                        wait(v, ("xload", tc))
                        xb = xbuf[tc % 2]
                        v.reduce_sum(colA[:], xb[:], axis=AX.X)
                        v.drain()
                        v.tensor_scalar(out=lnc[:, tc % 2:tc % 2 + 1], in0=colA[:],
                                        scalar1=float(1.0 / D), scalar2=None, op0=ALU.mult)
                        v.drain()
                        wait(v, ("sq", tc))
                        v.tensor_scalar(out=lnc[:, 2 + tc % 2:3 + tc % 2], in0=colB[:],
                                        scalar1=float(1.0 / D), scalar2=None, op0=ALU.mult)
                        v.tensor_tensor(out=colA[:], in0=lnc[:, tc % 2:tc % 2 + 1],
                                        in1=lnc[:, tc % 2:tc % 2 + 1], op=ALU.mult)
                        v.drain()
                        v.tensor_tensor(out=lnc[:, 2 + tc % 2:3 + tc % 2],
                                        in0=lnc[:, 2 + tc % 2:3 + tc % 2], in1=colA[:], op=ALU.subtract)
                        v.drain()
                        i = v.tensor_scalar(out=lnc[:, 2 + tc % 2:3 + tc % 2],
                                            in0=lnc[:, 2 + tc % 2:3 + tc % 2], scalar1=LN_EPS,
                                            scalar2=None, op0=ALU.add)
                        v.drain()
                        VE.inc(i, 1)
                        mark(("var", tc), VE)
                        if tc >= 1:
                            emit_xnb(tc - 1)
                            emit_evac(tc - 1)
                    emit_xnb(NT - 1)
                    emit_evac(NT - 1)

                @block.scalar
                def _(s):
                    wait(s, ("ropearg",))
                    wait(s, ("cosarg",))
                    s.activation(sinr[0:32, :], frq[:], AF.Sin)
                    i = s.activation(cosr[0:32, :], kfb[:], AF.Sin)
                    SE.inc(i, 1)
                    mark(("cossin",), SE)
                    for tc in range(NT):
                        wait(s, ("xload", tc))
                        xb = xbuf[tc % 2]
                        i = s.activation(trash[:], xb[:], AF.Square, accum_out=colB[:])
                        s.drain()
                        SE.inc(i, 1)
                        mark(("sq", tc), SE)
                        wait(s, ("var", tc))
                        s.activation(lnc[:, 4 + tc % 2:5 + tc % 2], lnc[:, 2 + tc % 2:3 + tc % 2], AF.Ln)
                        s.drain()
                        i = s.activation(lnc[:, 4 + tc % 2:5 + tc % 2],
                                         lnc[:, 4 + tc % 2:5 + tc % 2], AF.Exp, scale=-0.5)
                        s.drain()
                        SE.inc(i, 1)
                        mark(("rstd", tc), SE)


                @block.gpsimd
                def _(g):
                    wait(g, ("cossin",))
                    g.dma_start(cosr[32:64, :], cosr[0:32, :]).then_inc(sems["dMISC"].h, 16)
                    sems["dMISC"].n += 16
                    i = g.dma_start(sinr[32:64, :], sinr[0:32, :])
                    sems["dMISC"].inc(i, 16)
                    mark(("cossinrep",), sems["dMISC"])

                @block.tensor
                def _(t):
                    wait(t, ("ident",))
                    for tc in range(NT):
                        wait(t, ("xn", tc))
                        for half in range(2):
                            pt = psT0 if half == 0 else psT1
                            if tc >= 1:
                                wait(t, ("xnTevac", tc - 1, half))
                            for j in range(8):
                                dc = half * 8 + j
                                i = t.transpose(pt[:, j * 128:j * 128 + 128],
                                                xnb[:, dc * 128:(dc + 1) * 128], ident[:])
                            PE.inc(i, 1)
                            mark(("xpose", tc, half), PE)

            with ExitStack() as P:
                EP = P.enter_context
                kvab = EP(nc.sbuf_tensor("kvab", [128, ND * 576], BF16))
                kpe_bf = EP(nc.sbuf_tensor("kpe_bf", [64, 2048], BF16))
                kpeO = EP(nc.sbuf_tensor("kpeO", [32, 2048], BF16))
                krO32 = EP(nc.sbuf_tensor("krO32", [32, 2048], BF16))
                r1 = EP(nc.sbuf_tensor("r1", [32, 2048], BF16))
                r2 = EP(nc.sbuf_tensor("r2", [32, 2048], BF16))
                rwb = EP(nc.sbuf_tensor("rwb", [128, ND * NEXP], BF16))
                eq1 = EP(nc.sbuf_tensor("eq1", [128, NEXP], I32))
                eq2 = EP(nc.sbuf_tensor("eq2", [128, NEXP], I32))
                rprob = EP(nc.sbuf_tensor("rprob", [128, NEXP], F32))
                rp2 = EP(nc.sbuf_tensor("rp2", [128, NEXP], F32))
                rtmp = EP(nc.sbuf_tensor("rtmp", [128, NEXP], F32))
                rneg = EP(nc.sbuf_tensor("rneg", [128, NEXP], F32))
                psDummy = EP(nc.psum_tensor("psDummy", [128, 2048], F32))
                psP = [EP(nc.psum_tensor(f"psP{i}", [128, 512], F32)) for i in range(4)]

                @block.gpsimd
                def _(g):
                    wait(g, ("xnTevac", NT - 1, 1))
                    wait(g, ("cossin",))
                    i = g.dma_start(kvab[:].rearrange("p (dc m) -> p dc m", m=576),
                                    wkva.rearrange("(dc p) m -> p dc m", p=128))
                    sems["dKVA"].inc(i, 16)
                    mark(("kva",), sems["dKVA"])
                    i = g.dma_start(rwb[:].rearrange("p (dc e) -> p dc e", e=NEXP),
                                    rw.rearrange("(dc p) e -> p dc e", p=128))
                    sems["dRW"].inc(i, 16)
                    mark(("rwl",), sems["dRW"])
                    i = g.dma_start(wqnb[:].rearrange("p (dc c) -> p dc c", c=NOPE),
                                    wqn.rearrange("(dc p) c -> p dc c", p=128)[:, :, 0:NOPE])
                    sems["dWQN"].inc(i, 16)
                    mark(("wqnl", 0), sems["dWQN"])
                    i = g.dma_start(wqrb[:].rearrange("p (dc c) -> p dc c", c=128),
                                    wqr.rearrange("(dc p) c -> p dc c", p=128)[:, :, 0:128])
                    sems["dWQR"].inc(i, 16)
                    mark(("wqrl", 0), sems["dWQR"])
                    i = g.dma_start(knb[:].rearrange("p (lc c) -> p lc c", c=128),
                                    wkvb.rearrange("(lc p) c -> p lc c", p=128)[:, :, 0:128])
                    sems["dWKB"].inc(i, 16)
                    i = g.dma_start(vwb[:].rearrange("p (lc c) -> p lc c", c=128),
                                    wkvb.rearrange("(lc p) c -> p lc c", p=128)[:, :, 128:256])
                    sems["dWKB"].inc(i, 16)
                    mark(("wkbl", 0), sems["dWKB"])
                    wait(g, ("ckvevac", 19))
                    i = g.dma_start(kpeO[:], kpe_bf[32:64, :])
                    sems["dQRS"].inc(i, 16)
                    mark(("kpeO",), sems["dQRS"])
                    wait(g, ("kropePart",))
                    i = g.dma_start(krope[32:64, :], krO32[:])
                    sems["dQRS"].inc(i, 16)
                    mark(("krope",), sems["dQRS"])

                @block.tensor
                def _(t):
                    wait(t, ("kva",))
                    wait(t, ("rwl",))
                    for nb in range(4):
                        for tc in range(nb * 4, nb * 4 + 4):
                            ps = psP[tc % 4]
                            wait(t, ("xnTevac", tc, 1))
                            if tc >= 4:
                                wait(t, ("rexp", tc - 4))
                            for dc in range(ND):
                                i = t.matmul(ps[:, 0:NEXP],
                                             xnt(dc)[:, tc * 128:(tc + 1) * 128],
                                             rwb[:, dc * NEXP:(dc + 1) * NEXP],
                                             start=(dc == 0), stop=(dc == ND - 1))
                            PE.inc(i, 1)
                            mark(("rscore", tc), PE)
                        for mc in range(5):
                            k = nb * 5 + mc
                            ps = psDummy[:, (k % 4) * 512:(k % 4 + 1) * 512]
                            if k >= 4:
                                wait(t, ("ckvevac", k - 4))
                            Mrows = 128 if mc < 4 else 64
                            for dc in range(ND):
                                i = t.matmul(ps[0:Mrows],
                                             kvab[:, dc * 576 + mc * 128: dc * 576 + mc * 128 + Mrows],
                                             xnt(dc)[:, nb * 512:(nb + 1) * 512],
                                             start=(dc == 0), stop=(dc == ND - 1))
                            PE.inc(i, 1)
                            mark(("ckv", k), PE)

                @block.vector
                def _(v):
                    def emit_ckvevac(k):
                        nb2, mc2 = divmod(k, 5)
                        ps2 = psDummy[:, (k % 4) * 512:(k % 4 + 1) * 512]
                        wait(v, ("ckv", k))
                        if mc2 < 4:
                            i2 = v.tensor_copy(cT[:, mc2 * 2048 + nb2 * 512: mc2 * 2048 + (nb2 + 1) * 512], ps2)
                        else:
                            i2 = v.tensor_copy(kpe_bf[:, nb2 * 512:(nb2 + 1) * 512], ps2[0:64])
                        VE.inc(i2, 1)
                        mark(("ckvevac", k), VE)

                    wait(v, ("cossin",))
                    v.memset(rneg[:], NEG_BIG)
                    for tc in range(NT):
                        wait(v, ("rscore", tc))
                        ps = psP[tc % 4]
                        i = v.reduce_max(colA[:], ps[:, 0:NEXP], axis=AX.X, negate=True)
                        v.drain()
                        VE.inc(i, 1)
                        mark(("rmax", tc), VE)
                        wait(v, ("rexps", tc))
                        v.reduce_max(colA[:], rprob[:], axis=AX.X)
                        v.drain()
                        v.tensor_scalar(out=eq1[:], in0=rprob[:], scalar1=colA[:], scalar2=None, op0=ALU.is_equal)
                        v.tensor_copy(rp2[:], rprob[:])
                        v.drain()
                        v.copy_predicated(out=rp2[:], mask=eq1[:], data=rneg[:])
                        v.drain()
                        v.reduce_max(colB[:], rp2[:], axis=AX.X)
                        v.drain()
                        v.tensor_scalar(out=eq2[:], in0=rp2[:], scalar1=colB[:], scalar2=None, op0=ALU.is_equal)
                        v.drain()
                        v.tensor_tensor(out=colC[:], in0=colA[:], in1=colB[:], op=ALU.add)
                        v.drain()
                        v.reciprocal(colC[:], colC[:])
                        v.drain()
                        v.tensor_tensor(out=colD[:], in0=colA[:], in1=colC[:], op=ALU.mult)
                        v.tensor_tensor(out=colE[:], in0=colB[:], in1=colC[:], op=ALU.mult)
                        v.drain()
                        v.tensor_scalar(out=rtmp[:], in0=rprob[:], scalar1=0.0, scalar2=colD[:],
                                        op0=ALU.mult, op1=ALU.add)
                        v.tensor_scalar(out=rp2[:], in0=rprob[:], scalar1=0.0, scalar2=colE[:],
                                        op0=ALU.mult, op1=ALU.add)
                        cslice = comb[:, tc * NEXP:(tc + 1) * NEXP]
                        v.memset(cslice, 0.0)
                        v.drain()
                        v.copy_predicated(out=cslice, mask=eq1[:], data=rtmp[:])
                        v.drain()
                        i = v.copy_predicated(out=cslice, mask=eq2[:], data=rp2[:])
                        v.drain()
                        VE.inc(i, 1)
                        mark(("comb", tc), VE)
                        mark(("rexp", tc), VE)
                        done = (tc * 20) // NT
                        upto = ((tc + 1) * 20) // NT
                        for k2 in range(done, upto):
                            emit_ckvevac(k2)
                    wait(v, ("cossinrep",))
                    wait(v, ("kpeO",))
                    v.tensor_tensor(out=r1[:], in0=kpe_bf[0:32, :], in1=cosr[0:32, :], op=ALU.mult)
                    v.tensor_tensor(out=r2[:], in0=kpeO[:], in1=sinr[0:32, :], op=ALU.mult)
                    v.drain()
                    v.tensor_tensor(out=krope[0:32, :], in0=r1[:], in1=r2[:], op=ALU.subtract)
                    v.drain()
                    v.tensor_tensor(out=r1[:], in0=kpeO[:], in1=cosr[0:32, :], op=ALU.mult)
                    v.tensor_tensor(out=r2[:], in0=kpe_bf[0:32, :], in1=sinr[0:32, :], op=ALU.mult)
                    v.drain()
                    i = v.tensor_tensor(out=krO32[:], in0=r1[:], in1=r2[:], op=ALU.add)
                    v.drain()
                    VE.inc(i, 1)
                    mark(("kropePart",), VE)

                @block.scalar
                def _(s):
                    for tc in range(NT):
                        wait(s, ("rmax", tc))
                        ps = psP[tc % 4]
                        if tc >= 1:
                            wait(s, ("comb", tc - 1))
                        i = s.activation(rprob[:], ps[:, 0:NEXP], AF.Exp, bias=colA[:])
                        s.drain()
                        SE.inc(i, 1)
                        mark(("rexps", tc), SE)

            with ExitStack() as P:
                EP = P.enter_context
                oT = EP(nc.sbuf_tensor("oT", [128, 2 * 2048], BF16))
                woq = EP(nc.sbuf_tensor("woq", [128, 2 * 2048], BF16))
                kns = EP(nc.sbuf_tensor("kns", [128, 2048], BF16))
                vsr = EP(nc.sbuf_tensor("vsr", [128, NT * 132], BF16))
                qn = EP(nc.sbuf_tensor("qn", [128, 2048], BF16))
                qr = EP(nc.sbuf_tensor("qr", [128, 2048], BF16))
                probs = EP(nc.sbuf_tensor("probs", [128, 2048], BF16))
                probsO = EP(nc.sbuf_tensor("probsO", [64, 2048], BF16))
                ropeE = EP(nc.sbuf_tensor("ropeE", [64, 2048], BF16))
                ropeO = EP(nc.sbuf_tensor("ropeO", [64, 2048], BF16))
                o_tok = EP(nc.sbuf_tensor("o_tok", [128, 2048], BF16))
                probsT = EP(nc.sbuf_tensor("probsT", [128, 17408], BF16))
                stage = [EP(nc.sbuf_tensor(f"stage{i}", [128, 2048], F32)) for i in range(2)]
                psA = EP(nc.psum_tensor("psA", [128, 2048], F32))
                psT0 = EP(nc.psum_tensor("psTa", [128, 1024], BF16))
                psT1 = EP(nc.psum_tensor("psTb", [128, 1024], BF16))
                psAV = [EP(nc.psum_tensor(f"psAV{i}", [128, 512], F32)) for i in range(2)]

                BLK = [0] * (NT + 1)
                for jj in range(NT):
                    BLK[jj + 1] = BLK[jj] + (2048 - 128 * jj)

                psW_last = {i: None for i in range(4)}
                av_last = {i: None for i in range(2)}
                psT_last = {0: None, 1: None}

                def av_slot(qt):
                    return psAV[qt % 2][:, 0:132]

                @block.gpsimd
                def _(g):
                    wait(g, ("krope",))
                    g.memset(qr[64:128, :], 0.0)
                    g.drain()
                    for tc in range(NT):
                        i = g.memset(vsr[:, tc * 132 + 128: tc * 132 + 129], 1.0)
                    PO.inc(i, 1)
                    mark(("vones",), PO)
                    for h in range(H):
                        hp = h // 2
                        if h >= 1:
                            wait(g, ("qnMM", h - 1))
                            i = g.dma_start(wqnb[:].rearrange("p (dc c) -> p dc c", c=NOPE),
                                            wqn.rearrange("(dc p) c -> p dc c", p=128)[:, :, h * NOPE:(h + 1) * NOPE])
                            sems["dWQN"].inc(i, 16)
                            mark(("wqnl", h), sems["dWQN"])
                        if h % 2 == 0 and h >= 2:
                            wait(g, ("qrMM", hp - 1))
                            i = g.dma_start(wqrb[:].rearrange("p (dc c) -> p dc c", c=128),
                                            wqr.rearrange("(dc p) c -> p dc c", p=128)[:, :, hp * 128:(hp + 1) * 128])
                            sems["dWQR"].inc(i, 16)
                            mark(("wqrl", hp), sems["dWQR"])
                        if h >= 1:
                            wait(g, ("knvMM", h - 1))
                            i = g.dma_start(knb[:].rearrange("p (lc c) -> p lc c", c=128),
                                            wkvb.rearrange("(lc p) c -> p lc c", p=128)[:, :, h * 256:h * 256 + 128])
                            sems["dWKB"].inc(i, 16)
                            i = g.dma_start(vwb[:].rearrange("p (lc c) -> p lc c", c=128),
                                            wkvb.rearrange("(lc p) c -> p lc c", p=128)[:, :, h * 256 + 128:h * 256 + 256])
                            sems["dWKB"].inc(i, 16)
                            mark(("wkbl", h), sems["dWKB"])
                        if h % 2 == 0:
                            if hp >= 1:
                                wait(g, ("woMM", hp - 1))
                            for hh2 in range(2):
                                i = g.dma_start(woq[:, hh2 * 2048:(hh2 + 1) * 2048],
                                                wo[(2 * hp + hh2) * 128:(2 * hp + hh2) * 128 + 128, :])
                                sems["dWO"].inc(i, 16)
                            mark(("wol", hp), sems["dWO"])
                        if h % 2 == 0:
                            wait(g, ("ropemath", hp))
                        if h >= 1:
                            wait(g, ("sb", h - 1, NT - 1, 3))
                        r0 = 0 if h % 2 == 0 else 32
                        g.dma_start(qr[0:32, :], ropeE[r0:r0 + 32, :]).then_inc(sems["dQRS"].h, 16)
                        sems["dQRS"].n += 16
                        i = g.dma_start(qr[32:64, :], ropeO[r0:r0 + 32, :])
                        sems["dQRS"].inc(i, 16)
                        mark(("qrready2", h), sems["dQRS"])
                        if h % 2 == 1 and hp % 4 != 0:
                            ra = hp
                            for tc in range(NT):
                                wait(g, ("stg", ra, tc))
                                wait(g, ("scrw", ra - 1, tc))
                                i = g.dma_start(scr[2 * (ra // 4)][tc * 128:(tc + 1) * 128, :],
                                                stage[tc % 2][:], accum_op=ALU.add)
                                sems[f"dSCRA{tc % 2}"].inc(i, 16)
                                mark(("scrw", ra, tc), sems[f"dSCRA{tc % 2}"])

                @block.sync
                def _(sy):
                    for r in range(H // 2):
                        wait(sy, ("qevac", r))
                        if r >= 1:
                            wait(sy, ("ropemath", r - 1))
                        i = sy.dma_start(probsO[:], probs[64:128, :])
                        sems["dQR"].inc(i, 16)
                        mark(("qodd", r), sems["dQR"])
                        if r % 4 == 0:
                            for tc in range(NT):
                                wait(sy, ("stg", r, tc))
                                i = sy.dma_start(scr[2 * (r // 4)][tc * 128:(tc + 1) * 128, :],
                                                 stage[tc % 2][:])
                                sems[f"dSCR{tc % 2}"].inc(i, 16)
                                mark(("scrw", r, tc), sems[f"dSCR{tc % 2}"])

                @block.tensor
                def _(t):
                    wait(t, ("krope",))
                    for h in range(H):
                        hp = h // 2
                        wait(t, ("wqnl", h))
                        if h >= 1:
                            wait(t, ("exp", h - 1, NT - 1))
                        else:
                            wait(t, ("ckvevac", 19))
                        if h % 2 == 0:
                            wait(t, ("wqrl", hp))
                            for nb in range(4):
                                if psW_last[nb] is not None:
                                    wait(t, psW_last[nb])
                                    psW_last[nb] = None
                                for dc in range(ND):
                                    i = t.matmul(psA[:, nb * 512:(nb + 1) * 512],
                                                 wqrb[:, dc * 128:(dc + 1) * 128],
                                                 xnt(dc)[:, nb * 512:(nb + 1) * 512],
                                                 start=(dc == 0), stop=(dc == ND - 1))
                                PE.inc(i, 1)
                                mark(("qrpsb", hp, nb), PE)
                            mark(("qrMM", hp), PE)
                        for nb in range(4):
                            if psW_last[nb] is not None:
                                wait(t, psW_last[nb])
                                psW_last[nb] = None
                            if h % 2 == 0:
                                wait(t, ("qevacb", hp, nb))
                            for dc in range(ND):
                                i = t.matmul(psA[:, nb * 512:(nb + 1) * 512],
                                             wqnb[:, dc * NOPE:(dc + 1) * NOPE],
                                             xnt(dc)[:, nb * 512:(nb + 1) * 512],
                                             start=(dc == 0), stop=(dc == ND - 1))
                            PE.inc(i, 1)
                            mark(("qnpsb", h, nb), PE)
                        mark(("qnMM", h), PE)
                        wait(t, ("wkbl", h))
                        if h == 0:
                            wait(t, ("ckvevac", 19))
                        for nb in range(4):
                            wait(t, ("qnevacb", h, nb))
                            for lc in range(4):
                                i = t.matmul(psA[:, nb * 512:(nb + 1) * 512],
                                             knb[:, lc * 128:(lc + 1) * 128],
                                             cT[:, lc * 2048 + nb * 512: lc * 2048 + (nb + 1) * 512],
                                             start=(lc == 0), stop=(lc == 3))
                            PE.inc(i, 1)
                            mark(("knpsb", h, nb), PE)
                        for tc in range(NT):
                            if tc % 4 == 0:
                                wait(t, ("knevacb", h, tc // 4))
                            for lc in range(4):
                                i = t.matmul(psA[:, tc * 128:(tc + 1) * 128],
                                             cT[:, lc * 2048 + tc * 128: lc * 2048 + (tc + 1) * 128],
                                             vwb[:, lc * 128:(lc + 1) * 128],
                                             start=(lc == 0), stop=(lc == 3))
                            if tc % 4 == 3:
                                PE.inc(i, 1)
                                mark(("vps", h, tc // 4), PE)
                        mark(("knvMM", h), PE)

                        wait(t, ("qrready2", h))
                        if h == 0:
                            wait(t, ("vones",))

                        def emit_av(qt):
                            s6 = qt % 2
                            if av_last[s6] is not None:
                                wait(t, av_last[s6])
                            wait(t, ("exp", h, qt))
                            dstA = av_slot(qt)[:, 0:129]
                            for j in range(qt + 1):
                                i2 = t.matmul(dstA,
                                              probsT[:, BLK[j] + (qt - j) * 128: BLK[j] + (qt - j) * 128 + 128],
                                              vsr[:, j * 132: j * 132 + 129],
                                              start=(j == 0), stop=(j == qt))
                            PE.inc(i2, 1)
                            mark(("av", h, qt), PE)
                            av_last[s6] = ("oev", h, qt)

                        def emit_trp(b):
                            if psT_last[b] is not None:
                                wait(t, psT_last[b])
                            wait(t, ("oev", h, b * 8 + 7))
                            pt = psT0 if b == 0 else psT1
                            for j in range(8):
                                i2 = t.transpose(pt[:, j * 128:(j + 1) * 128],
                                                 o_tok[:, (b * 8 + j) * 128:(b * 8 + j + 1) * 128],
                                                 ident[:])
                            PE.inc(i2, 1)
                            mark(("trp", h, b), PE)
                            psT_last[b] = ("oTev", h, b)

                        for kt in range(NT):
                            kb0 = kt // 4
                            for kb in range(kb0, 4):
                                c0 = max(kt * 128, kb * 512)
                                c1 = (kb + 1) * 512
                                if kt == 0:
                                    wait(t, ("vevac", h, kb))
                                else:
                                    wait(t, ("expb", h, kt - 1, kb))
                                t.matmul(psA[:, c0:c1], kns[:, kt * 128:(kt + 1) * 128],
                                         qn[:, c0:c1], start=True, stop=False)
                            for kb in range(kb0, 4):
                                c0 = max(kt * 128, kb * 512)
                                c1 = (kb + 1) * 512
                                i = t.matmul(psA[:, c0:c1], krope[:, kt * 128:(kt + 1) * 128],
                                             qr[:, c0:c1], start=False, stop=(kb != kb0))
                                if kb != kb0:
                                    PE.inc(i, 1)
                                    mark(("sb", h, kt, kb), PE)
                            i = t.matmul(psA[:, kt * 128:(kt + 1) * 128], identNeg[:], ident[:],
                                         start=False, stop=True)
                            PE.inc(i, 1)
                            mark(("sb", h, kt, kb0), PE)
                            if kt >= 1:
                                emit_av(kt - 1)
                            if kt == 10:
                                emit_trp(0)
                        emit_av(NT - 1)
                        emit_trp(1)

                        if h % 2 == 1:
                            wait(t, ("wol", hp))
                            wait(t, ("oTev", h, 1))
                            for tc in range(NT):
                                for dcol in range(4):
                                    slot = (tc * 4 + dcol) % 6
                                    if slot < 4:
                                        psW = psA[:, slot * 512:(slot + 1) * 512]
                                        if psW_last[slot] is not None:
                                            wait(t, psW_last[slot])
                                    else:
                                        psW = psAV[slot - 4][:, 0:512]
                                        if av_last[slot - 4] is not None:
                                            wait(t, av_last[slot - 4])
                                            av_last[slot - 4] = None
                                    for h2 in range(2):
                                        i = t.matmul(psW,
                                                     oT[:, h2 * 2048 + tc * 128: h2 * 2048 + (tc + 1) * 128],
                                                     woq[:, h2 * 2048 + dcol * 512: h2 * 2048 + (dcol + 1) * 512],
                                                     start=(h2 == 0), stop=(h2 == 1))
                                    PE.inc(i, 1)
                                    mark(("wops", hp, tc, dcol), PE)
                                    if slot < 4:
                                        psW_last[slot] = ("stgevac", hp, tc, dcol)
                                    else:
                                        av_last[slot - 4] = ("stgevac", hp, tc, dcol)
                            mark(("woMM", hp), PE)

                @block.vector
                def _(v):
                    for h in range(H):
                        hp = h // 2
                        if h % 2 == 0:
                            wait(v, ("qodd", hp))
                            if hp >= 1:
                                wait(v, ("qrready2", 2 * hp - 1))
                            v.tensor_tensor(out=ropeE[:], in0=probs[0:64, :], in1=cosr[:], op=ALU.mult)
                            v.tensor_tensor(out=ropeO[:], in0=probsO[:], in1=cosr[:], op=ALU.mult)
                            v.drain()
                            v.tensor_tensor(out=probsO[:], in0=probsO[:], in1=sinr[:], op=ALU.mult)
                            v.tensor_tensor(out=probs[0:64, :], in0=probs[0:64, :], in1=sinr[:], op=ALU.mult)
                            v.drain()
                            v.tensor_tensor(out=ropeE[:], in0=ropeE[:], in1=probsO[:], op=ALU.subtract)
                            v.tensor_tensor(out=ropeO[:], in0=ropeO[:], in1=probs[0:64, :], op=ALU.add)
                            v.drain()
                            v.tensor_scalar(out=ropeE[:], in0=ropeE[:], scalar1=SCALE,
                                            scalar2=None, op0=ALU.mult)
                            i = v.tensor_scalar(out=ropeO[:], in0=ropeO[:], scalar1=SCALE,
                                                scalar2=None, op0=ALU.mult)
                            v.drain()
                            VE.inc(i, 1)
                            mark(("ropemath", hp), VE)
                        for qt in range(NT):
                            wait(v, ("av", h, qt))
                            if h >= 1:
                                wait(v, ("trp", h - 1, qt // 8))
                            sl = av_slot(qt)
                            v.reciprocal(colC[:], sl[:, 128:129])
                            v.drain()
                            i = v.tensor_scalar(out=o_tok[:, qt * 128:(qt + 1) * 128],
                                                in0=sl[:, 0:128], scalar1=colC[:],
                                                scalar2=None, op0=ALU.mult)
                            v.drain()
                            VE.inc(i, 1)
                            mark(("oev", h, qt), VE)
                        if h % 2 == 1:
                            for tc in range(NT):
                                for dcol in range(4):
                                    slot = (tc * 4 + dcol) % 6
                                    wait(v, ("wops", hp, tc, dcol))
                                    if dcol == 0:
                                        if tc >= 2:
                                            wait(v, ("scrw", hp, tc - 2))
                                        elif hp >= 1:
                                            wait(v, ("scrw", hp - 1, NT - 2 + tc))
                                    srcW = (psA[:, slot * 512:(slot + 1) * 512] if slot < 4
                                            else psAV[slot - 4][:, 0:512])
                                    i = v.tensor_copy(stage[tc % 2][:, dcol * 512:(dcol + 1) * 512], srcW)
                                    VE.inc(i, 1)
                                    mark(("stgevac", hp, tc, dcol), VE)
                                mark(("stg", hp, tc), VE)

                @block.scalar
                def _(s):
                    for h in range(H):
                        hp = h // 2
                        if h % 2 == 0:
                            if hp >= 1:
                                wait(s, ("ropemath", hp - 1))
                                wait(s, ("qodd", hp - 1))
                            for nb in range(4):
                                wait(s, ("qrpsb", hp, nb))
                                s.activation(probs[:, nb * 512:(nb + 1) * 512],
                                             psA[:, nb * 512:(nb + 1) * 512], AF.Copy)
                                i = s.drain(fusable=False)
                                SE.inc(i, 1)
                                mark(("qevacb", hp, nb), SE)
                            mark(("qevac", hp), SE)
                        for nb in range(4):
                            wait(s, ("qnpsb", h, nb))
                            s.activation(qn[:, nb * 512:(nb + 1) * 512],
                                         psA[:, nb * 512:(nb + 1) * 512], AF.Copy, scale=SCALE)
                            i = s.drain(fusable=False)
                            SE.inc(i, 1)
                            mark(("qnevacb", h, nb), SE)
                        for nb in range(4):
                            wait(s, ("knpsb", h, nb))
                            s.activation(kns[:, nb * 512:(nb + 1) * 512],
                                         psA[:, nb * 512:(nb + 1) * 512], AF.Copy)
                            i = s.drain(fusable=False)
                            SE.inc(i, 1)
                            mark(("knevacb", h, nb), SE)
                        if h == 0:
                            wait(s, ("comb", NT - 1))
                        for g4 in range(4):
                            wait(s, ("vps", h, g4))
                            for tsub in range(4):
                                tc = g4 * 4 + tsub
                                s.activation(vsr[:, tc * 132: tc * 132 + 128],
                                             psA[:, tc * 128:(tc + 1) * 128], AF.Copy)
                            i = s.drain(fusable=False)
                            SE.inc(i, 1)
                            mark(("vevac", h, g4), SE)
                        for kt in range(NT):
                            kb0 = kt // 4
                            for kb in range(kb0, 4):
                                c0 = max(kt * 128, kb * 512)
                                c1 = (kb + 1) * 512
                                wait(s, ("sb", h, kt, kb))
                                i = s.activation(probsT[:, BLK[kt] + c0 - kt * 128: BLK[kt] + c1 - kt * 128],
                                                 psA[:, c0:c1], AF.Exp)
                                SE.inc(i, 1)
                                mark(("expb", h, kt, kb), SE)
                            i = s.drain(fusable=False)
                            SE.inc(i, 1)
                            mark(("exp", h, kt), SE)
                            if kt == 11:
                                if h % 2 == 0 and hp >= 1:
                                    wait(s, ("wops", hp - 1, NT - 1, 3))
                                wait(s, ("trp", h, 0))
                                i = s.activation(oT[:, (h % 2) * 2048: (h % 2) * 2048 + 1024],
                                                 psT0[:, 0:1024], AF.Copy)
                                SE.inc(i, 1)
                                mark(("oTev", h, 0), SE)
                        wait(s, ("trp", h, 1))
                        s.activation(oT[:, (h % 2) * 2048 + 1024: (h % 2) * 2048 + 2048],
                                     psT1[:, 0:1024], AF.Copy)
                        i = s.drain(fusable=False)
                        SE.inc(i, 1)
                        mark(("oTev", h, 1), SE)


        with ExitStack() as P:
            EP = P.enter_context
            acc = EP(nc.sbuf_tensor("acc", [128, NT * 2048], BF16))

            with ExitStack() as P5:
                E5 = P5.enter_context
                hT = E5(nc.sbuf_tensor("hT", [128, NFC * 2048], BF16))
                gb = [E5(nc.sbuf_tensor(f"gb{i}", [128, 2048], BF16)) for i in range(2)]
                ub = [E5(nc.sbuf_tensor(f"ub{i}", [128, 2048], BF16)) for i in range(2)]
                db = E5(nc.sbuf_tensor("db", [128, NFC * 2048], BF16))
                sgt = [E5(nc.sbuf_tensor(f"sgt{i}", [128, 512], F32)) for i in range(2)]
                psG = [E5(nc.psum_tensor(f"psG{i}", [128, 512], F32)) for i in range(2)]
                psU = [E5(nc.psum_tensor(f"psU{i}", [128, 512], F32)) for i in range(2)]
                psD = [E5(nc.psum_tensor(f"psD{i}", [128, 512], F32)) for i in range(2)]

                psG_last = {0: None, 1: None}
                psU_last = {0: None, 1: None}
                psD_last = {0: None, 1: None}
                gslot_last = {0: None, 1: None}
                uslot_last = {0: None, 1: None}

                @block.gpsimd
                def _(g):
                    wait(g, ("woMM", 7))
                    wait(g, ("scrw", 7, NT - 1))
                    wait(g, ("scrw", 7, NT - 2))
                    for e in range(NEXP):
                        for fc in range(NFC):
                            if fc == 2:
                                if e >= 1:
                                    wait(g, ("dwMM", e - 1))
                                i = g.dma_start(db[:].rearrange("p (fc d) -> p fc d", d=2048),
                                                dw[e].rearrange("(fc p) d -> p fc d", p=128))
                                sems["dDW"].inc(i, 16)
                                mark(("dwl", e), sems["dDW"])
                            sl = fc % 2
                            if gslot_last[sl] is not None:
                                wait(g, gslot_last[sl])
                            i = g.dma_start(gb[sl][:].rearrange("p (dc f) -> p dc f", f=128),
                                            gw[e].rearrange("(dc p) f -> p dc f", p=128)[:, :, fc * 128:(fc + 1) * 128])
                            sems[f"dG{sl}"].inc(i, 16)
                            mark(("gl", e, fc), sems[f"dG{sl}"])
                            i = g.dma_start(ub[sl][:].rearrange("p (dc f) -> p dc f", f=128),
                                            uw[e].rearrange("(dc p) f -> p dc f", p=128)[:, :, fc * 128:(fc + 1) * 128])
                            sems[f"dU{sl}"].inc(i, 16)
                            mark(("ul", e, fc), sems[f"dU{sl}"])
                            gslot_last[sl] = ("gMM", e, fc)
                            uslot_last[sl] = ("gMM", e, fc)

                @block.tensor
                def _(t):
                    wait(t, ("stgevac", 7, NT - 1, 3))
                    wait(t, ("scrw", 7, NT - 1))
                    wait(t, ("scrw", 7, NT - 2))
                    for e in range(NEXP):
                        for fc in range(NFC):
                            sl = fc % 2
                            wait(t, ("gl", e, fc))
                            wait(t, ("ul", e, fc))
                            for tb in range(4):
                                k = fc * 4 + tb
                                ps_g, ps_u = psG[k % 2], psU[k % 2]
                                if psG_last[k % 2] is not None:
                                    wait(t, psG_last[k % 2])
                                for dc in range(ND):
                                    i = t.matmul(ps_g[:], gb[sl][:, dc * 128:(dc + 1) * 128],
                                                 xnt(dc)[:, tb * 512:(tb + 1) * 512],
                                                 start=(dc == 0), stop=(dc == ND - 1))
                                PE.inc(i, 1)
                                mark(("gps", e, fc, tb), PE)
                                for dc in range(ND):
                                    i = t.matmul(ps_u[:], ub[sl][:, dc * 128:(dc + 1) * 128],
                                                 xnt(dc)[:, tb * 512:(tb + 1) * 512],
                                                 start=(dc == 0), stop=(dc == ND - 1))
                                PE.inc(i, 1)
                                mark(("ups", e, fc, tb), PE)
                                psG_last[k % 2] = ("hT", e, fc, tb)
                                psU_last[k % 2] = ("hT", e, fc, tb)
                            mark(("gMM", e, fc), PE)
                        wait(t, ("dwl", e))
                        wait(t, ("hT", e, NFC - 1, 3))
                        for tc in range(NT):
                            for dcol in range(4):
                                k = tc * 4 + dcol
                                ps_d = psD[k % 2]
                                if psD_last[k % 2] is not None:
                                    wait(t, psD_last[k % 2])
                                for fc in range(NFC):
                                    i = t.matmul(ps_d[:],
                                                 hT[:, fc * 2048 + tc * 128: fc * 2048 + (tc + 1) * 128],
                                                 db[:, fc * 2048 + dcol * 512: fc * 2048 + (dcol + 1) * 512],
                                                 start=(fc == 0), stop=(fc == NFC - 1))
                                PE.inc(i, 1)
                                mark(("dps", e, tc, dcol), PE)
                                psD_last[k % 2] = ("accw", e, tc, dcol)
                        mark(("dwMM", e), PE)

                @block.scalar
                def _(s):
                    for e in range(NEXP):
                        for fc in range(NFC):
                            for tb in range(4):
                                k = fc * 4 + tb
                                wait(s, ("gps", e, fc, tb))
                                if (e, fc, tb) != (0, 0, 0):
                                    pass
                                prev = (e, fc, tb - 2) if tb >= 2 else None
                                if k >= 2 or e > 0:
                                    pk = k - 2
                                    pe_, pfc, ptb = e, pk // 4, pk % 4
                                    if pk < 0:
                                        pe_, pfc, ptb = e - 1, (pk + 24) // 4, (pk + 24) % 4
                                    wait(s, ("hT", pe_, pfc, ptb))
                                i = s.activation(sgt[k % 2][:], psG[k % 2][:], AF.Silu)
                                s.drain()
                                SE.inc(i, 1)
                                mark(("silu", e, fc, tb), SE)

                @block.vector
                def _(v):
                    for e in range(NEXP):
                        for fc in range(NFC):
                            for tb in range(4):
                                k = fc * 4 + tb
                                wait(v, ("silu", e, fc, tb))
                                wait(v, ("ups", e, fc, tb))
                                i = v.tensor_tensor(out=hT[:, fc * 2048 + tb * 512: fc * 2048 + (tb + 1) * 512],
                                                    in0=sgt[k % 2][:], in1=psU[k % 2][:], op=ALU.mult)
                                VE.inc(i, 1)
                                mark(("hT", e, fc, tb), VE)
                        for tc in range(NT):
                            for dcol in range(4):
                                k = tc * 4 + dcol
                                wait(v, ("dps", e, tc, dcol))
                                cslice = comb[:, tc * NEXP + e: tc * NEXP + e + 1]
                                dst = acc[:, tc * 2048 + dcol * 512: tc * 2048 + (dcol + 1) * 512]
                                if e == 0:
                                    i = v.tensor_scalar(out=dst, in0=psD[k % 2][:], scalar1=cslice,
                                                        scalar2=None, op0=ALU.mult)
                                else:
                                    i = v.scalar_tensor_tensor(out=dst, in0=psD[k % 2][:], scalar=cslice,
                                                               in1=dst, op0=ALU.mult, op1=ALU.add)
                                VE.inc(i, 1)
                                mark(("accw", e, tc, dcol), VE)

            with ExitStack() as P6:
                E6 = P6.enter_context
                rb = [E6(nc.sbuf_tensor(f"rb{i}", [128, 2048], F32)) for i in range(4)]
                ost = [E6(nc.sbuf_tensor(f"ost{i}", [128, 2048], F32)) for i in range(2)]
                xrb = [E6(nc.sbuf_tensor(f"xrb{i}", [128, 2048], F32)) for i in range(2)]

                @block.scalar
                def _(s):
                    wait(s, ("dwMM", NEXP - 1))
                    wait(s, ("accw", NEXP - 1, NT - 1, 3))
                    for tc in range(NT):
                        if tc >= 2:
                            wait(s, ("madd", tc - 2, 2))
                        i = s.dma_start(xrb[tc % 2][:], x[tc * 128:(tc + 1) * 128, :])
                        sems[f"dXLD{tc % 2}"].inc(i, 16)
                        mark(("xld", tc), sems[f"dXLD{tc % 2}"])

                @block.sync
                def _(sy):
                    wait(sy, ("scrw", 7, NT - 1))
                    wait(sy, ("scrw", 7, NT - 2))
                    wait(sy, ("dwMM", NEXP - 1))
                    wait(sy, ("accw", NEXP - 1, NT - 1, 3))
                    for tc in range(NT):
                        if tc >= 1:
                            wait(sy, ("madd", tc - 1, 2))
                            i = sy.dma_start(out[(tc - 1) * 128:tc * 128, :], ost[(tc - 1) % 2][:])
                            sems[f"dOUT{(tc - 1) % 2}"].inc(i, 16)
                            mark(("outw", tc - 1), sems[f"dOUT{(tc - 1) % 2}"])
                        if tc >= 2:
                            wait(sy, ("madd", tc - 2, 0))
                        i = sy.dma_start(rb[2 * (tc % 2)][:], scr[0][tc * 128:(tc + 1) * 128, :])
                        sems[f"dMRG{tc % 2}"].inc(i, 16)
                        mark(("mld", tc, 0), sems[f"dMRG{tc % 2}"])
                    wait(sy, ("madd", NT - 1, 2))
                    i = sy.dma_start(out[(NT - 1) * 128:NT * 128, :], ost[(NT - 1) % 2][:])
                    sems[f"dOUT{(NT - 1) % 2}"].inc(i, 16)
                    mark(("outw", NT - 1), sems[f"dOUT{(NT - 1) % 2}"])
                    sy.wait_ge(sems["dOUT0"].h, sems["dOUT0"].n)
                    sy.wait_ge(sems["dOUT1"].h, sems["dOUT1"].n)

                @block.gpsimd
                def _(g):
                    wait(g, ("scrw", 7, NT - 1))
                    wait(g, ("scrw", 7, NT - 2))
                    wait(g, ("dwMM", NEXP - 1))
                    wait(g, ("accw", NEXP - 1, NT - 1, 3))
                    for tc in range(NT):
                        if tc >= 2:
                            wait(g, ("madd", tc - 2, 0))
                        i = g.dma_start(rb[2 * (tc % 2) + 1][:], scr[2][tc * 128:(tc + 1) * 128, :])
                        sems[f"dMG{tc % 2}"].inc(i, 16)
                        mark(("gld", tc, 0), sems[f"dMG{tc % 2}"])

                @block.vector
                def _(v):
                    for tc in range(NT):
                        wait(v, ("mld", tc, 0))
                        wait(v, ("gld", tc, 0))
                        if tc >= 2:
                            wait(v, ("outw", tc - 2))
                        v.tensor_tensor(out=ost[tc % 2][:], in0=rb[2 * (tc % 2)][:],
                                        in1=rb[2 * (tc % 2) + 1][:], op=ALU.add)
                        i = v.drain(fusable=False)
                        VE.inc(i, 1)
                        mark(("madd", tc, 0), VE)
                        wait(v, ("accw", NEXP - 1, tc, 3))
                        v.tensor_tensor(out=ost[tc % 2][:], in0=ost[tc % 2][:],
                                        in1=acc[:, tc * 2048:(tc + 1) * 2048], op=ALU.add)
                        i = v.drain(fusable=False)
                        VE.inc(i, 1)
                        mark(("madd", tc, 1), VE)
                        wait(v, ("xld", tc))
                        v.tensor_tensor(out=ost[tc % 2][:], in0=ost[tc % 2][:],
                                        in1=xrb[tc % 2][:], op=ALU.add)
                        i = v.drain(fusable=False)
                        VE.inc(i, 1)
                        mark(("madd", tc, 2), VE)

    return nc, marks


_CACHED = {}


def _get_nc():
    if "nc" not in _CACHED:
        _, marks = build_nc(None)
        nc, _ = build_nc(marks)
        _CACHED["nc"] = nc
    return _CACHED["nc"]


def _host_prep(inputs):
    wq = np.asarray(inputs["L0_attn_wq"], dtype=np.float32)
    wkv_a = np.asarray(inputs["L0_attn_wkv_a"], dtype=np.float32)
    nope_cols = np.concatenate([np.arange(h * 192, h * 192 + 128) for h in range(H)])
    wqn_ = np.ascontiguousarray(wq[:, nope_cols])
    rope_cols = []
    for hp in range(H // 2):
        b0 = (2 * hp) * 192 + 128
        b1 = (2 * hp + 1) * 192 + 128
        rope_cols += [b0 + 2 * i for i in range(32)]
        rope_cols += [b1 + 2 * i for i in range(32)]
        rope_cols += [b0 + 2 * i + 1 for i in range(32)]
        rope_cols += [b1 + 2 * i + 1 for i in range(32)]
    wqr_ = np.ascontiguousarray(wq[:, rope_cols])
    kpe_cols = [LORA + 2 * i for i in range(32)] + [LORA + 2 * i + 1 for i in range(32)]
    wkva_ = np.ascontiguousarray(
        np.concatenate([wkv_a[:, :LORA], wkv_a[:, kpe_cols]], axis=1))
    e = np.arange(32, dtype=np.float32) * np.float32(2.0 / 64.0)
    invf_ = (np.float32(1.0) / np.power(np.float32(10000.0), e, dtype=np.float32)
             ).astype(np.float32).reshape(32, 1)
    return wqn_, wqr_, wkva_, invf_


def kernel(L0_attn_x, L0_attn_wq, L0_attn_wkv_a, L0_attn_wkv_b, L0_attn_wo,
           L0_ffn_router_w, L0_ffn_gate_w, L0_ffn_up_w, L0_ffn_down_w):
    x = np.ascontiguousarray(np.asarray(L0_attn_x, dtype=np.float32))
    wqn_, wqr_, wkva_, invf_ = _host_prep({
        "L0_attn_wq": L0_attn_wq, "L0_attn_wkv_a": L0_attn_wkv_a})
    wkvb_ = np.ascontiguousarray(np.asarray(L0_attn_wkv_b, dtype=np.float32))
    wo_ = np.ascontiguousarray(np.asarray(L0_attn_wo, dtype=np.float32))
    rw_ = np.ascontiguousarray(np.asarray(L0_ffn_router_w, dtype=np.float32))
    gw_ = np.ascontiguousarray(np.asarray(L0_ffn_gate_w, dtype=np.float32))
    uw_ = np.ascontiguousarray(np.asarray(L0_ffn_up_w, dtype=np.float32))
    dw_ = np.ascontiguousarray(np.asarray(L0_ffn_down_w, dtype=np.float32))

    nc = _get_nc()
    in_maps = []
    for core in range(8):
        in_maps.append({
            "x": np.ascontiguousarray(x[core]),
            "wqn": wqn_, "wqr": wqr_, "wkva": wkva_, "wkvb": wkvb_,
            "wo": wo_, "rw": rw_, "gw": gw_, "uw": uw_, "dw": dw_,
            "invf": invf_,
        })
    res = run_bass_kernel_spmd(nc, in_maps, core_ids=list(range(8)))
    return np.stack([np.asarray(r["out"]) for r in res.results], axis=0)
